# version 09
# baseline (speedup 1.0000x reference)
import numpy as np
import concourse.bass as bass
import concourse.mybir as mybir
from concourse.bass_utils import run_bass_kernel_spmd
from concourse.bass_core_rust import IndirectOffsetOnAxis

F32 = mybir.dt.float32
BF16 = mybir.dt.bfloat16
I32 = mybir.dt.int32
AF = mybir.ActivationFunctionType
ALU = mybir.AluOpType
AX = mybir.AxisListType

N_CORES = 8
T = 2048
D = 1024
NH = 16
HD = 64
TCH = T // 128
DCH = D // 128
QT = 512
NQT = T // QT
EPS_LN = 1e-5
EPS_RMS = 1e-6
A_TTT = float(np.float32(0.01) * (np.float32(2.0) / np.float32(1024.0)))
EXP_SHIFT = -14.0
VPW = HD + 1
CAP = 2560


SEM_ALIAS = {
    "efw30": "ropew0", "efw31": "ropew1",
    "efg0": "e6g0", "efg1": "e6g1",
    "egg0": "mw0", "egg1": "mw1",
    "eout0": "e4s", "eout1": "e3s",
    "etpp": "trpt", "etpe": "tret",
    "egv": "lnla", "ega": "lnlv", "egs": "lnlgo", "egr": "lnlrs",
    "ego": "e7v", "egc": "e7tp", "eg2": "e7t2", "epc": "e2w",
}


class Ctx:
    def __init__(self, nc):
        self.nc = nc
        self.sems = {}
        self.cnt = {}

    def sem(self, name):
        name = SEM_ALIAS.get(name, name)
        if name not in self.sems:
            self.sems[name] = self.nc.alloc_semaphore(name)
            self.cnt[name] = 0
        return self.sems[name]

    def inc(self, ins, name, by=1):
        name = SEM_ALIAS.get(name, name)
        ins.then_inc(self.sem(name), by)
        self.cnt[name] += by
        return self.cnt[name]

    def val(self, name):
        name = SEM_ALIAS.get(name, name)
        self.sem(name)
        return self.cnt[name]

    def wait(self, eng, name, val=None):
        name = SEM_ALIAS.get(name, name)
        eng.wait_ge(self.sem(name), self.cnt[name] if val is None else val)

    def breg(self, g, val):
        if not hasattr(self, "bregs"):
            self.bregs = {}
        if val not in self.bregs:
            r = g.alloc_register("bnd%d" % val)
            g.reg_mov(r, val)
            self.bregs[val] = r
        return self.bregs[val]


def edge(ctx, eng, ins, tag=None):
    c = ctx.inc(ins, "ech")
    ctx.wait(eng, "ech", c)



class Arena:

    def __init__(self, nc, name="sma", nbytes=11712):
        self.h = nc.alloc_sbuf_tensor(name, [128, nbytes // 2], BF16)
        self.h32 = self.h.bitcast(F32)
        self.hi = self.h.bitcast(I32)
        self.off = 0

    def reset(self):
        self.off = 0

    def take(self, shape, dt):
        ib = {BF16: 2, F32: 4, I32: 4}[dt]
        self.off = (self.off + 3) & ~3
        cols = shape[1]
        start = self.off // ib
        self.off += cols * ib
        assert self.off <= self.h.shape[1] * 2, "arena overflow %d" % self.off
        src = {BF16: self.h, F32: self.h32, I32: self.hi}[dt]
        return src[0:shape[0], start:start + cols]


def host_constants():
    c = {}
    c["c_ident"] = np.eye(128, dtype=np.float32)
    c["c_identf"] = np.eye(128, dtype=np.float32)
    c["c_ones"] = np.ones((128, 128), dtype=np.float32)
    c["c_onesb"] = np.ones((128, 128), dtype=np.float32)
    m = np.zeros((128, 4, QT), dtype=np.float32)
    for j in range(4):
        for p in range(128):
            m[p, j, j * 128 + p:] = 1.0
    c["c_mask"] = m.reshape(128, 4 * QT)
    tril_s = np.tril(np.ones((128, 128), dtype=np.float32), -1) * np.float32(A_TTT)
    c["c_trils"] = tril_s
    c["c_trius"] = np.ascontiguousarray(tril_s.T)
    c["c_uincl"] = np.ascontiguousarray(np.triu(np.ones((128, 128), dtype=np.float32)))
    c["c_iota_e"] = np.broadcast_to(np.arange(8, dtype=np.float32), (128, 8)).copy()
    return c


CONST_SPECS = {
    "c_ident": ([128, 128], BF16),
    "c_identf": ([128, 128], F32),
    "c_ones": ([128, 128], F32),
    "c_onesb": ([128, 128], BF16),
    "c_mask": ([128, 4 * QT], BF16),
    "c_trils": ([128, 128], F32),
    "c_trius": ([128, 128], F32),
    "c_uincl": ([128, 128], F32),
    "c_iota_e": ([128, 8], F32),
}


RESIDENT_CONSTS = ("c_ident", "c_identf", "c_ones", "c_onesb")


def setup_constants(ctx, resident_only=False):
    nc = ctx.nc
    cs = {}
    drams = {}
    for name, (shape, dt) in CONST_SPECS.items():
        drams[name] = nc.dram_tensor(name, shape, F32, kind="ExternalInput")
        if (not resident_only) or name in RESIDENT_CONSTS:
            cs[name] = nc.alloc_sbuf_tensor(name + "_s", shape, dt)
    names = RESIDENT_CONSTS if resident_only else tuple(CONST_SPECS)
    with nc.Block() as blk:
        @blk.gpsimd
        def _(g):
            for name in names:
                ins = g.dma_start(cs[name][:, :], drams[name][:, :])
                ctx.inc(ins, "cld", 16)
            ctx.wait(g, "cld")
    cs["_drams"] = drams
    return cs


def emit_arena_consts(ctx, cs, ar, names):
    nc = ctx.nc
    for name in names:
        cs[name] = ar.take(*CONST_SPECS[name])
    with nc.Block() as blk:
        @blk.gpsimd
        def _(g):
            for name in names:
                ins = g.dma_start(cs[name][:, :], cs["_drams"][name][:, :])
                ctx.inc(ins, "cld", 16)
            ctx.wait(g, "cld")



_PSUM_UID = [0]


def emit_transposes(ctx, tag, pairs, cs, dtype=BF16):
    nc = ctx.nc
    _PSUM_UID[0] += 1
    uid = str(_PSUM_UID[0])
    with (
        nc.psum_tensor("pt0" + tag + uid, [128, 128], dtype) as p0,
        nc.psum_tensor("pt1" + tag + uid, [128, 128], dtype) as p1,
        nc.Block() as blk,
    ):
        ps = [p0, p1]

        base_p = ctx.val("trp" + tag)
        base_e = ctx.val("tre" + tag)

        @blk.tensor
        def _(t):
            for n, (src, dst) in enumerate(pairs):
                if n >= 2:
                    ctx.wait(t, "tre" + tag, base_e + n - 1)
                ins = t.transpose(ps[n % 2][:, :], src, cs["c_ident"][:, :])
                ctx.inc(ins, "trp" + tag)

        @blk.vector
        def _(v):
            for n, (src, dst) in enumerate(pairs):
                ctx.wait(v, "trp" + tag, base_p + n + 1)
                ins = v.tensor_copy(dst, ps[n % 2][:, :])
                ctx.inc(ins, "tre" + tag)


def attn_scratch(sm):
    return {"E": sm["ws"][:, 0:2048],
            "shift": sm["bc0"][:, 768:769],
            "rcp": sm["bc0"][:, 0:512],
            "bcs": sm["bc0"].bitcast(BF16)[0:64, 1024:1536]}


def emit_tok2fm(ctx, tag, x_tok, x_fm, cs):
    pairs = []
    for i in range(TCH):
        for k in range(DCH):
            pairs.append((x_tok[:, i * D + k * 128:i * D + (k + 1) * 128],
                          x_fm[:, k * T + i * 128:k * T + (i + 1) * 128]))
    emit_transposes(ctx, tag, pairs, cs)


def emit_matmul_tiles(ctx, tag, jobs, psums, evac, dmas=None, job_wreq=None,
                      dma_pereq=None):
    nc = ctx.nc
    nb = len(psums)
    base_p = ctx.val(tag + "p")
    base_e = ctx.val(tag + "e")
    with nc.Block() as blk:
        if dmas:
            @blk.gpsimd
            def _(g):
                for i, (dst, src, semname) in enumerate(dmas):
                    if dma_pereq and dma_pereq[i] > 0:
                        ctx.wait(g, tag + "p", base_p + dma_pereq[i])
                    ins = g.dma_start(dst, src)
                    ctx.inc(ins, semname, 16)

        @blk.tensor
        def _(t):
            for n, job in enumerate(jobs):
                if job_wreq and job_wreq[n]:
                    for semname, val in job_wreq[n]:
                        ctx.wait(t, semname, val)
                if n >= nb:
                    ctx.wait(t, tag + "e", base_e + n - nb + 1)
                ps = psums[n % nb]
                na = len(job)
                for a, (lh, rh) in enumerate(job):
                    ins = t.matmul(ps, lh, rh, start=(a == 0), stop=(a == na - 1))
                ctx.inc(ins, tag + "p")

        @blk.vector
        def _(v):
            for n, job in enumerate(jobs):
                ctx.wait(v, tag + "p", base_p + n + 1)
                ins = evac(n, psums[n % nb], v)
                ctx.inc(ins, tag + "e")


def emit_ln_stats_apply(ctx, tag, src_tok, dst_tok, st, eps_val, rms=False,
                        gamma_bc=None, dst_f32_tiles=None, n_t=TCH, dbg=None):
    nc = ctx.nc
    base_la = ctx.val(tag + "la")
    base_lv = ctx.val(tag + "lv")
    with nc.Block() as blk:
        @blk.scalar
        def _(a):
            for tt in range(n_t):
                if tt >= 2:
                    ctx.wait(a, tag + "lv", base_lv + tt - 1)
                ins = a.activation(st["sq"][:, (tt % 2) * D:(tt % 2 + 1) * D],
                                   src_tok[:, tt * D:(tt + 1) * D], AF.Square)
                ctx.inc(ins, tag + "la")

        @blk.vector
        def _(v):
            v.memset(st["eps"][:, :], eps_val)
            for tt in range(n_t):
                ctx.wait(v, tag + "la", base_la + tt + 1)
                ins = v.tensor_reduce(st["ss"][:, tt:tt + 1],
                                      st["sq"][:, (tt % 2) * D:(tt % 2 + 1) * D],
                                      AX.X, ALU.add)
                ctx.inc(ins, tag + "lv")
                if not rms:
                    v.tensor_reduce(st["mean"][:, tt:tt + 1],
                                    src_tok[:, tt * D:(tt + 1) * D], AX.X, ALU.add)
    with nc.Block() as blk:
        @blk.vector
        def _(v):
            v.tensor_scalar(st["var"][:, :n_t], st["ss"][:, :n_t], 1.0 / D, None,
                            ALU.mult)
            if not rms:
                ins = v.tensor_scalar(st["mean"][:, :n_t], st["mean"][:, :n_t],
                                      1.0 / D, None, ALU.mult)
                edge(ctx, v, ins, tag + "c1")
                ins = v.tensor_tensor(st["msq"][:, :n_t], st["mean"][:, :n_t],
                                      st["mean"][:, :n_t], ALU.mult)
                edge(ctx, v, ins, tag + "c2")
                v.tensor_tensor(st["var"][:, :n_t], st["var"][:, :n_t],
                                st["msq"][:, :n_t], ALU.subtract)
            ins = v.engine_nop()
            ctx.inc(ins, tag + "lgo")
        @blk.scalar
        def _(a):
            ctx.wait(a, tag + "lgo")
            ins = a.activation(st["rstd"][:, :n_t], st["var"][:, :n_t], AF.Ln,
                               bias=st["eps"][:, :])
            edge(ctx, a, ins)
            ins = a.activation(st["rstd"][:, :n_t], st["rstd"][:, :n_t], AF.Exp,
                               scale=-0.5)
            ctx.inc(ins, tag + "lrs")
    with nc.Block() as blk:
        @blk.vector
        def _(v):
            ctx.wait(v, tag + "lrs")
            for tt in range(n_t):
                src = src_tok[:, tt * D:(tt + 1) * D]
                dst = (dst_f32_tiles(tt) if dst_f32_tiles is not None
                       else dst_tok[:, tt * D:(tt + 1) * D])
                if rms:
                    ins = v.tensor_scalar(dst, src, st["rstd"][:, tt:tt + 1], None,
                                          ALU.mult)
                else:
                    ins = v.tensor_scalar(dst, src, st["mean"][:, tt:tt + 1],
                                          st["rstd"][:, tt:tt + 1], ALU.subtract,
                                          ALU.mult)
                if gamma_bc is not None:
                    edge(ctx, v, ins, tag + "c3")
                    v.tensor_tensor(dst, dst, gamma_bc[:, :], ALU.mult)


def alloc_ln_stats(nc, sq_backing=None):
    st = {}
    for nm in ("ss", "mean", "var", "rstd", "msq"):
        st[nm] = nc.alloc_sbuf_tensor("lnst_" + nm, [128, TCH], F32)
    st["eps"] = nc.alloc_sbuf_tensor("lnst_eps", [128, 1], F32)
    if sq_backing is not None:
        st["sq"] = sq_backing
    else:
        st["sq"] = nc.alloc_sbuf_tensor("lnst_sq", [128, 2 * D], BF16)
    return st


def emit_bcast_rows_simple(ctx, tag, items, cs):
    nc = ctx.nc
    _PSUM_UID[0] += 1
    tag2 = tag + "u" + str(_PSUM_UID[0])
    with (
        nc.psum_tensor("bc0" + tag2, [128, 512], F32) as p0,
        nc.psum_tensor("bc1" + tag2, [128, 512], F32) as p1,
        nc.Block() as blk,
    ):
        ps = [p0, p1]

        base_bp = ctx.val(tag + "bp")
        base_be = ctx.val(tag + "be")

        @blk.tensor
        def _(t):
            for n, (row, dst) in enumerate(items):
                w = row.shape[-1] if hasattr(row, "shape") else 512
                if n >= 2:
                    ctx.wait(t, tag + "be", base_be + n - 1)
                ins = t.matmul(ps[n % 2][:, 0:dst.shape[-1]],
                               cs["c_onesb" if row.dtype == BF16 else "c_ones"][0:1, :],
                               row, start=True, stop=True)
                ctx.inc(ins, tag + "bp")

        @blk.vector
        def _(v):
            for n, (row, dst) in enumerate(items):
                ctx.wait(v, tag + "bp", base_bp + n + 1)
                ins = v.tensor_copy(dst, ps[n % 2][:, 0:dst.shape[-1]])
                ctx.inc(ins, tag + "be")



def emit_attention(ctx, tag, qf, kf, vp, yf, cs, scr, scale=0.125):
    nc = ctx.nc
    tiles = []
    groups = []
    n = 0
    for h in range(NH):
        for qt in range(NQT):
            nk = 4 * qt + 4
            groups.append((h, qt, nk, n))
            for kc in range(nk):
                tiles.append((h, qt, kc, nk))
                n += 1
    NT = len(tiles)

    def npartial(idx):
        c = 0
        for i in range(idx + 1):
            h, qt, kc, nk = tiles[i]
            if kc >= nk - 4:
                c += 1
        return c

    pref = []
    c = 0
    for i in range(NT):
        h, qt, kc, nk = tiles[i]
        if kc >= nk - 4:
            c += 1
        pref.append(c)
    group_of_tile = []
    for gi, (h, qt, nk, f) in enumerate(groups):
        group_of_tile += [gi] * nk
    islast = [tiles[i][2] == tiles[i][3] - 1 for i in range(NT)]
    nlast = []
    c2 = 0
    for i in range(NT):
        if islast[i]:
            c2 += 1
        nlast.append(c2)

    E = scr["E"]
    shift = scr["shift"]
    rcp = scr["rcp"]
    bcs = scr["bcs"]
    _PSUM_UID[0] += 1
    tag_u = tag + "u" + str(_PSUM_UID[0])
    bases = {k: ctx.val(k + tag) for k in
             ("ap", "aa", "am", "ad", "aq", "ar", "ab", "av", "ae")}
    with (
        nc.psum_tensor("sc0" + tag_u, [128, QT], F32) as sc0,
        nc.psum_tensor("sc1" + tag_u, [128, QT], F32) as sc1,
        nc.psum_tensor("sc2" + tag_u, [128, QT], F32) as sc2,
        nc.psum_tensor("sc3" + tag_u, [128, QT], F32) as sc3,
        nc.psum_tensor("yp0" + tag_u, [VPW, QT], F32) as yp0,
        nc.psum_tensor("yp1" + tag_u, [VPW, QT], F32) as yp1,
        nc.psum_tensor("bcp" + tag_u, [64, QT], F32) as bcp,
    ):
        with nc.Block() as blk0:
            @blk0.vector
            def _(v):
                v.memset(shift[:, :], EXP_SHIFT)
        scs = [sc0, sc1, sc2, sc3]
        yps = [yp0, yp1]

        def eslice(i):
            return E[:, (i % 4) * QT:(i % 4 + 1) * QT]

        with nc.Block() as blk:
            @blk.tensor
            def _(t):
                def issue_score(i):
                    h, qt, kc, nk = tiles[i]
                    ch, rw = h // 2, (h % 2) * 64
                    if i >= 4:
                        ctx.wait(t, "aa" + tag, bases["aa"] + i - 3)
                    ins = t.matmul(scs[i % 4][:, :],
                                   kf[rw:rw + 64, ch * T + kc * 128:ch * T + (kc + 1) * 128],
                                   qf[rw:rw + 64, ch * T + qt * QT:ch * T + (qt + 1) * QT],
                                   start=True, stop=True)
                    ctx.inc(ins, "ap" + tag)

                def issue_av(i):
                    h, qt, kc, nk = tiles[i]
                    g = group_of_tile[i]
                    ctx.wait(t, "aa" + tag, bases["aa"] + i + 1)
                    if pref[i]:
                        ctx.wait(t, "am" + tag, bases["am"] + pref[i])
                    if kc == 0 and g >= 2:
                        ctx.wait(t, "ae" + tag, bases["ae"] + g - 1)
                    ins = t.matmul(yps[g % 2][:, :],
                                   vp[:, kc * (NH * VPW) + h * VPW:
                                      kc * (NH * VPW) + (h + 1) * VPW],
                                   eslice(i),
                                   start=(kc == 0), stop=(kc == nk - 1))
                    if kc == nk - 1:
                        ctx.inc(ins, "aq" + tag)
                    else:
                        ctx.inc(ins, "ad" + tag)

                def issue_bcast(g):
                    ctx.wait(t, "ar" + tag, bases["ar"] + g + 1)
                    if g >= 1:
                        ctx.wait(t, "av" + tag, bases["av"] + g)
                    ins = t.matmul(bcp[:, :], cs["c_ones"][64:65, 0:64],
                                   rcp[64:65, :], start=True, stop=True)
                    ctx.inc(ins, "ab" + tag)

                bcast_at = {}
                for gi in range(len(groups) - 1):
                    last_g = groups[gi + 1][3] - 1
                    defer = min(3, groups[gi + 1][2] - 4)
                    bcast_at.setdefault(last_g + defer, []).append(gi)

                def after_av(j):
                    for gi in bcast_at.get(j, []):
                        issue_bcast(gi)

                for i in range(NT):
                    issue_score(i)
                    if i >= 2:
                        issue_av(i - 2)
                        after_av(i - 2)
                issue_av(NT - 2)
                after_av(NT - 2)
                issue_av(NT - 1)
                after_av(NT - 1)
                issue_bcast(len(groups) - 1)

            @blk.scalar
            def _(a):
                for i in range(NT):
                    ctx.wait(a, "ap" + tag, bases["ap"] + i + 1)
                    if i >= 4:
                        nl = nlast[i - 4]
                        nn = (i - 3) - nl
                        if nn:
                            ctx.wait(a, "ad" + tag, bases["ad"] + nn)
                        if nl:
                            ctx.wait(a, "aq" + tag, bases["aq"] + nl)
                        if pref[i - 4]:
                            ctx.wait(a, "am" + tag, bases["am"] + pref[i - 4])
                    ins = a.activation(eslice(i), scs[i % 4][:, :], AF.Exp,
                                       bias=shift[:, :], scale=scale)
                    ctx.inc(ins, "aa" + tag)

            @blk.vector
            def _(v):
                for i in range(NT):
                    h, qt, kc, nk = tiles[i]
                    if kc >= nk - 4:
                        ctx.wait(v, "aa" + tag, bases["aa"] + i + 1)
                        j = kc - (nk - 4)
                        ins = v.tensor_tensor(eslice(i), eslice(i),
                                              cs["c_mask"][:, j * QT:(j + 1) * QT],
                                              ALU.mult)
                        ctx.inc(ins, "am" + tag)
                    if kc == nk - 1:
                        g = group_of_tile[i]
                        h2, qt2 = h, qt
                        ch, rw = h2 // 2, (h2 % 2) * 64
                        ctx.wait(v, "aq" + tag, bases["aq"] + g + 1)
                        ins = v.reciprocal(rcp[64:65, :], yps[g % 2][64:65, :])
                        ctx.inc(ins, "ar" + tag)
                        ctx.wait(v, "ab" + tag, bases["ab"] + g + 1)
                        ins = v.tensor_copy(bcs[:, :], bcp[:, :])
                        ctx.inc(ins, "av" + tag)
                        ctx.wait(v, "av" + tag)
                        ins = v.tensor_tensor(
                            yf[rw:rw + 64, ch * T + qt2 * QT:ch * T + (qt2 + 1) * QT],
                            yps[g % 2][0:64, :], bcs[:, :], ALU.mult)
                        ctx.inc(ins, "ae" + tag)



def stage_l0_attn(ctx, cs, st, sm, wd, x_in, x_out, rfm, rq, rk, rvp,
                  dd=None, rg=None):
    nc = ctx.nc
    dd = dd or {}
    ws = sm["ws"]
    if "xa" in dd:
        emit_store_raw(ctx, x_in, dd["xa"], "xa")
    emit_tok2fm(ctx, "t", x_in, rfm, cs)
    if "xfm" in dd:
        emit_store_raw(ctx, rfm, dd["xfm"], "xfm")

    emit_bcast_rows_simple(ctx, "bc", [
        (sm["bv_row"][0:1, j * 512:(j + 1) * 512],
         sm["bc0"][:, j * 512:(j + 1) * 512]) for j in range(2)
    ], cs)

    with (
        nc.psum_tensor("l0a_p0", [128, QT], F32) as p0,
        nc.psum_tensor("l0a_p1", [128, QT], F32) as p1,
        nc.psum_tensor("l0a_p2", [128, QT], F32) as p2,
        nc.psum_tensor("l0a_p3", [128, QT], F32) as p3,
    ):
        jobs, meta, wreq = [], [], []
        dmas, dpereq = [], []
        bw = [ctx.val("mw0"), ctx.val("mw1")]
        for oc in range(16):
            dmas.append((ws[:, (oc % 2) * 1024:(oc % 2) * 1024 + 1024]
                         .rearrange("p (dc c) -> p dc c", c=128),
                         wd["W_attn_ocm"][oc].rearrange("(dc p) c -> p dc c", p=128),
                         "mw" + str(oc % 2)))
            dpereq.append(max(0, (oc - 1) * NQT))
            for tt in range(NQT):
                jobs.append([(ws[:, (oc % 2) * 1024 + dc * 128:(oc % 2) * 1024 + (dc + 1) * 128],
                              rfm[:, dc * T + tt * QT:dc * T + (tt + 1) * QT])
                             for dc in range(DCH)])
                meta.append((oc, tt))
                wreq.append([("mw" + str(oc % 2), bw[oc % 2] + 16 * (oc // 2 + 1))])

        def qk_evac(n, ps, v):
            oc, tt = meta[n]
            dst = rq if oc < 8 else rk
            c = oc % 8
            return v.tensor_scalar(dst[:, c * T + tt * QT:c * T + (tt + 1) * QT],
                                   ps, sm["b3_pp"][:, oc:oc + 1], None, ALU.add)

        emit_matmul_tiles(ctx, "mm", jobs, [p0[:, :], p1[:, :], p2[:, :], p3[:, :]],
                          qk_evac, dmas=dmas, job_wreq=wreq, dma_pereq=dpereq)

        bw = [ctx.val("mw0"), ctx.val("mw1")]
        with nc.Block() as blk:
            @blk.vector
            def _(v):
                v.memset(rvp[:, :TCH * NH * VPW], 0.0)
            @blk.gpsimd
            def _(g):
                for half in range(2):
                    for j in range(4):
                        oc = 16 + half * 4 + j
                        dst = ws[:, half * 4096:(half + 1) * 4096].rearrange(
                            "p (dc j c) -> p dc j c", j=4, c=128)[:, :, j, :]
                        ins = g.dma_start(
                            dst,
                            wd["W_attn_ocm"][oc].rearrange("(dc p) c -> p dc c", p=128))
                        ctx.inc(ins, "mw" + str(half), 16)

        vjobs, vmeta, vwreq = [], [], []
        for half in range(2):
            for tc in range(TCH):
                vjobs.append([(rfm[:, dc * T + tc * 128:dc * T + (tc + 1) * 128],
                               ws[:, half * 4096 + dc * 512:half * 4096 + (dc + 1) * 512])
                              for dc in range(DCH)])
                vmeta.append((half, tc))
                vwreq.append([("mw" + str(half), bw[half] + 64)])

        def v_evac(n, ps, v):
            half, tc = vmeta[n]
            base = tc * (NH * VPW) + half * 8 * VPW
            return v.tensor_add(
                rvp[:, base:base + 8 * VPW].rearrange("p (h w) -> p h w", w=VPW)[:, :, 0:HD],
                ps.rearrange("p (h w) -> p h w", w=HD),
                sm["bc0"][:, half * 512:(half + 1) * 512].rearrange(
                    "p (h w) -> p h w", w=HD))

        emit_matmul_tiles(ctx, "mm", vjobs, [p0[:, :], p1[:, :], p2[:, :], p3[:, :]],
                          v_evac, job_wreq=vwreq)
        with nc.Block() as blk:
            @blk.vector
            def _(v):
                v.memset(rvp[:, :TCH * NH * VPW].rearrange(
                    "p (t h w) -> p t h w", h=NH, w=VPW)[:, :, :, HD:HD + 1], 1.0)

    if "qf" in dd:
        emit_store_raw(ctx, rq, dd["qf"], "qf")
    if "kf" in dd:
        emit_store_raw(ctx, rk, dd["kf"], "kf")
    if "vp" in dd:
        emit_store_raw(ctx, rvp, dd["vp"], "vp")
    emit_attention(ctx, "at", rq, rk, rvp, rfm, cs, attn_scratch(sm))
    if "yf" in dd:
        emit_store_raw(ctx, rfm, dd["yf"], "yf")

    emit_bcast_rows_simple(ctx, "bc", [
        (sm["bp_row"][0:1, j * 512:(j + 1) * 512],
         sm["bc1"][:, j * 512:(j + 1) * 512]) for j in range(2)
    ], cs)

    bw = [ctx.val("mw0"), ctx.val("mw1")]
    with nc.Block() as blk:
        @blk.gpsimd
        def _(g):
            for half in range(2):
                ins = g.dma_start(
                    ws[:, half * 4096:(half + 1) * 4096]
                    .rearrange("p (dc c) -> p dc c", c=512),
                    wd["W_proj_hm"][half].rearrange("(dc p) c -> p dc c", p=128))
                ctx.inc(ins, "mw" + str(half), 16)

    zjobs, zmeta, zwreq = [], [], []
    for half in range(2):
        for tc in range(TCH):
            zjobs.append([(rfm[:, dc * T + tc * 128:dc * T + (tc + 1) * 128],
                           ws[:, half * 4096 + dc * 512:half * 4096 + (dc + 1) * 512])
                          for dc in range(DCH)])
            zmeta.append((half, tc))
            zwreq.append([("mw" + str(half), bw[half] + 16)])

    def z_evac(n, ps, v):
        half, tc = zmeta[n]
        sl = slice(tc * D + half * 512, tc * D + (half + 1) * 512)
        ins = v.tensor_add(x_out[:, sl], ps, sm["bc1"][:, half * 512:(half + 1) * 512])
        edge(ctx, v, ins, "l0azc")
        return v.tensor_tensor(x_out[:, sl], x_out[:, sl], x_in[:, sl], ALU.add)

    with (
        nc.psum_tensor("l0a_z0", [128, QT], F32) as z0,
        nc.psum_tensor("l0a_z1", [128, QT], F32) as z1,
        nc.psum_tensor("l0a_z2", [128, QT], F32) as z2,
        nc.psum_tensor("l0a_z3", [128, QT], F32) as z3,
    ):
        emit_matmul_tiles(ctx, "mm", zjobs, [z0[:, :], z1[:, :], z2[:, :], z3[:, :]],
                          z_evac, job_wreq=zwreq)

    if "xpre" in dd:
        emit_store_raw(ctx, x_out, dd["xpre"], "xpre")
    dbg = None
    if "lnst" in dd:
        dbg = {k: dd["lnst"][:, i * TCH:(i + 1) * TCH]
               for i, k in enumerate(("ss", "mean", "var", "rstd"))}
    emit_ln_stats_apply(ctx, "ln", x_out, x_out, st, EPS_LN, dbg=dbg)



def emit_proj_tok(ctx, tag, src_fm, wname_dram, dst_cb, psums, wsem_offset=0):
    pass


def stage_l0_ffn_attn(ctx, cs, st, sm, wd, x_in, rA, rB, rC, rvp):
    nc = ctx.nc
    ws = sm["ws"]
    emit_bcast_rows_simple(ctx, "bc", [
        (sm["g1_row"][0:1, j * 512:(j + 1) * 512],
         sm["bc0"][:, j * 512:(j + 1) * 512]) for j in range(2)
    ], cs)
    emit_ln_stats_apply(ctx, "ln", x_in, rA, st, EPS_RMS, rms=True,
                        gamma_bc=sm["bc0"])
    emit_tok2fm(ctx, "t", rA, rB, cs)

    def tok_proj(tag, wkey, dst_or_cb):
        jobs, meta, wreq = [], [], []
        bw = [ctx.val("mw0"), ctx.val("mw1")]
        dmas = [(ws[:, half * 4096:(half + 1) * 4096]
                 .rearrange("p (dc c) -> p dc c", c=512),
                 wd[wkey][half].rearrange("(dc p) c -> p dc c", p=128),
                 "mw" + str(half))
                for half in range(2)]
        for half in range(2):
            for tc in range(TCH):
                jobs.append([(rB[:, dc * T + tc * 128:dc * T + (tc + 1) * 128],
                              ws[:, half * 4096 + dc * 512:half * 4096 + (dc + 1) * 512])
                             for dc in range(DCH)])
                meta.append((half, tc))
                wreq.append([("mw" + str(half), bw[half] + 16)])

        def evac(n, ps, v):
            half, tc = meta[n]
            if callable(dst_or_cb):
                return dst_or_cb(half, tc, ps, v)
            return v.tensor_copy(
                dst_or_cb[:, tc * D + half * 512:tc * D + (half + 1) * 512], ps)

        with (
            nc.psum_tensor(tag + "_p0", [128, QT], F32) as p0,
            nc.psum_tensor(tag + "_p1", [128, QT], F32) as p1,
            nc.psum_tensor(tag + "_p2", [128, QT], F32) as p2,
            nc.psum_tensor(tag + "_p3", [128, QT], F32) as p3,
        ):
            emit_matmul_tiles(ctx, "mm", jobs,
                              [p0[:, :], p1[:, :], p2[:, :], p3[:, :]], evac,
                              dmas=dmas, job_wreq=wreq)

    tok_proj("l0fq", "l0f_wq", rA)
    tok_proj("l0fk", "l0f_wk", rC)
    with nc.Block() as blk:
        @blk.vector
        def _(v):
            v.memset(rvp[:, :TCH * NH * VPW], 0.0)

    def v_evac(half, tc, ps, v):
        base = tc * (NH * VPW) + half * 8 * VPW
        return v.tensor_copy(
            rvp[:, base:base + 8 * VPW].rearrange("p (h w) -> p h w", w=VPW)[:, :, 0:HD],
            ps.rearrange("p (h w) -> p h w", w=HD))

    tok_proj("l0fv", "l0f_wv", v_evac)
    with nc.Block() as blk:
        @blk.vector
        def _(v):
            v.memset(rvp[:, :TCH * NH * VPW].rearrange(
                "p (t h w) -> p t h w", h=NH, w=VPW)[:, :, :, HD:HD + 1], 1.0)

    with nc.Block() as blk:
        @blk.gpsimd
        def _(g):
            for tt in range(TCH):
                if tt >= 2:
                    ctx.wait(g, "ropev", (tt - 1) * 2)
                ins = g.dma_start(sm["cr"][:, (tt % 2) * 512:(tt % 2 + 1) * 512],
                                  wd["cr"][tt * 128:(tt + 1) * 128, :])
                ctx.inc(ins, "ropew" + str(tt % 2), 16)
                ins = g.dma_start(sm["ci"][:, (tt % 2) * 512:(tt % 2 + 1) * 512],
                                  wd["ci"][tt * 128:(tt + 1) * 128, :])
                ctx.inc(ins, "ropew" + str(tt % 2), 16)

        @blk.vector
        def _(v):
            t1 = sm["bc0"][:, 0:512]
            t2 = sm["bc0"][:, 512:1024]
            t3 = sm["bc1"][:, 0:512]
            t4 = sm["bc1"][:, 512:1024]
            for tt in range(TCH):
                ctx.wait(v, "ropew" + str(tt % 2), 32 * (tt // 2 + 1))
                cr = sm["cr"][:, (tt % 2) * 512:(tt % 2 + 1) * 512]
                ci = sm["ci"][:, (tt % 2) * 512:(tt % 2 + 1) * 512]
                for reg in (rA, rC):
                    tile = reg[:, tt * D:(tt + 1) * D].rearrange(
                        "p (hi two) -> p hi two", two=2)
                    EV = tile[:, :, 0]
                    OD = tile[:, :, 1]
                    v.tensor_tensor(t1, EV, cr, ALU.mult)
                    v.tensor_tensor(t2, OD, ci, ALU.mult)
                    v.tensor_tensor(t3, OD, cr, ALU.mult)
                    ins = v.tensor_tensor(t4, EV, ci, ALU.mult)
                    edge(ctx, v, ins, "ropec")
                    v.tensor_tensor(EV, t1, t2, ALU.subtract)
                    ins = v.tensor_tensor(OD, t3, t4, ALU.add)
                    edge(ctx, v, ins)
                ins = v.engine_nop()
                ctx.inc(ins, "ropev", 2)

    emit_tok2fm(ctx, "t", rA, rB, cs)
    emit_tok2fm(ctx, "t", rC, rA, cs)

    emit_attention(ctx, "at", rB, rA, rvp, rC, cs, attn_scratch(sm))

    def h_evac(half, tc, ps, v):
        sl = slice(tc * D + half * 512, tc * D + (half + 1) * 512)
        return v.tensor_add(rA[:, sl], ps, x_in[:, sl])

    jobs, meta, wreq = [], [], []
    bw = [ctx.val("mw0"), ctx.val("mw1")]
    dmas = [(ws[:, half * 4096:(half + 1) * 4096]
             .rearrange("p (dc c) -> p dc c", c=512),
             wd["l0f_wo"][half].rearrange("(dc p) c -> p dc c", p=128),
             "mw" + str(half))
            for half in range(2)]
    for half in range(2):
        for tc in range(TCH):
            jobs.append([(rC[:, dc * T + tc * 128:dc * T + (tc + 1) * 128],
                          ws[:, half * 4096 + dc * 512:half * 4096 + (dc + 1) * 512])
                         for dc in range(DCH)])
            meta.append((half, tc))
            wreq.append([("mw" + str(half), bw[half] + 16)])

    def evac(n, ps, v):
        half, tc = meta[n]
        return h_evac(half, tc, ps, v)

    with (
        nc.psum_tensor("l0fo_p0", [128, QT], F32) as p0,
        nc.psum_tensor("l0fo_p1", [128, QT], F32) as p1,
        nc.psum_tensor("l0fo_p2", [128, QT], F32) as p2,
        nc.psum_tensor("l0fo_p3", [128, QT], F32) as p3,
    ):
        emit_matmul_tiles(ctx, "mm", jobs,
                          [p0[:, :], p1[:, :], p2[:, :], p3[:, :]], evac,
                          dmas=dmas, job_wreq=wreq)


def declare_l0f_attn_weights(nc):
    wd = {}
    for wname in ("wq", "wk", "wv", "wo"):
        w = nc.dram_tensor("l0f_" + wname, [2, D, 512], F32, kind="ExternalInput")
        wd["l0f_" + wname] = [w[i] for i in range(2)]
    wd["cr"] = nc.dram_tensor("cr", [T, 512], F32, kind="ExternalInput")
    wd["ci"] = nc.dram_tensor("ci", [T, 512], F32, kind="ExternalInput")
    wd["g1_row"] = nc.dram_tensor("g1_row", [1, D], F32, kind="ExternalInput")
    return wd


def host_l0f_attn_weight_maps(ins):
    m = {}
    for wname in ("wq", "wk", "wv", "wo"):
        W = np.asarray(ins["L0_ffn_" + wname])
        m["l0f_" + wname] = np.ascontiguousarray(
            W.reshape(D, 2, 512).transpose(1, 0, 2))
    fr = np.asarray(ins["L0_ffn_freqs_cis"]).reshape(T, 32, 2)
    m["cr"] = np.ascontiguousarray(np.tile(fr[:, :, 0], (1, NH)))
    m["ci"] = np.ascontiguousarray(np.tile(fr[:, :, 1], (1, NH)))
    m["g1_row"] = np.asarray(ins["L0_ffn_attn_norm_w"]).reshape(1, D)
    return m


def build_stage_b_program():
    nc = bass.Bass("TRN2", target_bir_lowering=False)
    ctx = Ctx(nc)
    x_d = nc.dram_tensor("x_in", [T, D], F32, kind="ExternalInput")
    out_d = nc.dram_tensor("x_out", [T, D], F32, kind="ExternalOutput")
    wd = declare_l0f_attn_weights(nc)
    cs = setup_constants(ctx)
    rg = alloc_regions(nc)
    sm = alloc_smalls(nc)
    sm["g1_row"] = nc.alloc_sbuf_tensor("g1_row_s", [1, D], BF16)
    sm["cr"] = nc.alloc_sbuf_tensor("cr_s", [128, 1024], BF16)
    sm["ci"] = nc.alloc_sbuf_tensor("ci_s", [128, 1024], BF16)
    st = alloc_ln_stats(nc, sq_backing=sm["ws"][:, :2 * D])
    emit_load_small(ctx, [(sm["g1_row"][:, :], wd["g1_row"][:, :])])
    emit_load_x(ctx, x_d, rg["r2"])
    stage_l0_ffn_attn(ctx, cs, st, sm, wd, x_in=rg["r2"], rA=rg["r0"],
                      rB=rg["r1"], rC=rg["r3"], rvp=rg["rvp"])
    emit_store_x(ctx, rg["r0"], out_d)
    nc.finalize()
    return nc



def emit_combine_weights(ctx, tag, rFm, cmbw, sm, cs, rw_key, dram_rw, nacc=DCH,
                         in_stride=T):
    nc = ctx.nc
    with nc.Block() as blk:
        @blk.gpsimd
        def _(g):
            ins = g.dma_start(sm[rw_key][:, :].rearrange("p (dc e) -> p dc e", e=8),
                              dram_rw)
            ctx.inc(ins, tag + "rw", 16)
    with (
        nc.psum_tensor(tag + "_p0", [128, 8], F32) as prt0,
        nc.psum_tensor(tag + "_p1", [128, 8], F32) as prt1,
        nc.Block() as blk,
    ):
        prt = [prt0, prt1]
        @blk.tensor
        def _(t):
            ctx.wait(t, tag + "rw")
            for tt in range(TCH):
                if tt >= 2:
                    ctx.wait(t, tag + "re", tt - 1)
                for dc in range(nacc):
                    ins = t.matmul(prt[tt % 2][:, :],
                                   rFm[:, dc * in_stride + tt * 128:
                                       dc * in_stride + (tt + 1) * 128],
                                   sm[rw_key][:, dc * 8:(dc + 1) * 8],
                                   start=(dc == 0), stop=(dc == nacc - 1))
                ctx.inc(ins, tag + "rp")

        @blk.vector
        def _(v):
            L = sm["cmb_l"]
            m1 = sm["cmb_s"][:, 0:1]
            m2 = sm["cmb_s"][:, 1:2]
            nm1 = sm["cmb_s"][:, 2:3]
            den = sm["cmb_s"][:, 3:4]
            eq = sm["cmb_t"][:, 0:8]
            l2 = sm["cmb_t"][:, 8:16]
            e1 = sm["cmb_t"][:, 16:24]
            ge = sm["cmb_t"][:, 24:32]
            for tt in range(TCH):
                ctx.wait(v, tag + "rp", tt + 1)
                ins = v.tensor_copy(L[:, :], prt[tt % 2][:, :])
                ctx.inc(ins, tag + "re")
                ctx.wait(v, tag + "re")
                ins = v.tensor_reduce(m1[:, :], L[:, :], AX.X, ALU.max)
                edge(ctx, v, ins, tag + "c")
                ins = v.tensor_scalar(eq[:, :], L[:, :], m1[:, :], None, ALU.is_equal)
                edge(ctx, v, ins, tag + "c")
                ins = v.scalar_tensor_tensor(l2[:, :], eq[:, :], -1e30, L[:, :],
                                             ALU.mult, ALU.add)
                edge(ctx, v, ins, tag + "c")
                ins = v.tensor_reduce(m2[:, :], l2[:, :], AX.X, ALU.max)
                edge(ctx, v, ins, tag + "c")
                ins = v.tensor_scalar(nm1[:, :], m1[:, :], -1.0, None, ALU.mult)
                ins = v.tensor_scalar(ge[:, :], L[:, :], m2[:, :], None, ALU.is_ge)
                ctx.inc(ins, tag + "cx")
                ctx.wait(v, tag + "ca", tt + 1)
                ins = v.tensor_tensor(e1[:, :], e1[:, :], ge[:, :], ALU.mult)
                edge(ctx, v, ins, tag + "c")
                ins = v.tensor_reduce(den[:, :], e1[:, :], AX.X, ALU.add)
                edge(ctx, v, ins, tag + "c")
                ins = v.reciprocal(den[:, :], den[:, :])
                edge(ctx, v, ins, tag + "c")
                ins = v.tensor_scalar(cmbw[:, tt * 8:(tt + 1) * 8], e1[:, :],
                                      den[:, :], None, ALU.mult)
                ctx.inc(ins, tag + "cv")

        @blk.scalar
        def _(a):
            L = sm["cmb_l"]
            nm1 = sm["cmb_s"][:, 2:3]
            e1 = sm["cmb_t"][:, 16:24]
            for tt in range(TCH):
                ctx.wait(a, tag + "cx", tt + 1)
                ins = a.activation(e1[:, :], L[:, :], AF.Exp, bias=nm1[:, :])
                ctx.inc(ins, tag + "ca")


def stage_l0_moe(ctx, cs, st, sm, wd, rH, rX1, rOut, rFm, h_sb, acc32):
    nc = ctx.nc
    TH = T // 2

    emit_bcast_rows_simple(ctx, "bc", [
        (sm["g2_row"][0:1, j * 512:(j + 1) * 512],
         sm["bc0"][:, j * 512:(j + 1) * 512]) for j in range(2)
    ], cs)
    emit_ln_stats_apply(ctx, "ln", rH, rOut, st, EPS_RMS, rms=True,
                        gamma_bc=sm["bc0"])
    with nc.Block() as blk:
        @blk.vector
        def _(v):
            for tc in range(TCH):
                v.tensor_tensor(rX1[:, tc * D:(tc + 1) * D],
                                rX1[:, tc * D:(tc + 1) * D],
                                rH[:, tc * D:(tc + 1) * D], ALU.add)
    emit_tok2fm(ctx, "t", rOut, rFm, cs)

    emit_combine_weights(ctx, "l0mr", rFm, sm["cmbw"], sm, cs, "rw",
                         wd["router_w"][:, :].rearrange("(dc p) e -> p dc e", p=128))

    cmbT = sm["cmbT"]
    with (
        nc.psum_tensor("l0mt_p", [8, 128], F32) as ptp,
        nc.Block() as blk,
    ):
        @blk.tensor
        def _(t):
            for tt in range(TCH):
                if tt >= 1:
                    ctx.wait(t, "l0mte", tt)
                ins = t.transpose(ptp[:, :], sm["cmbw"][:, tt * 8:(tt + 1) * 8],
                                  cs["c_identf"][:, :])
                ctx.inc(ins, "l0mtp")

        @blk.vector
        def _(v):
            for tt in range(TCH):
                ctx.wait(v, "l0mtp", tt + 1)
                ins = v.tensor_copy(cmbT[:, tt * 128:(tt + 1) * 128], ptp[:, :])
                ctx.inc(ins, "l0mte")
    with nc.Block() as blk:
        @blk.gpsimd
        def _(g):
            ins = g.dma_start(wd["cmb_d"][:, :], cmbT[:, :])
            ctx.inc(ins, "l0mtd", 16)
            ctx.wait(g, "l0mtd")

    w1s, w3s, w2s, stmp, cbrow = (sm["w1s"], sm["w3s"], sm["w2s"],
                                  sm["stmp"], sm["cbrow"])
    cbh = rOut[:, 8 * 1024:16 * 1024]
    cnt = {"wpair": 0, "w2": 0, "jh": 0, "jy": 0}

    for half in range(2):
        hof = half * TH
        with nc.Block() as blk:
            @blk.vector
            def _(v):
                v.memset(acc32[0][:, :], 0.0)
                v.memset(acc32[1][:, :], 0.0)
        with (
            nc.psum_tensor("l0mcb_p" + str(half), [128, 512], F32) as pcb,
            nc.Block() as blk,
        ):
            @blk.gpsimd
            def _(g):
                for th in range(2):
                    for e in range(8):
                        n = half * 16 + th * 8 + e
                        if n >= 1:
                            ctx.wait(g, "l0mbp", n)
                        src = bass.AP(wd["cmb_d"], e * T + hof + th * 512,
                                      [[1, 1], [1, 512]])
                        ins = g.dma_start(cbrow[:, :], src)
                        ctx.inc(ins, "l0mcw", 16)

            @blk.tensor
            def _(t):
                for th in range(2):
                    for e in range(8):
                        n = half * 16 + th * 8 + e
                        ctx.wait(t, "l0mcw", 16 * (n + 1))
                        if n >= 1:
                            ctx.wait(t, "l0mbe", n)
                        ins = t.matmul(pcb[:, :], cs["c_onesb"][0:1, :],
                                       cbrow[0:1, :],
                                       start=True, stop=True)
                        ctx.inc(ins, "l0mbp")

            @blk.vector
            def _(v):
                for th in range(2):
                    for e in range(8):
                        n = half * 16 + th * 8 + e
                        ctx.wait(v, "l0mbp", n + 1)
                        ins = v.tensor_copy(
                            cbh[:, e * 1024 + th * 512:e * 1024 + (th + 1) * 512],
                            pcb[:, :])
                        ctx.inc(ins, "l0mbe")

        wp0, w20 = cnt["wpair"], cnt["w2"]
        jh0, jy0 = cnt["jh"], cnt["jy"]
        h_end_of_e = {}
        y_end_of_e = {}
        jh, jy, wp, w2i = jh0, jy0, wp0, w20
        for e in range(8):
            jh += 32
            wp += 16
            h_end_of_e[e] = jh
            jy += 16
            w2i += 8
            y_end_of_e[e] = jy

        with (
            nc.psum_tensor("l0mh_a0" + str(half), [128, 512], F32) as ha0,
            nc.psum_tensor("l0mh_b0" + str(half), [128, 512], F32) as hb0,
            nc.psum_tensor("l0mh_a1" + str(half), [128, 512], F32) as ha1,
            nc.psum_tensor("l0mh_b1" + str(half), [128, 512], F32) as hb1,
            nc.psum_tensor("l0mh_a2" + str(half), [128, 512], F32) as ha2,
            nc.psum_tensor("l0mh_b2" + str(half), [128, 512], F32) as hb2,
            nc.psum_tensor("l0my_0" + str(half), [128, 512], F32) as yq0,
            nc.psum_tensor("l0my_1" + str(half), [128, 512], F32) as yq1,
            nc.Block() as blk,
        ):
            hpa = [ha0, ha1, ha2]
            hpb = [hb0, hb1, hb2]
            yps = [yq0, yq1]

            @blk.gpsimd
            def _(g):
                wpx, w2x = wp0, w20
                for e in range(8):
                    for ic in range(16):
                        if wpx >= 2:
                            ctx.wait(g, "mhp", (wpx - 1) * 2)
                        for wkey, dst in (("w1_l", w1s), ("w3_l", w3s)):
                            ins = g.dma_start(
                                dst[:, (wpx % 2) * 1024:(wpx % 2 + 1) * 1024]
                                .rearrange("p (dc c) -> p dc c", c=128),
                                wd[wkey][e * 16 + ic].rearrange(
                                    "(dc p) c -> p dc c", p=128))
                            ctx.inc(ins, "mhw" + str(wpx % 2), 16)
                        wpx += 1
                    for dc in range(8):
                        if w2x >= 2:
                            ctx.wait(g, "myp", (w2x - 1) * 2)
                        ins = g.dma_start(
                            w2s[:, (w2x % 2) * 2048:(w2x % 2 + 1) * 2048]
                            .rearrange("p (ic c) -> p ic c", c=128),
                            wd["w2_l"][e * 8 + dc].rearrange(
                                "(ic p) c -> p ic c", p=128))
                        ctx.inc(ins, "myw" + str(w2x % 2), 16)
                        w2x += 1

            @blk.tensor
            def _(t):
                jhx, jyx = jh0, jy0
                for e in range(8):
                    for ic in range(16):
                        wpair = wp0 + e * 16 + ic
                        ctx.wait(t, "mhw" + str(wpair % 2), 32 * (wpair // 2 + 1))
                        for th in range(2):
                            if jhx >= 3:
                                ctx.wait(t, "mhe", jhx - 2)
                                ctx.wait(t, "mha", jhx - 2)
                            pa, pb = hpa[jhx % 3], hpb[jhx % 3]
                            for dc in range(DCH):
                                t.matmul(pa[:, :],
                                         w1s[:, (wpair % 2) * 1024 + dc * 128:
                                             (wpair % 2) * 1024 + (dc + 1) * 128],
                                         rFm[:, dc * T + hof + th * 512:
                                             dc * T + hof + (th + 1) * 512],
                                         start=(dc == 0), stop=(dc == DCH - 1))
                            for dc in range(DCH):
                                ins = t.matmul(pb[:, :],
                                               w3s[:, (wpair % 2) * 1024 + dc * 128:
                                                   (wpair % 2) * 1024 + (dc + 1) * 128],
                                               rFm[:, dc * T + hof + th * 512:
                                                   dc * T + hof + (th + 1) * 512],
                                               start=(dc == 0), stop=(dc == DCH - 1))
                            ctx.inc(ins, "mhp")
                            jhx += 1
                    for dc in range(8):
                        w2x = w20 + e * 8 + dc
                        ctx.wait(t, "myw" + str(w2x % 2), 16 * (w2x // 2 + 1))
                        for th in range(2):
                            if jyx >= 2:
                                ctx.wait(t, "mye", jyx - 1)
                            ctx.wait(t, "mhe", h_end_of_e[e])
                            for ic in range(16):
                                ins = t.matmul(
                                    yps[jyx % 2][:, :],
                                    w2s[:, (w2x % 2) * 2048 + ic * 128:
                                        (w2x % 2) * 2048 + (ic + 1) * 128],
                                    h_sb[:, ic * 1024 + th * 512:
                                         ic * 1024 + (th + 1) * 512],
                                    start=(ic == 0), stop=(ic == 15))
                            ctx.inc(ins, "myp")
                            jyx += 1

            @blk.scalar
            def _(a):
                jhx = jh0
                for e in range(8):
                    for ic in range(16):
                        for th in range(2):
                            ctx.wait(a, "mhp", jhx + 1)
                            ins = a.activation(
                                stmp[:, (jhx % 3) * 512:(jhx % 3 + 1) * 512],
                                hpa[jhx % 3][:, :], AF.Silu)
                            ctx.inc(ins, "mha")
                            jhx += 1

            @blk.vector
            def _(v):
                jhx, jyx = jh0, jy0
                for e in range(8):
                    for ic in range(16):
                        for th in range(2):
                            ctx.wait(v, "mha", jhx + 1)
                            if ic == 0 and th == 0 and jyx > jy0 - 0 and e > 0:
                                ctx.wait(v, "myp", jyx)
                            elif e == 0 and ic == 0 and th == 0 and jy0 > 0:
                                ctx.wait(v, "myp", jy0)
                            dst = h_sb[:, ic * 1024 + th * 512:
                                       ic * 1024 + (th + 1) * 512]
                            ins = v.tensor_tensor(
                                dst, stmp[:, (jhx % 3) * 512:(jhx % 3 + 1) * 512],
                                hpb[jhx % 3][:, :], ALU.mult)
                            edge(ctx, v, ins, "mhc")
                            ins = v.tensor_tensor(
                                dst, dst,
                                cbh[:, e * 1024 + th * 512:e * 1024 + (th + 1) * 512],
                                ALU.mult)
                            ctx.inc(ins, "mhe")
                            jhx += 1
                    for dc in range(8):
                        for th in range(2):
                            ctx.wait(v, "myp", jyx + 1)
                            sl = acc32[th][:, dc * 512:(dc + 1) * 512]
                            ins = v.tensor_tensor(sl, sl, yps[jyx % 2][:, :], ALU.add)
                            ctx.inc(ins, "mye")
                            jyx += 1
        cnt["wpair"] = wp
        cnt["w2"] = w2i
        cnt["jh"] = jh
        cnt["jy"] = jy

        with (
            nc.psum_tensor("l0ma_p0" + str(half), [128, 128], F32) as tp0,
            nc.psum_tensor("l0ma_p1" + str(half), [128, 128], F32) as tp1,
            nc.Block() as blk,
        ):
            tps = [tp0, tp1]

            base_tp = ctx.val("mtp")
            base_te = ctx.val("mte")

            @blk.tensor
            def _(t):
                n = 0
                for ts in range(8):
                    for dc in range(8):
                        if n >= 2:
                            ctx.wait(t, "mte", base_te + n - 1)
                        ins = t.transpose(
                            tps[n % 2][:, :],
                            acc32[ts // 4]
                            [:, dc * 512 + (ts % 4) * 128:
                             dc * 512 + (ts % 4 + 1) * 128],
                            cs["c_identf"][:, :])
                        ctx.inc(ins, "mtp")
                        n += 1

            @blk.vector
            def _(v):
                n = 0
                for ts in range(8):
                    tc = half * 8 + ts
                    for dc in range(8):
                        ctx.wait(v, "mtp", base_tp + n + 1)
                        ins = v.tensor_add(
                            rOut[:, tc * D + dc * 128:tc * D + (dc + 1) * 128],
                            tps[n % 2][:, :],
                            rX1[:, tc * D + dc * 128:tc * D + (dc + 1) * 128])
                        ctx.inc(ins, "mte")
                        n += 1

    emit_ln_stats_apply(ctx, "ln", rOut, rOut, st, EPS_LN)


def declare_l0_moe_weights(nc):
    wd = {}
    w1 = nc.dram_tensor("w1_l", [8 * 16, D, 128], F32, kind="ExternalInput")
    wd["w1_l"] = [w1[i] for i in range(128)]
    w3 = nc.dram_tensor("w3_l", [8 * 16, D, 128], F32, kind="ExternalInput")
    wd["w3_l"] = [w3[i] for i in range(128)]
    w2 = nc.dram_tensor("w2_l", [8 * 8, 16 * 128, 128], F32, kind="ExternalInput")
    wd["w2_l"] = [w2[i] for i in range(64)]
    wd["router_w"] = nc.dram_tensor("router_w", [D, 8], F32, kind="ExternalInput")
    wd["g2_row"] = nc.dram_tensor("g2_row", [1, D], F32, kind="ExternalInput")
    wd["cmb_d"] = nc.dram_tensor("cmb_d", [8, T], BF16)
    return wd


def host_l0_moe_weight_maps(ins):
    m = {}
    w1 = np.asarray(ins["L0_ffn_w1"])
    m["w1_l"] = np.ascontiguousarray(
        w1.reshape(8, D, 16, 128).transpose(0, 2, 1, 3).reshape(128, D, 128))
    w3 = np.asarray(ins["L0_ffn_w3"])
    m["w3_l"] = np.ascontiguousarray(
        w3.reshape(8, D, 16, 128).transpose(0, 2, 1, 3).reshape(128, D, 128))
    w2 = np.asarray(ins["L0_ffn_w2"])
    m["w2_l"] = np.ascontiguousarray(
        w2.reshape(8, 16, 128, 8, 128).transpose(0, 3, 1, 2, 4)
        .reshape(64, 16 * 128, 128))
    m["router_w"] = np.asarray(ins["L0_ffn_router_w"])
    m["g2_row"] = np.asarray(ins["L0_ffn_ffn_norm_w"]).reshape(1, D)
    return m


def alloc_moe_smalls(nc, sm, ar=None):
    uid = [0]
    def _mk(sh, dt):
        uid[0] += 1
        return nc.alloc_sbuf_tensor("moe_%d" % uid[0], sh, dt)
    A = ar.take if ar else _mk
    sm["rw"] = A([128, 64], BF16)
    sm["cmbw"] = A([128, TCH * 8], F32)
    sm["cmb_l"] = A([128, 8], F32)
    sm["cmb_s"] = A([128, 4], F32)
    sm["cmb_t"] = A([128, 32], F32)
    sm["cmbT"] = A([8, T], BF16)
    ws = sm["ws"]
    sm["w1s"] = ws[:, 0:2048]
    sm["w3s"] = ws[:, 2048:4096]
    sm["w2s"] = ws[:, 4096:8192]
    sm["stmp"] = sm["bc0"].bitcast(BF16)[:, 0:1536]
    sm["cbrow"] = (ar.take([1, 512], BF16) if ar else
                   nc.alloc_sbuf_tensor("cbrow", [1, 512], BF16))
    return sm


def emit_load_x2(ctx, x_d, xa):
    nc = ctx.nc
    with nc.Block() as blk:
        @blk.gpsimd
        def _(g):
            for tc in range(TCH):
                ins = g.dma_start(xa[:, tc * D:(tc + 1) * D],
                                  x_d[tc * 128:(tc + 1) * 128, :])
                ctx.inc(ins, "xld2", 16)
            ctx.wait(g, "xld2")


def build_stage_c_program():
    nc = bass.Bass("TRN2", target_bir_lowering=False)
    ctx = Ctx(nc)
    h_d = nc.dram_tensor("h_in", [T, D], F32, kind="ExternalInput")
    x1_d = nc.dram_tensor("x1_in", [T, D], F32, kind="ExternalInput")
    out_d = nc.dram_tensor("x_out", [T, D], F32, kind="ExternalOutput")
    wd = declare_l0_moe_weights(nc)
    cs = setup_constants(ctx)
    rg = alloc_regions(nc)
    sm = alloc_smalls(nc)
    sm["g2_row"] = nc.alloc_sbuf_tensor("g2_row_s", [1, D], BF16)
    alloc_moe_smalls(nc, sm)
    st = alloc_ln_stats(nc, sq_backing=sm["ws"][:, :2 * D])
    emit_load_small(ctx, [(sm["g2_row"][:, :], wd["g2_row"][:, :])])
    emit_load_x(ctx, h_d, rg["r0"])
    emit_load_x2(ctx, x1_d, rg["r2"])
    acc32 = [rg["r0"].bitcast(F32)[:, 0:4096], rg["r0"].bitcast(F32)[:, 4096:8192]]
    stage_l0_moe(ctx, cs, st, sm, wd, rH=rg["r0"], rX1=rg["r2"], rOut=rg["r1"],
                 rFm=rg["r3"], h_sb=rg["rvp"][:, :16 * 1024], acc32=acc32)
    emit_store_x(ctx, rg["r1"], out_d)
    nc.finalize()
    return nc



def stage_ttt(ctx, cs, st, sm, wd, rX, rFm, rTH, rvp, wreg, extra_gdma=None):
    nc = ctx.nc
    ws = sm["ws"]
    ktok = ws[:, 0:1024]
    vtok = ws[:, 1024:2048]
    qtok = ws[:, 2048:3072]
    kfm = ws[:, 3072:4096]
    qfm = ws[:, 4096:5120]
    ubf = ws[:, 5120:6144]
    Xs = [ws[:, 6144:6272], ws[:, 6400:6528]]
    Ys = [ws[:, 6272:6400], ws[:, 6528:6656]]
    Msb = ws[:, 6656:6784]
    sq2 = ws[:, 6784:7808]
    wbf = rvp[:, 0:8192]
    thq = rvp[:, 8192:16384]
    bbc = sm["bc1"]
    tmpO = sm["ttO"]
    tmpB = sm["bc0"]
    brow = sm["tt_brow"]
    Uf = sm["ttU"]
    st2 = dict(st)
    st2["sq"] = sq2

    emit_tok2fm(ctx, "t", rX, rFm, cs)

    with nc.Block() as blk:
        @blk.gpsimd
        def _(g):
            for dc in range(DCH):
                for key, dst in (("thK", rTH[:, dc * 1024:(dc + 1) * 1024]),
                                 ("thV", rTH[:, 8192 + dc * 1024:8192 + (dc + 1) * 1024]),
                                 ("thQ", thq[:, dc * 1024:(dc + 1) * 1024])):
                    ins = g.dma_start(dst, wd[key][dc])
                    ctx.inc(ins, "ttw", 16)
            for ec in range(DCH):
                ins = g.dma_start(wreg[:, ec * 1024:(ec + 1) * 1024], wd["w0T"][ec])
                ctx.inc(ins, "ttw", 16)
                ins = g.dma_start(wbf[:, ec * 1024:(ec + 1) * 1024], wd["w0T"][ec])
                ctx.inc(ins, "ttw", 16)
            ins = g.dma_start(brow[:, :], wd["b0"][:, :])
            ctx.inc(ins, "ttw", 16)
            if extra_gdma is not None:
                extra_gdma(g)
            ctx.wait(g, "ttw")
    emit_bcast_rows_simple(ctx, "bc", [
        (brow[0:1, j * 512:(j + 1) * 512], bbc[:, j * 512:(j + 1) * 512])
        for j in range(2)], cs)

    for c in range(TCH):
        tag = "tt"
        jobs, meta = [], []
        for src, dst in ((rTH, ktok), (None, vtok), (None, qtok)):
            pass
        proj_srcs = [(0, ktok), (1, vtok), (2, qtok)]
        jobs = []
        dsts = []
        for wi, dstt in proj_srcs:
            for it in range(2):
                job = []
                for dc in range(DCH):
                    if wi == 0:
                        rhs = rTH[:, dc * 1024 + it * 512:dc * 1024 + (it + 1) * 512]
                    elif wi == 1:
                        rhs = rTH[:, 8192 + dc * 1024 + it * 512:
                                  8192 + dc * 1024 + (it + 1) * 512]
                    else:
                        rhs = thq[:, dc * 1024 + it * 512:dc * 1024 + (it + 1) * 512]
                    job.append((rFm[:, dc * T + c * 128:dc * T + (c + 1) * 128], rhs))
                jobs.append(job)
                dsts.append((dstt, it))

        def proj_evac(n, ps, v):
            dstt, it = dsts[n]
            return v.tensor_copy(dstt[:, it * 512:(it + 1) * 512], ps)

        with (
            nc.psum_tensor(tag + str(c) + "p0", [128, 512], F32) as p0,
            nc.psum_tensor(tag + str(c) + "p1", [128, 512], F32) as p1,
            nc.psum_tensor(tag + str(c) + "p2", [128, 512], F32) as p2,
            nc.psum_tensor(tag + str(c) + "p3", [128, 512], F32) as p3,
        ):
            emit_matmul_tiles(ctx, "mm", jobs,
                              [p0[:, :], p1[:, :], p2[:, :], p3[:, :]], proj_evac)

        pairs = []
        for ec in range(DCH):
            pairs.append((ktok[:, ec * 128:(ec + 1) * 128],
                          kfm[:, ec * 128:(ec + 1) * 128]))
            pairs.append((qtok[:, ec * 128:(ec + 1) * 128],
                          qfm[:, ec * 128:(ec + 1) * 128]))
        emit_transposes(ctx, "t", pairs, cs)

        with (
            nc.psum_tensor(tag + str(c) + "B0", [128, 512], F32) as B0,
            nc.psum_tensor(tag + str(c) + "B1", [128, 512], F32) as B1,
            nc.psum_tensor(tag + str(c) + "A", [128, 128], F32) as Aps,
            nc.psum_tensor(tag + str(c) + "KQ", [128, 128], F32) as KQps,
            nc.psum_tensor(tag + str(c) + "O0", [128, 512], F32) as O0,
            nc.psum_tensor(tag + str(c) + "O1", [128, 512], F32) as O1,
            nc.psum_tensor(tag + str(c) + "S0", [128, 512], F32) as S0,
            nc.psum_tensor(tag + str(c) + "S1", [128, 512], F32) as S1,
        ):
            Bps = [B0, B1]
            Ops = [O0, O1]
            Sps = [S0, S1]
            with nc.Block() as blk:
                @blk.tensor
                def _(t):
                    for it in range(2):
                        for ec in range(DCH):
                            t.matmul(Bps[it][:, :],
                                     kfm[:, ec * 128:(ec + 1) * 128],
                                     wbf[:, ec * 1024 + it * 512:ec * 1024 + (it + 1) * 512],
                                     start=(ec == 0), stop=(ec == DCH - 1))
                    for ec in range(DCH):
                        t.matmul(Aps[:, :], kfm[:, ec * 128:(ec + 1) * 128],
                                 kfm[:, ec * 128:(ec + 1) * 128],
                                 start=(ec == 0), stop=(ec == DCH - 1))
                    for ec in range(DCH):
                        t.matmul(KQps[:, :], kfm[:, ec * 128:(ec + 1) * 128],
                                 qfm[:, ec * 128:(ec + 1) * 128],
                                 start=(ec == 0), stop=(ec == DCH - 1))
                    for it in range(2):
                        for ec in range(DCH):
                            ins = t.matmul(Ops[it][:, :],
                                           qfm[:, ec * 128:(ec + 1) * 128],
                                           wbf[:, ec * 1024 + it * 512:
                                               ec * 1024 + (it + 1) * 512],
                                           start=(ec == 0), stop=False)
                    ctx.inc(ins, tag + "mm1")
            with nc.Block() as blk:
                @blk.vector
                def _(v):
                    ctx.wait(v, tag + "mm1")
                    for it in range(2):
                        sl = slice(it * 512, (it + 1) * 512)
                        ins = v.tensor_add(tmpB[:, sl], Bps[it][:, :], bbc[:, sl])
                        edge(ctx, v, ins, tag + "c")
                        ins = v.tensor_tensor(tmpB[:, sl], tmpB[:, sl], vtok[:, sl],
                                              ALU.subtract)
                        edge(ctx, v, ins, tag + "c")
                        ins = v.tensor_scalar(Uf[:, sl], tmpB[:, sl], -A_TTT, None,
                                              ALU.mult)
                        edge(ctx, v, ins, tag + "c")
                        v.tensor_copy(ubf[:, sl], Uf[:, sl])
                    ins = v.scalar_tensor_tensor(Xs[0][:, :], Aps[:, :], 1.0,
                                                 cs["c_trils"][:, :], ALU.add, ALU.mult)
                    ins = v.scalar_tensor_tensor(Ys[0][:, :], Aps[:, :], 1.0,
                                                 cs["c_trius"][:, :], ALU.add, ALU.mult)
                    ins = v.scalar_tensor_tensor(Msb[:, :], KQps[:, :], 1.0,
                                                 cs["c_uincl"][:, :], ALU.add, ALU.mult)

            base_svu = ctx.val(tag + "svu")
            with nc.Block() as blk:
                @blk.tensor
                def _(t):
                    for lvl in range(7):
                        cur = lvl % 2
                        if lvl >= 1:
                            ctx.wait(t, tag + "svu", base_svu + lvl)
                        for it in range(2):
                            ins = t.matmul(Sps[it][:, :], Ys[cur][:, :],
                                           ubf[:, it * 512:(it + 1) * 512],
                                           start=True, stop=True)
                        ctx.inc(ins, tag + "sva")
                        if lvl < 6:
                            t.matmul(KQps[:, :], Ys[cur][:, :], Xs[cur][:, :],
                                     start=True, stop=True)
                            ins = t.matmul(Aps[:, :], Xs[cur][:, :], Ys[cur][:, :],
                                           start=True, stop=True)
                            ctx.inc(ins, tag + "svb")

                @blk.vector
                def _(v):
                    for lvl in range(7):
                        nxt = (lvl + 1) % 2
                        ctx.wait(v, tag + "sva", ctx.val(tag + "sva") - 6 + lvl)
                        op = ALU.subtract if lvl == 0 else ALU.add
                        for it in range(2):
                            sl = slice(it * 512, (it + 1) * 512)
                            ins = v.tensor_tensor(Uf[:, sl], Uf[:, sl],
                                                  Sps[it][:, :], op)
                            edge(ctx, v, ins, tag + "c")
                            ins = v.tensor_copy(ubf[:, sl], Uf[:, sl])
                        if lvl < 6:
                            ctx.wait(v, tag + "svb", ctx.val(tag + "svb") - 5 + lvl)
                            v.tensor_copy(Xs[nxt][:, :], KQps[:, :])
                            ins = v.tensor_copy(Ys[nxt][:, :], Aps[:, :])
                        ctx.inc(ins, tag + "svu")

            with nc.Block() as blk:
                @blk.tensor
                def _(t):
                    for it in range(2):
                        ins = t.matmul(Ops[it][:, :], Msb[:, :],
                                       ubf[:, it * 512:(it + 1) * 512],
                                       start=False, stop=True)
                    ctx.inc(ins, tag + "o2")
                @blk.vector
                def _(v):
                    ctx.wait(v, tag + "o2")
                    for it in range(2):
                        sl = slice(it * 512, (it + 1) * 512)
                        ins = v.tensor_add(tmpO[:, sl], Ops[it][:, :], bbc[:, sl])
                        edge(ctx, v, ins, tag + "c")
                        v.tensor_tensor(tmpO[:, sl], tmpO[:, sl],
                                        rX[:, c * D + it * 512:c * D + (it + 1) * 512],
                                        ALU.add)

            emit_ln_stats_apply(ctx, "ln", tmpO, rX[:, c * D:(c + 1) * D],
                                st2, EPS_LN, n_t=1)

            base_we = ctx.val(tag + "we")
            base_wp = ctx.val(tag + "wp")
            with nc.Block() as blk:
                @blk.tensor
                def _(t):
                    n = 0
                    for ec in range(DCH):
                        for it in range(2):
                            if n >= 2:
                                ctx.wait(t, tag + "we", base_we + n - 1)
                            ins = t.matmul(Sps[n % 2][:, :],
                                           ktok[:, ec * 128:(ec + 1) * 128],
                                           ubf[:, it * 512:(it + 1) * 512],
                                           start=True, stop=True)
                            ctx.inc(ins, tag + "wp")
                            n += 1
                    for it in range(2):
                        ins = t.matmul(Bps[it][0:1, :], cs["c_onesb"][:, 0:1],
                                       ubf[:, it * 512:(it + 1) * 512],
                                       start=True, stop=True)
                    ctx.inc(ins, tag + "bp")
                @blk.vector
                def _(v):
                    n = 0
                    for ec in range(DCH):
                        for it in range(2):
                            ctx.wait(v, tag + "wp", base_wp + n + 1)
                            sl = slice(ec * 1024 + it * 512, ec * 1024 + (it + 1) * 512)
                            ins = v.tensor_tensor(wreg[:, sl], wreg[:, sl],
                                                  Sps[n % 2][:, :], ALU.add)
                            edge(ctx, v, ins, tag + "c")
                            ins = v.tensor_copy(wbf[:, sl], wreg[:, sl])
                            ctx.inc(ins, tag + "we")
                            n += 1
                    ctx.wait(v, tag + "bp")
                    for it in range(2):
                        sl = slice(it * 512, (it + 1) * 512)
                        ins = v.tensor_tensor(brow[0:1, sl], brow[0:1, sl],
                                              Bps[it][0:1, :], ALU.add)
                    ctx.inc(ins, tag + "bv")
            with nc.Block() as blk:
                @blk.tensor
                def _(t):
                    ctx.wait(t, tag + "bv")
                    for it in range(2):
                        ins = t.matmul(Sps[it][:, :], cs["c_ones"][0:1, :],
                                       brow[0:1, it * 512:(it + 1) * 512],
                                       start=True, stop=True)
                    ctx.inc(ins, tag + "b2")
                @blk.vector
                def _(v):
                    ctx.wait(v, tag + "b2")
                    for it in range(2):
                        v.tensor_copy(bbc[:, it * 512:(it + 1) * 512],
                                      Sps[it][:, :])


def declare_ttt_weights(nc):
    wd = {}
    for key, src in (("thK", "L1_attn_theta_K"), ("thV", "L1_attn_theta_V"),
                     ("thQ", "L1_attn_theta_Q")):
        w = nc.dram_tensor(key, [DCH, 128, D], F32, kind="ExternalInput")
        wd[key] = [w[i] for i in range(DCH)]
    w = nc.dram_tensor("w0T", [DCH, 128, D], F32, kind="ExternalInput")
    wd["w0T"] = [w[i] for i in range(DCH)]
    wd["b0"] = nc.dram_tensor("b0", [1, D], F32, kind="ExternalInput")
    return wd


def host_ttt_weight_maps(ins):
    m = {}
    for key, src in (("thK", "L1_attn_theta_K"), ("thV", "L1_attn_theta_V"),
                     ("thQ", "L1_attn_theta_Q")):
        W = np.asarray(ins[src])
        m[key] = np.ascontiguousarray(W.T.reshape(DCH, 128, D))
    W0 = np.asarray(ins["L1_attn_W0"])
    m["w0T"] = np.ascontiguousarray(W0.T.reshape(DCH, 128, D))
    m["b0"] = np.asarray(ins["L1_attn_b0"]).reshape(1, D)
    return m


def alloc_ttt_smalls(nc, sm, ar=None):
    A = ar.take if ar else (lambda sh, dt: nc.alloc_sbuf_tensor(
        "ttt_%d_%d" % (sh[0], sh[1]), sh, dt))
    sm["ttO"] = sm["bc0"]
    sm["ttU"] = A([128, D], F32)
    sm["tt_brow"] = A([1, D], F32)
    return sm


def build_stage_d_program():
    nc = bass.Bass("TRN2", target_bir_lowering=False)
    ctx = Ctx(nc)
    x_d = nc.dram_tensor("x_in", [T, D], F32, kind="ExternalInput")
    out_d = nc.dram_tensor("x_out", [T, D], F32, kind="ExternalOutput")
    wd = declare_ttt_weights(nc)
    cs = setup_constants(ctx)
    rg = alloc_regions(nc)
    sm = alloc_smalls(nc)
    alloc_ttt_smalls(nc, sm)
    st = alloc_ln_stats(nc, sq_backing=sm["ws"][:, :2 * D])
    emit_load_x(ctx, x_d, rg["r1"])
    stage_ttt(ctx, cs, st, sm, wd, rX=rg["r1"], rFm=rg["r3"], rTH=rg["r0"],
              rvp=rg["rvp"], wreg=rg["r2"].bitcast(F32))
    emit_store_x(ctx, rg["r1"], out_d)
    nc.finalize()
    return nc



CAPSL = 20


def stage_l1_ffn(ctx, cs, st, sm, wd, rX, rLT, cta, ctb, h_sb, rvp, out_d,
                 r0=None):
    nc = ctx.nc
    ws = sm["ws"]
    sm["lcum"] = cta[0:8, 4104:4104 + T]

    with nc.Block() as blk:
        @blk.sync
        def _(s):
            ins = s.dma_start(
                wd["xg_loc"][:, :].rearrange("(tt p) d -> p tt d", p=128),
                rX[:, :].rearrange("p (tt d) -> p tt d", d=D))
            ctx.inc(ins, "e4s", 16)
        @blk.gpsimd
        def _(g):
            g.wait_ge(ctx.sem("e4s"), 16)
            ins = g.collective_compute(
                "AllGather", ALU.bypass, replica_groups=[list(range(N_CORES))],
                ins=[wd["xg_loc"].ap().opt()], outs=[wd["xg_all"].ap().opt()])
            ctx.inc(ins, "ccx")

    emit_tok2fm(ctx, "t", rX, rLT, cs)

    emit_bcast_rows_simple(ctx, "bc", [(sm["bg_row"][0:1, 0:8], sm["bg_bc"][:, 0:8])],
                           cs)
    lg_tok = sm["lg_tok"]
    with nc.Block() as blk:
        @blk.gpsimd
        def _(g):
            ins = g.dma_start(sm["gw"][:, :].rearrange("p (dc e) -> p dc e", e=8),
                              wd["Wg"][:, :].rearrange("(dc p) e -> p dc e", p=128))
            ctx.inc(ins, "e2w", 16)
    with (
        nc.psum_tensor("e2_p0", [128, 8], F32) as pg0,
        nc.psum_tensor("e2_p1", [128, 8], F32) as pg1,
        nc.Block() as blk,
    ):
        pgs = [pg0, pg1]
        @blk.tensor
        def _(t):
            ctx.wait(t, "e2w")
            for tt in range(TCH):
                if tt >= 2:
                    ctx.wait(t, "e2e", tt - 1)
                for dc in range(DCH):
                    ins = t.matmul(pgs[tt % 2][:, :],
                                   rLT[:, dc * T + tt * 128:dc * T + (tt + 1) * 128],
                                   sm["gw"][:, dc * 8:(dc + 1) * 8],
                                   start=(dc == 0), stop=(dc == DCH - 1))
                ctx.inc(ins, "e2p")
        @blk.vector
        def _(v):
            for tt in range(TCH):
                ctx.wait(v, "e2p", tt + 1)
                ins = v.tensor_add(lg_tok[:, tt * 8:(tt + 1) * 8], pgs[tt % 2][:, :],
                                   sm["bg_bc"][:, 0:8])
                ctx.inc(ins, "e2e")
    L3 = lg_tok[:, :].rearrange("p (t e) -> p t e", e=8)
    m1, m2 = sm["e_m1"], sm["e_m2"]
    eqm1, eqm2, gem = sm["e_eq1"], sm["e_eq2"], sm["e_ge"]
    w1t, w2t = sm["e_w1"], sm["e_w2"]
    with nc.Block() as blk:
        @blk.vector
        def _(v):
            ins = v.tensor_reduce(m1[:, :], L3, AX.X, ALU.max)
            edge(ctx, v, ins, "ec")
            ins = v.tensor_tensor(eqm1[:, :].rearrange("p (t e) -> p t e", e=8),
                                  L3, m1[:, :].unsqueeze(2).broadcast_to((128, 16, 8)),
                                  ALU.is_equal)
            edge(ctx, v, ins, "ec")
            ins = v.scalar_tensor_tensor(eqm2[:, :], eqm1[:, :], -1e30,
                                         lg_tok[:, :], ALU.mult, ALU.add)
            edge(ctx, v, ins, "ec")
            ins = v.tensor_reduce(m2[:, :], eqm2[:, :].rearrange(
                "p (t e) -> p t e", e=8), AX.X, ALU.max)
            edge(ctx, v, ins, "ec")
            ins = v.tensor_tensor(gem[:, :].rearrange("p (t e) -> p t e", e=8),
                                  L3, m2[:, :].unsqueeze(2).broadcast_to((128, 16, 8)),
                                  ALU.is_ge)
            edge(ctx, v, ins, "ec")
            ins = v.tensor_tensor(eqm2[:, :].rearrange("p (t e) -> p t e", e=8),
                                  L3, m2[:, :].unsqueeze(2).broadcast_to((128, 16, 8)),
                                  ALU.is_equal)
            edge(ctx, v, ins, "ec")
            ins = v.tensor_tensor(w2t[:, :], m2[:, :], m1[:, :], ALU.subtract)
            ctx.inc(ins, "ecx")
        @blk.scalar
        def _(a):
            ctx.wait(a, "ecx")
            ins = a.activation(w2t[:, :], w2t[:, :], AF.Exp)
            ctx.inc(ins, "eca")
        @blk.vector
        def _(v):
            ctx.wait(v, "eca")
            ins = v.tensor_scalar(w1t[:, :], w2t[:, :], 1.0, None, ALU.add)
            edge(ctx, v, ins, "ec")
            ins = v.reciprocal(w1t[:, :], w1t[:, :])
            edge(ctx, v, ins, "ec")
            ins = v.tensor_tensor(w2t[:, :], w2t[:, :], w1t[:, :], ALU.mult)
            edge(ctx, v, ins, "ec")

    labl = sm["labl"]
    with (
        nc.psum_tensor("e5t0", [8, 128], F32) as pt0,
        nc.psum_tensor("e5t1", [8, 128], F32) as pt1,
        nc.Block() as blk,
    ):
        pts = [pt0, pt1]
        @blk.tensor
        def _(t):
            for n in range(TCH):
                if n >= 2:
                    ctx.wait(t, "e5e", n - 1)
                ins = t.transpose(pts[n % 2][:, :], gem[:, n * 8:(n + 1) * 8],
                                  cs["c_identf"][:, :])
                ctx.inc(ins, "e5p")
        @blk.vector
        def _(v):
            for n in range(TCH):
                ctx.wait(v, "e5p", n + 1)
                ins = v.tensor_copy(labl[0:8, n * 128:(n + 1) * 128], pts[n % 2][:, :])
                ctx.inc(ins, "e5e")
    with nc.Block() as blk:
        @blk.sync
        def _(s):
            ins = s.dma_start(wd["lab_loc"][:, :], labl[0:8, :])
            ctx.inc(ins, "e3s", 16)
        @blk.gpsimd
        def _(g):
            g.wait_ge(ctx.sem("e3s"), 16)
            ins = g.collective_compute(
                "AllGather", ALU.bypass, replica_groups=[list(range(N_CORES))],
                ins=[wd["lab_loc"].ap().opt()], outs=[wd["lab_all"].ap().opt()])
            ctx.inc(ins, "ccl")
    with nc.Block() as blk:
        @blk.gpsimd
        def _(g):
            ctx.wait(g, "ccl")
            src = bass.AP(wd["lab_all"], 0, [[T, 8], [8 * T, 8], [1, T]])
            ins = g.dma_start(
                rLT[0:8, 0:N_CORES * T].rearrange("e (k t) -> e k t", t=T), src)
            ctx.inc(ins, "e5w", 16)
            ctx.wait(g, "e5w")

    strip = cta[0:8, 0:4096]
    lastc = cta[0:8, 4096:4097]
    with nc.Block() as blk:
        @blk.vector
        def _(v):
            v.memset(lastc, 0.0)
    for q in range(4):
        with nc.Block() as blk:
            @blk.vector
            def _(v):
                if q >= 1:
                    ctx.wait(v, "e6s", 16 * q)
                ins = v.tensor_tensor_scan(strip, rLT[0:8, q * 4096:(q + 1) * 4096],
                                           rLT[0:8, q * 4096:(q + 1) * 4096],
                                           lastc, ALU.add, ALU.bypass)
                edge(ctx, v, ins, "ec")
                v.tensor_copy(lastc, strip[:, 4095:4096])
        with (
            nc.psum_tensor("e6_p0q%d" % q, [1, 512], F32) as pc0,
            nc.psum_tensor("e6_p1q%d" % q, [1, 512], F32) as pc1,
            nc.Block() as blk,
        ):
            pcs = [pc0, pc1]
            base6e = ctx.val("e6e")
            base6p = ctx.val("e6p")
            base6g = ctx.val("e6g")
            @blk.sync
            def _(s):
                ins = s.dma_start(
                    wd["c_all"][:, :].rearrange("(e k) t -> e k t", k=8)
                    [:, 2 * q:2 * q + 2, :],
                    strip.rearrange("e (k t) -> e k t", t=2048))
                ctx.inc(ins, "e6s", 16)
            @blk.tensor
            def _(t):
                for j in range(8):
                    if j >= 2:
                        ctx.wait(t, "e6e", base6e + j - 1)
                    ins = t.matmul(pcs[j % 2][:, :], sm["myhot"][0:8, 0:1],
                                   strip[:, j * 512:(j + 1) * 512],
                                   start=True, stop=True)
                    ctx.inc(ins, "e6p")
            base_g0 = ctx.val("e6g0")
            base_g1 = ctx.val("e6g1")

            @blk.vector
            def _(v):
                for j in range(8):
                    ctx.wait(v, "e6p", base6p + j + 1)
                    if j >= 2:
                        ctx.wait(v, "e6g" + str(j % 2),
                                 (base_g0 if j % 2 == 0 else base_g1) + 16 * (j // 2))
                    ins = v.tensor_copy(
                        sm["crow2"][0:1, (j % 2) * 512:(j % 2 + 1) * 512],
                        pcs[j % 2][:, :])
                    ctx.inc(ins, "e6e")
            @blk.gpsimd
            def _(g):
                for j in range(8):
                    ctx.wait(g, "e6e", base6e + j + 1)
                    ins = g.dma_start(
                        bass.AP(wd["cseg_d"], q * 4096 + j * 512, [[1, 1], [1, 512]]),
                        sm["crow2"][0:1, (j % 2) * 512:(j % 2 + 1) * 512])
                    ctx.inc(ins, "e6g" + str(j % 2), 16)
                ctx.wait(g, "e6g0")
                ctx.wait(g, "e6g1")

    with nc.Block() as blk:
        @blk.gpsimd
        def _(g):
            ins = g.dma_start(sm["cseg_s"][:, :], wd["cseg_d"][:, :])
            ctx.inc(ins, "e7w", 16)
            g.iota(sm["svals"][:, :], [[0, 1]], base=0, channel_multiplier=1,
                   allow_small_or_imprecise_dtypes=True)
    with (
        nc.psum_tensor("e7_t", [1, 128], F32) as ptr,
        nc.psum_tensor("e7_b0", [128, 128], F32) as pb0,
        nc.psum_tensor("e7_b1", [128, 128], F32) as pb1,
    ):
        with nc.Block() as blk:
            @blk.tensor
            def _(t):
                ctx.wait(t, "e7w")
                ins = t.transpose(ptr[:, :], sm["cseg_s"][:, 127:128],
                                  cs["c_identf"][:, :])
                ctx.inc(ins, "e7tp")
            @blk.vector
            def _(v):
                ctx.wait(v, "e7tp")
                ins = v.tensor_copy(sm["prow"][0:1, :], ptr[:, :])
                ctx.inc(ins, "e7v")
                ctx.wait(v, "e7v")
                ins = v.tensor_copy(sm["crow"][0:1, 1:128], sm["prow"][0:1, 0:127])
                edge(ctx, v, ins, "ec")
                ins = v.memset(sm["crow"][0:1, 0:1], 0.0)
                edge(ctx, v, ins, "ec")
                ins = v.tensor_tensor(sm["crow"][0:1, 0:128], sm["prow"][0:1, :],
                                      sm["crow"][0:1, 0:128], ALU.subtract)
                ctx.inc(ins, "e7v")
        with nc.Block() as blk:
            @blk.tensor
            def _(t):
                ins = t.matmul(pb0[:, :], cs["c_ones"][0:1, :], sm["prow"][0:1, :],
                               start=True, stop=True)
                ins = t.matmul(pb1[:, :], cs["c_ones"][0:1, :],
                               sm["crow"][0:1, 0:128], start=True, stop=True)
                ctx.inc(ins, "e7t2")
            @blk.vector
            def _(v):
                ctx.wait(v, "e7t2")
                v.tensor_copy(sm["pbc"][:, :], pb0[:, :])
                v.tensor_copy(sm["cbc"][:, :], pb1[:, :])
    with nc.Block() as blk:
        @blk.vector
        def _(v):
            for s_t in range(CAPSL):
                ins = v.tensor_scalar(sm["sv2"][:, s_t:s_t + 1], sm["svals"][:, :],
                                      float(s_t * 128), None, ALU.add)
                edge(ctx, v, ins, "ec")
                ins = v.tensor_scalar(sm["msk"][:, :], sm["pbc"][:, :],
                                      sm["sv2"][:, s_t:s_t + 1], None, ALU.is_le)
                edge(ctx, v, ins, "ec")
                ins = v.tensor_reduce(sm["gval"][:, s_t:s_t + 1], sm["msk"][:, :],
                                      AX.X, ALU.add)
                edge(ctx, v, ins, "ec")
            ins = v.tensor_copy(sm["gidx"][:, :], sm["gval"][:, :])
            edge(ctx, v, ins, "ec")
            v.memset(sm["cg"][:, :], 0.0)
    for s_t in range(CAPSL):
        with nc.Block() as blk:
            @blk.gpsimd
            def _(g):
                ins = g.indirect_dma_start(
                    sm["cg"][:, :], None, wd["cseg_d"][:, :],
                    IndirectOffsetOnAxis(sm["gidx"][:, s_t:s_t + 1], 0),
                    bounds_check=ctx.breg(g, 127), oob_is_err=False)
                ctx.inc(ins, "e8g", 16)
                ctx.wait(g, "e8g")
        with nc.Block() as blk:
            @blk.vector
            def _(v):
                ins = v.tensor_scalar(sm["msk"][:, :], sm["cg"][:, :],
                                      sm["sv2"][:, s_t:s_t + 1], None, ALU.is_le)
                edge(ctx, v, ins, "ec")
                ins = v.tensor_reduce(sm["tok1"][:, :], sm["msk"][:, :], AX.X,
                                      ALU.add)
                edge(ctx, v, ins, "ec")
                ins = v.scalar_tensor_tensor(sm["tok1"][:, :],
                                             sm["gval"][:, s_t:s_t + 1], 128.0,
                                             sm["tok1"][:, :], ALU.mult, ALU.add)
                edge(ctx, v, ins, "ec")
                ins = v.tensor_copy(sm["tokidx"][:, s_t:s_t + 1], sm["tok1"][:, :])
                edge(ctx, v, ins, "ec")
                v.memset(sm["cg"][:, :], 0.0)

    with nc.Block() as blk:
        @blk.gpsimd
        def _(g):
            ctx.wait(g, "e6s")
            ins = g.indirect_dma_start(
                sm["lcum"][0:8, :], None, wd["c_all"][:, :],
                IndirectOffsetOnAxis(sm["lcum_idx"][0:8, 0:1], 0),
                bounds_check=ctx.breg(g, 63), oob_is_err=False)
            ctx.inc(ins, "e9g", 16)
            ctx.wait(g, "e9g")
    with (
        nc.psum_tensor("e9t0", [128, 8], F32) as pq0,
        nc.psum_tensor("e9t1", [128, 8], F32) as pq1,
        nc.Block() as blk,
    ):
        pqs = [pq0, pq1]
        @blk.tensor
        def _(t):
            for tt in range(TCH):
                if tt >= 2:
                    ctx.wait(t, "e9e", tt - 1)
                ins = t.transpose(pqs[tt % 2][:, :],
                                  sm["lcum"][0:8, tt * 128:(tt + 1) * 128],
                                  cs["c_identf"][0:8, 0:8])
                ctx.inc(ins, "e9p")
        @blk.vector
        def _(v):
            for tt in range(TCH):
                ctx.wait(v, "e9p", tt + 1)
                ins = v.tensor_copy(sm["seqt"][:, tt * 8:(tt + 1) * 8],
                                    pqs[tt % 2][:, :])
                ctx.inc(ins, "e9e")
    with nc.Block() as blk:
        @blk.vector
        def _(v):
            ins = v.tensor_scalar(sm["vmat"][:, :], sm["seqt"][:, :], float(8 * 320),
                                  None, ALU.is_le)
            edge(ctx, v, ins, "ec")
            ins = v.tensor_tensor(sm["vmat"][:, :], sm["vmat"][:, :], gem[:, :],
                                  ALU.mult)
            edge(ctx, v, ins, "ec")
            ins = v.tensor_scalar(sm["fmat"][:, :], sm["seqt"][:, :], -1.0, None,
                                  ALU.add)
            edge(ctx, v, ins, "ec")
            ins = v.tensor_scalar(sm["cg"][:, :], sm["fmat"][:, :], 512.0, None,
                                  ALU.is_ge)
            edge(ctx, v, ins, "ec")
            for thr in (1024.0, 1536.0, 2048.0):
                ins = v.tensor_scalar(sm["msk"][:, :], sm["fmat"][:, :], thr,
                                      None, ALU.is_ge)
                edge(ctx, v, ins, "ec")
                ins = v.tensor_tensor(sm["cg"][:, :], sm["cg"][:, :],
                                      sm["msk"][:, :], ALU.add)
                edge(ctx, v, ins, "ec")
            ins = v.scalar_tensor_tensor(sm["fmat"][:, :], sm["cg"][:, :],
                                         3584.0, sm["fmat"][:, :], ALU.mult,
                                         ALU.add)
            edge(ctx, v, ins, "ec")
            for tt in range(TCH):
                ins = v.scalar_tensor_tensor(
                    sm["fmat"][:, tt * 8:(tt + 1) * 8], cs["c_iota_e"][:, :],
                    512.0, sm["fmat"][:, tt * 8:(tt + 1) * 8], ALU.mult, ALU.add)
            edge(ctx, v, ins, "ec")
            ins = v.tensor_tensor(sm["fmat"][:, :], sm["fmat"][:, :], sm["vmat"][:, :],
                                  ALU.mult)
            edge(ctx, v, ins, "ec")
            ins = v.tensor_scalar(sm["vmat"][:, :], sm["vmat"][:, :], -20480.0, 20480.0,
                                  ALU.mult, ALU.add)
            edge(ctx, v, ins, "ec")
            ins = v.tensor_tensor(sm["fmat"][:, :], sm["fmat"][:, :], sm["vmat"][:, :],
                                  ALU.add)
            edge(ctx, v, ins, "ec")
            ins = v.tensor_tensor(sm["msk"][:, :], eqm1[:, :], sm["fmat"][:, :],
                                  ALU.mult)
            edge(ctx, v, ins, "ec")
            ins = v.tensor_reduce(sm["fl1"][:, :], sm["msk"][:, :].rearrange(
                "p (t e) -> p t e", e=8), AX.X, ALU.add)
            edge(ctx, v, ins, "ec")
            ins = v.tensor_tensor(sm["msk"][:, :], eqm2[:, :], sm["fmat"][:, :],
                                  ALU.mult)
            edge(ctx, v, ins, "ec")
            ins = v.tensor_reduce(sm["fl2"][:, :], sm["msk"][:, :].rearrange(
                "p (t e) -> p t e", e=8), AX.X, ALU.add)
            edge(ctx, v, ins, "ec")
            ins = v.tensor_copy(sm["fl1i"][:, :], sm["fl1"][:, :])
            ins = v.tensor_copy(sm["fl2i"][:, :], sm["fl2"][:, :])

    bt = [rLT[:, 0:4096], r0[:, 0:4096]]
    bf = [rLT[:, 4096:8192], rLT[:, 8192:12288]]
    y_fm = rLT[:, 12288:16384]
    y_tok = r0[:, 4096:8192]
    w3sl = [r0[:, 8192:12288], r0[:, 12288:16384]]
    w1s = ws[:, 0:2048]
    w2s = ws[:, 2048:4096]
    stmp = sm["bc0"].bitcast(BF16)[:, 0:1024]

    tp_done = {}
    ytp_done = {}
    bS = {n: ctx.val(n) for n in
          ("etpp", "etpe", "efg0", "efg1", "efw10", "efw11", "efw20", "efw21",
           "efw30", "efw31", "efs")}
    epc_t = ctx.val("epc")

    with (
        nc.psum_tensor("effh0", [128, 512], F32) as hp0,
        nc.psum_tensor("effh1", [128, 512], F32) as hp1,
        nc.psum_tensor("effh2", [128, 512], F32) as hp2,
        nc.psum_tensor("effh3", [128, 512], F32) as hp3,
        nc.psum_tensor("effy0", [128, 512], F32) as yq0,
        nc.psum_tensor("effy1", [128, 512], F32) as yq1,
        nc.psum_tensor("efftp0", [128, 128], BF16) as tpq0,
        nc.psum_tensor("efftp1", [128, 128], BF16) as tpq1,
        nc.Block() as blk,
    ):
        hpa = [hp0, hp2]
        hpb = [hp1, hp3]
        yqs = [yq0, yq1]
        tpq = [tpq0, tpq1]

        def t_pairs(ft):
            return [(bt[ft % 2][:, j * 1024 + dc * 128:j * 1024 + (dc + 1) * 128],
                     bf[ft % 2][:, dc * 512 + j * 128:dc * 512 + (j + 1) * 128])
                    for j in range(4) for dc in range(DCH)]

        def yt_pairs(ft):
            return [(y_fm[:, dc * 512 + j * 128:dc * 512 + (j + 1) * 128],
                     y_tok[:, j * 1024 + dc * 128:j * 1024 + (dc + 1) * 128])
                    for dc in range(DCH) for j in range(4)]

        @blk.tensor
        def _(t):
            tpn = [0]

            def run_tp(pairs, pre=None):
                for i, (src, dst) in enumerate(pairs):
                    n = tpn[0]
                    if i == 0 and pre:
                        for semname, val in pre:
                            ctx.wait(t, semname, bS.get(semname, 0) + val)
                    if n >= 2:
                        ctx.wait(t, "etpe", bS["etpe"] + n - 1)
                    ins = t.transpose(tpq[n % 2][:, :],
                                      src, cs["c_ident"][:, :])
                    ctx.inc(ins, "etpp")
                    tpn[0] += 1

            jh = 0
            jy = 0
            for ft in range(5):
                if ft == 0:
                    run_tp(t_pairs(0), pre=[("efg0", 64)])
                    tp_done[0] = ctx.val("etpp")
                for hc in range(32):
                    if hc == 0:
                        ctx.wait(t, "etpe", tp_done[ft] - bS["etpp"] + bS["etpe"])
                    ctx.wait(t, "efw1" + str(jh % 2),
                             bS["efw1" + str(jh % 2)] + 16 * (jh // 2 + 1))
                    ctx.wait(t, "efw2" + str(jh % 2),
                             bS["efw2" + str(jh % 2)] + 16 * (jh // 2 + 1))
                    if jh >= 2:
                        ctx.wait(t, "efhe", jh - 1)
                        ctx.wait(t, "efha", jh - 1)
                    pa, pb = hpa[jh % 2], hpb[jh % 2]
                    for dc in range(DCH):
                        t.matmul(pa[:, :],
                                 w1s[:, (jh % 2) * 1024 + dc * 128:
                                     (jh % 2) * 1024 + (dc + 1) * 128],
                                 bf[ft % 2][:, dc * 512:(dc + 1) * 512],
                                 start=(dc == 0), stop=(dc == DCH - 1))
                    for dc in range(DCH):
                        ins = t.matmul(pb[:, :],
                                       w2s[:, (jh % 2) * 1024 + dc * 128:
                                           (jh % 2) * 1024 + (dc + 1) * 128],
                                       bf[ft % 2][:, dc * 512:(dc + 1) * 512],
                                       start=(dc == 0), stop=(dc == DCH - 1))
                    ctx.inc(ins, "efhp")
                    jh += 1
                if ft < 4:
                    run_tp(t_pairs(ft + 1),
                           pre=[("efg" + str((ft + 1) % 2),
                                 64 * ((ft + 1) // 2 + 1))])
                    tp_done[ft + 1] = ctx.val("etpp")
                for dc in range(8):
                    ctx.wait(t, "efw3" + str(jy % 2),
                             bS["efw3" + str(jy % 2)] + 16 * (jy // 2 + 1))
                    if jy >= 2:
                        ctx.wait(t, "efye", jy - 1)
                    if dc == 0:
                        ctx.wait(t, "efhe", 32 * (ft + 1))
                    for hc in range(32):
                        ins = t.matmul(yqs[jy % 2][:, :],
                                       w3sl[jy % 2][:, hc * 128:(hc + 1) * 128],
                                       h_sb[:, hc * 512:(hc + 1) * 512],
                                       start=(hc == 0), stop=(hc == 31))
                    ctx.inc(ins, "efyp")
                    jy += 1
                run_tp(yt_pairs(ft), pre=[("efye", 8 * (ft + 1)),
                                          ("efs", 16 * ft)])

        @blk.vector
        def _(v):
            tpe = [0]

            def evac_tp(pairs):
                for src, dst in pairs:
                    n = tpe[0]
                    ctx.wait(v, "etpp", bS["etpp"] + n + 1)
                    ins = v.tensor_copy(dst, tpq[n % 2][:, :])
                    ctx.inc(ins, "etpe")
                    tpe[0] += 1

            jh = 0
            jy = 0
            for ft in range(5):
                if ft == 0:
                    evac_tp(t_pairs(0))
                for hc in range(32):
                    ctx.wait(v, "efha", jh + 1)
                    if hc == 0 and ft > 0:
                        ctx.wait(v, "efyp", 8 * ft)
                    ins = v.tensor_tensor(
                        h_sb[:, hc * 512:(hc + 1) * 512],
                        stmp[:, (jh % 2) * 512:(jh % 2 + 1) * 512],
                        hpb[jh % 2][:, :], ALU.mult)
                    ctx.inc(ins, "efhe")
                    jh += 1
                if ft < 4:
                    evac_tp(t_pairs(ft + 1))
                for dc in range(8):
                    ctx.wait(v, "efyp", jy + 1)
                    ins = v.tensor_copy(y_fm[:, dc * 512:(dc + 1) * 512],
                                        yqs[jy % 2][:, :])
                    ctx.inc(ins, "efye")
                    jy += 1
                evac_tp(yt_pairs(ft))
                ytp_done[ft] = ctx.val("etpe")

        @blk.scalar
        def _(a):
            ctx.wait(a, "epc", epc_t)
            for hw in range(2):
                ins = a.dma_start(w1s[:, hw * 1024:(hw + 1) * 1024],
                                  wd["W1b"][:, hw * 1024:(hw + 1) * 1024])
                ctx.inc(ins, "efw1" + str(hw % 2), 16)
            jh = 0
            for ft in range(5):
                for hc in range(32):
                    hw = jh + 2
                    if hw < 160:
                        ctx.wait(a, "efhp", hw - 1)
                        hc2 = hw % 32
                        ins = a.dma_start(
                            w1s[:, (hw % 2) * 1024:(hw % 2 + 1) * 1024],
                            wd["W1b"][:, hc2 * 1024:(hc2 + 1) * 1024])
                        ctx.inc(ins, "efw1" + str(hw % 2), 16)
                    ctx.wait(a, "efhp", jh + 1)
                    ins = a.activation(stmp[:, (jh % 2) * 512:(jh % 2 + 1) * 512],
                                       hpa[jh % 2][:, :], AF.Silu)
                    ctx.inc(ins, "efha")
                    jh += 1

        @blk.sync
        def _(s):
            ctx.wait(s, "epc", epc_t)
            for hw in range(2):
                ins = s.dma_start(w2s[:, hw * 1024:(hw + 1) * 1024],
                                  wd["W2b"][:, hw * 1024:(hw + 1) * 1024])
                ctx.inc(ins, "efw2" + str(hw % 2), 16)
            jh = 0
            for ft in range(5):
                if ft > 0:
                    ctx.wait(s, "etpe", ytp_done[ft - 1])
                    ins = s.dma_start(
                        wd["ye_loc"][(ft - 1) * 512:ft * 512, :]
                        .rearrange("(j p) d -> p j d", p=128),
                        y_tok[:, :].rearrange("p (j d) -> p j d", d=D))
                    ctx.inc(ins, "efs", 16)
                for hc in range(32):
                    hw = jh + 2
                    if hw < 160:
                        ctx.wait(s, "efhp", hw - 1)
                        hc2 = hw % 32
                        ins = s.dma_start(
                            w2s[:, (hw % 2) * 1024:(hw % 2 + 1) * 1024],
                            wd["W2b"][:, hc2 * 1024:(hc2 + 1) * 1024])
                        ctx.inc(ins, "efw2" + str(hw % 2), 16)
                    jh += 1
            ctx.wait(s, "etpe", ytp_done[4])
            ins = s.dma_start(
                wd["ye_loc"][4 * 512:5 * 512, :]
                .rearrange("(j p) d -> p j d", p=128),
                y_tok[:, :].rearrange("p (j d) -> p j d", d=D))
            ctx.inc(ins, "efs", 16)

        @blk.gpsimd
        def _(g):
            ctx.wait(g, "epc", epc_t)
            ctx.wait(g, "ccx")

            def gather(ft):
                if ft >= 2:
                    ctx.wait(g, "etpp", tp_done[ft - 2])
                for j in range(4):
                    ins = g.indirect_dma_start(
                        bt[ft % 2][:, j * 1024:(j + 1) * 1024], None,
                        wd["xg_all"][:, :],
                        IndirectOffsetOnAxis(
                            sm["tokidx"][:, ft * 4 + j:ft * 4 + j + 1], 0),
                        bounds_check=ctx.breg(g, N_CORES * T - 1),
                        oob_is_err=False)
                    ctx.inc(ins, "efg" + str(ft % 2), 16)

            def w3_loads(ft):
                for dc in range(8):
                    jy = ft * 8 + dc
                    if jy >= 2:
                        ctx.wait(g, "efyp", jy - 1)
                    ins = g.dma_start(w3sl[jy % 2][:, :],
                                      wd["W3b"][:, dc * 4096:(dc + 1) * 4096])
                    ctx.inc(ins, "efw3" + str(jy % 2), 16)

            base_efs = bS["efs"]

            def ye_ag(ft):
                ctx.wait(g, "efs", base_efs + 16 * (ft + 1))
                src = bass.AP(wd["ye_loc"], ft * 512 * D, [[1, 512 * D]])
                dst = bass.AP(wd["ye_all"], ft * N_CORES * 512 * D,
                              [[1, N_CORES * 512 * D]])
                ins = g.collective_compute(
                    "AllGather", ALU.bypass,
                    replica_groups=[list(range(N_CORES))],
                    ins=[src.opt()], outs=[dst.opt()])
                ctx.inc(ins, "ccy")

            gather(0)
            gather(1)
            gather(2)
            w3_loads(0)
            gather(3)
            w3_loads(1)
            gather(4)
            w3_loads(2)
            ye_ag(0)
            w3_loads(3)
            ye_ag(1)
            w3_loads(4)
            ye_ag(2)
            ye_ag(3)
            ye_ag(4)

    rows = [(rvp[:, 0:1024], rvp[:, 1024:2048]),
            (rvp[:, 2048:3072], rvp[:, 3072:4096])]
    ocmb3 = [rvp[:, 4096 + 2048 * i:4096 + 2048 * (i + 1)].bitcast(F32)
             for i in range(3)]
    sqs = rvp[:, 10240:12288]
    outs = [sm["bc1"], sm["bc0"]]
    with nc.Block() as blk:
        @blk.vector
        def _(v):
            ins = v.memset(st["eps"][:, :], EPS_LN)
            edge(ctx, v, ins, "ec")
            v.memset(rvp[:, 0:4096].bitcast(F32), 0.0)

    bC = {n: ctx.val(n) for n in
          ("egc", "egv", "ega", "egs", "egr", "ego", "egg0", "egg1",
           "eout0", "eout1")}
    ccy_t = ctx.val("ccy")
    with nc.Block() as blk:
        @blk.gpsimd
        def _(g):
            ctx.wait(g, "ccy", ccy_t)
            for tt in range(TCH):
                if tt >= 2:
                    ctx.wait(g, "egc", bC["egc"] + tt - 1)
                r1, r2 = rows[tt % 2]
                ins = g.indirect_dma_start(
                    r1, None, wd["ye_all"][:, :],
                    IndirectOffsetOnAxis(sm["fl1i"][:, tt:tt + 1], 0),
                    bounds_check=ctx.breg(g, N_CORES * CAP - 1), oob_is_err=False)
                ctx.inc(ins, "egg" + str(tt % 2), 16)
                ins = g.indirect_dma_start(
                    r2, None, wd["ye_all"][:, :],
                    IndirectOffsetOnAxis(sm["fl2i"][:, tt:tt + 1], 0),
                    bounds_check=ctx.breg(g, N_CORES * CAP - 1), oob_is_err=False)
                ctx.inc(ins, "egg" + str(tt % 2), 16)

        @blk.vector
        def _(v):
            for r in range(TCH + 2):
                if r < TCH:
                    tt = r
                    p = tt % 2
                    oc = ocmb3[tt % 3]
                    r1, r2 = rows[p]
                    ctx.wait(v, "egg" + str(p),
                             bC["egg" + str(p)] + 32 * (tt // 2 + 1))
                    if tt >= 3:
                        ctx.wait(v, "egs", bC["egs"] + tt - 2)
                        ctx.wait(v, "ego", bC["ego"] + tt - 2)
                    ins = v.tensor_scalar(oc[:, :], r1, w1t[:, tt:tt + 1],
                                          None, ALU.mult)
                    edge(ctx, v, ins, "ec")
                    ins = v.scalar_tensor_tensor(oc[:, :], r2, w2t[:, tt:tt + 1],
                                                 oc[:, :], ALU.mult, ALU.add)
                    edge(ctx, v, ins, "ec")
                    ins = v.tensor_tensor(oc[:, :], oc[:, :],
                                          rX[:, tt * D:(tt + 1) * D], ALU.add)
                    ctx.inc(ins, "egv")
                    ins = v.memset(r1, 0.0)
                    edge(ctx, v, ins, "ec")
                    ins = v.memset(r2, 0.0)
                    ctx.inc(ins, "egc")
                if 1 <= r <= TCH:
                    tt = r - 1
                    oc = ocmb3[tt % 3]
                    ctx.wait(v, "ega", bC["ega"] + tt + 1)
                    ins = v.tensor_reduce(st["ss"][:, tt:tt + 1],
                                          sqs[:, (tt % 2) * 1024:(tt % 2 + 1) * 1024],
                                          AX.X, ALU.add)
                    edge(ctx, v, ins, "ec")
                    ins = v.tensor_reduce(st["mean"][:, tt:tt + 1], oc[:, :],
                                          AX.X, ALU.add)
                    edge(ctx, v, ins, "ec")
                    ins = v.tensor_scalar(st["var"][:, tt:tt + 1],
                                          st["ss"][:, tt:tt + 1], 1.0 / D, None,
                                          ALU.mult)
                    edge(ctx, v, ins, "ec")
                    ins = v.tensor_scalar(st["mean"][:, tt:tt + 1],
                                          st["mean"][:, tt:tt + 1], 1.0 / D, None,
                                          ALU.mult)
                    edge(ctx, v, ins, "ec")
                    ins = v.tensor_tensor(st["msq"][:, tt:tt + 1],
                                          st["mean"][:, tt:tt + 1],
                                          st["mean"][:, tt:tt + 1], ALU.mult)
                    edge(ctx, v, ins, "ec")
                    ins = v.tensor_tensor(st["var"][:, tt:tt + 1],
                                          st["var"][:, tt:tt + 1],
                                          st["msq"][:, tt:tt + 1], ALU.subtract)
                    ctx.inc(ins, "egs")
                if r >= 2:
                    tt = r - 2
                    oc = ocmb3[tt % 3]
                    ctx.wait(v, "egr", bC["egr"] + tt + 1)
                    if tt >= 2:
                        ctx.wait(v, "eout" + str(tt % 2),
                                 bC["eout" + str(tt % 2)] + 16 * (tt // 2))
                    ins = v.tensor_scalar(outs[tt % 2][:, :], oc[:, :],
                                          st["mean"][:, tt:tt + 1],
                                          st["rstd"][:, tt:tt + 1],
                                          ALU.subtract, ALU.mult)
                    ctx.inc(ins, "ego")

        @blk.scalar
        def _(a):
            for r in range(TCH + 1):
                if r < TCH:
                    tt = r
                    ctx.wait(a, "egv", bC["egv"] + tt + 1)
                    if tt >= 2:
                        ctx.wait(a, "egs", bC["egs"] + tt - 1)
                    ins = a.activation(sqs[:, (tt % 2) * 1024:(tt % 2 + 1) * 1024],
                                       ocmb3[tt % 3][:, :], AF.Square)
                    ctx.inc(ins, "ega")
                if r >= 1:
                    tt = r - 1
                    ctx.wait(a, "egs", bC["egs"] + tt + 1)
                    ins = a.activation(st["rstd"][:, tt:tt + 1],
                                       st["var"][:, tt:tt + 1], AF.Ln,
                                       bias=st["eps"][:, :])
                    c2 = ctx.inc(ins, "eg2")
                    ctx.wait(a, "eg2", c2)
                    ins = a.activation(st["rstd"][:, tt:tt + 1],
                                       st["rstd"][:, tt:tt + 1], AF.Exp,
                                       scale=-0.5)
                    ctx.inc(ins, "egr")

        @blk.sync
        def _(s):
            for tt in range(TCH):
                ctx.wait(s, "ego", bC["ego"] + tt + 1)
                ins = s.dma_start(out_d[tt * 128:(tt + 1) * 128, :],
                                  outs[tt % 2][:, :])
                ctx.inc(ins, "eout" + str(tt % 2), 16)
            ctx.wait(s, "eout0")
            ctx.wait(s, "eout1")


def declare_l1_ffn_weights(nc):
    wd = {}
    for key, n in (("W1e", 32), ("W2e", 32)):
        w = nc.dram_tensor(key, [n, D, 128], F32, kind="ExternalInput")
        wd[key] = [w[i] for i in range(n)]
        wd[key + "_t"] = w
    w = nc.dram_tensor("W3e", [8, 32 * 128, 128], F32, kind="ExternalInput")
    wd["W3e"] = [w[i] for i in range(8)]
    wd["W3e_t"] = w
    wd["W1b"] = nc.dram_tensor("W1b", [128, 32768], BF16)
    wd["W2b"] = nc.dram_tensor("W2b", [128, 32768], BF16)
    wd["W3b"] = nc.dram_tensor("W3b", [128, 32768], BF16)
    wd["Wg"] = nc.dram_tensor("Wg", [D, 8], F32, kind="ExternalInput")
    wd["bg_row"] = nc.dram_tensor("bg_row", [1, 8], F32, kind="ExternalInput")
    wd["myhot"] = nc.dram_tensor("myhot", [8, 1], F32, kind="ExternalInput")
    wd["lcum_idx"] = nc.dram_tensor("lcum_idx", [8, 1], I32, kind="ExternalInput")
    wd["xg_loc"] = nc.dram_tensor("xg_loc", [T, D], BF16)
    wd["xg_all"] = nc.dram_tensor("xg_all", [N_CORES * T, D], BF16)
    wd["lab_loc"] = nc.dram_tensor("lab_loc", [8, T], BF16)
    wd["lab_all"] = nc.dram_tensor("lab_all", [N_CORES * 8, T], BF16)
    wd["c_all"] = nc.dram_tensor("c_all", [64, T], F32)
    wd["cseg_d"] = nc.dram_tensor("cseg_d", [128, 128], F32)
    wd["ye_loc"] = nc.dram_tensor("ye_loc", [CAP, D], BF16)
    wd["ye_all"] = nc.dram_tensor("ye_all", [N_CORES * CAP, D], BF16)
    return wd


def host_l1_ffn_weight_maps(ins, core):
    m = {}
    W1 = np.asarray(ins["L1_ffn_W1"])[core]
    m["W1e"] = np.ascontiguousarray(W1.reshape(D, 32, 128).transpose(1, 0, 2))
    W2 = np.asarray(ins["L1_ffn_W2"])[core]
    m["W2e"] = np.ascontiguousarray(W2.reshape(D, 32, 128).transpose(1, 0, 2))
    W3 = np.asarray(ins["L1_ffn_W3"])[core]
    m["W3e"] = np.ascontiguousarray(
        W3.reshape(32, 128, 8, 128).transpose(2, 0, 1, 3).reshape(8, 32 * 128, 128))
    m["Wg"] = np.asarray(ins["L1_ffn_Wg"])
    m["bg_row"] = np.asarray(ins["L1_ffn_bg"]).reshape(1, 8)
    hot = np.zeros((8, 1), np.float32)
    hot[core, 0] = 1.0
    m["myhot"] = hot
    m["lcum_idx"] = (np.arange(8, dtype=np.int32) * 8 + core).reshape(8, 1)
    return m


def emit_l1_precast(ctx, wd, g):
    for key, bkey in (("W1e_t", "W1b"), ("W2e_t", "W2b")):
        w = wd[key]
        tw = wd[bkey]
        for h0 in range(0, 32, 8):
            src = w[h0:h0 + 8].rearrange("hc (dc p) c -> (hc dc) p c", p=128)
            dst = tw[:, h0 * 1024:(h0 + 8) * 1024].rearrange(
                "p (hc dc c) -> (hc dc) p c", hc=8, c=128)
            ins = g.dma_start(dst, src)
            ctx.inc(ins, "epc", 16)
    w = wd["W3e_t"]
    tw = wd["W3b"]
    for d0 in range(0, 8, 2):
        src = w[d0:d0 + 2].rearrange("dc (hc p) c -> (dc hc) p c", p=128)
        dst = tw[:, d0 * 4096:(d0 + 2) * 4096].rearrange(
            "p (dc hc c) -> (dc hc) p c", dc=2, c=128)
        ins = g.dma_start(dst, src)
        ctx.inc(ins, "epc", 16)
    wd["_epc"] = ctx.val("epc")


def alloc_l1_smalls(nc, sm, ar=None):
    uid = [0]
    def _mk(sh, dt):
        uid[0] += 1
        return nc.alloc_sbuf_tensor("l1_%d" % uid[0], sh, dt)
    A = ar.take if ar else _mk
    sm["gw"] = A([128, 64], BF16)
    sm["bg_row"] = A([1, 8], F32)
    sm["bg_bc"] = A([128, 8], F32)
    sm["lg_tok"] = A([128, 128], F32)
    sm["e_m1"] = A([128, 16], F32)
    sm["e_m2"] = A([128, 16], F32)
    sm["e_eq1"] = A([128, 128], F32)
    sm["e_eq2"] = A([128, 128], F32)
    sm["e_ge"] = A([128, 128], F32)
    sm["e_w1"] = A([128, 16], F32)
    sm["e_w2"] = A([128, 16], F32)
    sm["labl"] = sm["ws"][0:8, 0:T]
    sm["myhot"] = A([8, 1], F32)
    sm["crow2"] = A([1, 1024], F32)
    sm["crow"] = sm["crow2"][0:1, 0:128]
    sm["prow"] = sm["crow2"][0:1, 128:256]
    sm["cseg_s"] = A([128, 128], F32)
    sm["svals"] = A([128, 1], F32)
    sm["sv2"] = A([128, CAPSL], F32)
    sm["pbc"] = A([128, 128], F32)
    sm["cbc"] = A([128, 128], F32)
    sm["msk"] = A([128, 128], F32)
    sm["gval"] = A([128, CAPSL], F32)
    sm["gidx"] = A([128, CAPSL], I32)
    sm["cg"] = A([128, 128], F32)
    sm["tok1"] = A([128, 1], F32)
    sm["tokidx"] = A([128, CAPSL], I32)
    sm["lcum_idx"] = A([8, 1], I32)
    sm["seqt"] = A([128, 128], F32)
    sm["vmat"] = A([128, 128], F32)
    sm["fmat"] = A([128, 128], F32)
    sm["fl1"] = A([128, 16], F32)
    sm["fl2"] = A([128, 16], F32)
    sm["fl1i"] = A([128, 16], I32)
    sm["fl2i"] = A([128, 16], I32)
    return sm


def emit_load_l1_smalls(ctx, sm, wd):
    emit_load_small(ctx, [
        (sm["bg_row"][:, :], wd["bg_row"][:, :]),
        (sm["myhot"][:, :], wd["myhot"][:, :]),
        (sm["lcum_idx"][:, :], wd["lcum_idx"][:, :]),
    ])


def build_stage_e_program():
    nc = bass.Bass("TRN2", target_bir_lowering=False,
                   dynamic_dma_scratch_size=26624)
    ctx = Ctx(nc)
    x_d = nc.dram_tensor("x_in", [T, D], F32, kind="ExternalInput")
    out_d = nc.dram_tensor("x_out", [T, D], F32, kind="ExternalOutput")
    wd = declare_l1_ffn_weights(nc)
    cs = setup_constants(ctx, resident_only=True)
    rg = alloc_regions(nc, names=("r0", "r1", "r2", "r3", "rvp"))
    sm = alloc_smalls(nc, skip_rows=True)
    ar = Arena(nc)
    alloc_l1_smalls(nc, sm, ar)
    emit_arena_consts(ctx, cs, ar, ("c_iota_e",))
    st = alloc_ln_stats(nc, sq_backing=sm["ws"][:, :2 * D])
    with nc.Block() as blk:
        @blk.gpsimd
        def _(g):
            emit_l1_precast(ctx, wd, g)
    emit_load_l1_smalls(ctx, sm, wd)
    emit_load_x(ctx, x_d, rg["r1"])
    stage_l1_ffn(ctx, cs, st, sm, wd, rX=rg["r1"], rLT=rg["r2"],
                 cta=rg["r3"].bitcast(F32), ctb=None,
                 h_sb=rg["rvp"][:, 0:16 * 1024], rvp=rg["rvp"],
                 out_d=out_d, r0=rg["r0"])
    nc.finalize()
    return nc



def alloc_regions(nc, names=("r0", "r1", "r2", "r3", "rvp")):
    b = {}
    for nm in names:
        if nm == "rvp":
            b[nm] = nc.alloc_sbuf_tensor(nm, [128, TCH * NH * VPW], BF16)
        else:
            b[nm] = nc.alloc_sbuf_tensor(nm, [128, TCH * D], BF16)
    return b


def alloc_smalls(nc, skip_rows=False):
    sm = {}
    sm["ws"] = nc.alloc_sbuf_tensor("ws", [128, 8192], BF16)
    sm["bc0"] = nc.alloc_sbuf_tensor("bc0", [128, D], F32)
    sm["bc1"] = nc.alloc_sbuf_tensor("bc1", [128, D], F32)
    if not skip_rows:
        sm["b3_pp"] = nc.alloc_sbuf_tensor("b3_pp_s", [128, 24], F32)
        sm["bv_row"] = nc.alloc_sbuf_tensor("bv_row_s", [1, D], BF16)
        sm["bp_row"] = nc.alloc_sbuf_tensor("bp_row_s", [1, D], BF16)
    return sm


def emit_load_x(ctx, x_d, xa):
    nc = ctx.nc
    with nc.Block() as blk:
        @blk.gpsimd
        def _(g):
            for tc in range(TCH):
                ins = g.dma_start(xa[:, tc * D:(tc + 1) * D],
                                  x_d[tc * 128:(tc + 1) * 128, :])
                ctx.inc(ins, "xld", 16)
            ctx.wait(g, "xld")


def emit_store_raw(ctx, sb, out_d, tag):
    nc = ctx.nc
    with nc.Block() as blk:
        @blk.gpsimd
        def _(g):
            ins = g.dma_start(out_d[:, :], sb[:, :])
            ctx.inc(ins, "dmp" + tag, 16)
            ctx.wait(g, "dmp" + tag)


def emit_store_x(ctx, x_tok, out_d):
    nc = ctx.nc
    with nc.Block() as blk:
        @blk.gpsimd
        def _(g):
            for tc in range(TCH):
                ins = g.dma_start(out_d[tc * 128:(tc + 1) * 128, :],
                                  x_tok[:, tc * D:(tc + 1) * D])
                ctx.inc(ins, "xst", 16)
            ctx.wait(g, "xst")


def emit_load_small(ctx, items):
    nc = ctx.nc
    with nc.Block() as blk:
        @blk.gpsimd
        def _(s):
            for dst, src in items:
                ins = s.dma_start(dst, src)
                ctx.inc(ins, "sld", 16)
            ctx.wait(s, "sld")


def declare_l0a_weights(nc):
    wd = {}
    wa = nc.dram_tensor("W_attn_ocm", [24, D, 128], F32, kind="ExternalInput")
    wd["W_attn_ocm"] = [wa[i] for i in range(24)]
    wp = nc.dram_tensor("W_proj_hm", [2, D, 512], F32, kind="ExternalInput")
    wd["W_proj_hm"] = [wp[i] for i in range(2)]
    wd["b3_pp"] = nc.dram_tensor("b3_pp", [128, 24], F32, kind="ExternalInput")
    wd["bv_row"] = nc.dram_tensor("bv_row", [1, D], F32, kind="ExternalInput")
    wd["bp_row"] = nc.dram_tensor("bp_row", [1, D], F32, kind="ExternalInput")
    return wd


def host_l0a_weight_maps(ins):
    m = {}
    W = np.asarray(ins["L0_attn_W_attn"])
    m["W_attn_ocm"] = np.ascontiguousarray(
        W.reshape(D, 24, 128).transpose(1, 0, 2))
    Wp = np.asarray(ins["L0_attn_W_proj"])
    m["W_proj_hm"] = np.ascontiguousarray(
        Wp.reshape(D, 2, 512).transpose(1, 0, 2))
    m["b3_pp"] = np.ascontiguousarray(ins["L0_attn_b_attn"].reshape(24, 128).T)
    m["bv_row"] = np.ascontiguousarray(ins["L0_attn_b_attn"][2048:].reshape(1, D))
    m["bp_row"] = np.ascontiguousarray(ins["L0_attn_b_proj"].reshape(1, D))
    return m


def build_stage_a_program(dumps=()):
    nc = bass.Bass("TRN2", target_bir_lowering=False)
    ctx = Ctx(nc)
    x_d = nc.dram_tensor("x_in", [T, D], F32, kind="ExternalInput")
    out_d = nc.dram_tensor("x_out", [T, D], F32, kind="ExternalOutput")
    wd = declare_l0a_weights(nc)
    cs = setup_constants(ctx)
    rg = alloc_regions(nc)
    sm = alloc_smalls(nc)
    st = alloc_ln_stats(nc, sq_backing=sm['ws'][:, :2 * D])
    emit_load_small(ctx, [(sm["b3_pp"][:, :], wd["b3_pp"][:, :]),
                          (sm["bv_row"][:, :], wd["bv_row"][:, :]),
                          (sm["bp_row"][:, :], wd["bp_row"][:, :])])
    emit_load_x(ctx, x_d, rg["r0"])
    dd = {}
    for name in dumps:
        w = TCH * NH * VPW if name == "vp" else (4 * TCH if name == "lnst" else TCH * D)
        dd[name] = nc.dram_tensor("dump_" + name, [128, w], F32,
                                  kind="ExternalOutput")
    stage_l0_attn(ctx, cs, st, sm, wd, x_in=rg["r0"], x_out=rg["r2"],
                  rfm=rg["r1"], rq=rg["r2"], rk=rg["r3"], rvp=rg["rvp"],
                  dd=dd, rg=rg)
    emit_store_x(ctx, rg["r2"], out_d)
    nc.finalize()
    return nc


_PROGS = {}


def _prog(name):
    if name not in _PROGS:
        _PROGS[name] = {
            "a": build_stage_a_program,
            "b": build_stage_b_program,
            "c": build_stage_c_program,
            "d": build_stage_d_program,
            "e": build_stage_e_program,
            "fused": build_fused_program,
        }[name]()
    return _PROGS[name]


def kernel(**inputs):
    return kernel_fused(**inputs)


def kernel_multi(**inputs):
    ins = inputs
    cs = host_constants()
    x = np.ascontiguousarray(np.asarray(ins["L0_attn_x"], dtype=np.float32))

    def run(name, per_core_maps):
        nc = _prog(name)
        res = run_bass_kernel_spmd(nc, per_core_maps, core_ids=list(range(N_CORES)))
        return [np.ascontiguousarray(np.asarray(r["x_out"], dtype=np.float32))
                for r in res.results]

    wa = host_l0a_weight_maps(ins)
    maps = [dict(cs, **wa, x_in=np.ascontiguousarray(x[c])) for c in range(N_CORES)]
    x1 = run("a", maps)

    wb = host_l0f_attn_weight_maps(ins)
    maps = [dict(cs, **wb, x_in=x1[c]) for c in range(N_CORES)]
    h = run("b", maps)

    wc = host_l0_moe_weight_maps(ins)
    maps = [dict(cs, **wc, h_in=h[c], x1_in=x1[c]) for c in range(N_CORES)]
    x2 = run("c", maps)

    wdt = host_ttt_weight_maps(ins)
    maps = [dict(cs, **wdt, x_in=x2[c]) for c in range(N_CORES)]
    x3 = run("d", maps)

    maps = [dict(cs, **host_l1_ffn_weight_maps(ins, c), x_in=x3[c])
            for c in range(N_CORES)]
    out = run("e", maps)
    return np.stack(out, axis=0)


def build_fused_program():
    nc = bass.Bass("TRN2", target_bir_lowering=False,
                   dynamic_dma_scratch_size=26624)
    ctx = Ctx(nc)
    x_d = nc.dram_tensor("x_in", [T, D], F32, kind="ExternalInput")
    out_d = nc.dram_tensor("x_out", [T, D], F32, kind="ExternalOutput")
    wda = declare_l0a_weights(nc)
    wdb = declare_l0f_attn_weights(nc)
    wdc = declare_l0_moe_weights(nc)
    wdd = declare_ttt_weights(nc)
    wde = declare_l1_ffn_weights(nc)
    cs = setup_constants(ctx, resident_only=True)
    rg = alloc_regions(nc)
    sm = alloc_smalls(nc, skip_rows=True)
    ar = Arena(nc)
    st = alloc_ln_stats(nc, sq_backing=sm["ws"][:, :2 * D])
    sm["b3_pp"] = ar.take([128, 24], F32)
    sm["bv_row"] = ar.take([1, D], BF16)
    sm["bp_row"] = ar.take([1, D], BF16)
    emit_arena_consts(ctx, cs, ar, ("c_mask",))
    emit_load_small(ctx, [(sm["b3_pp"][:, :], wda["b3_pp"][:, :]),
                          (sm["bv_row"][:, :], wda["bv_row"][:, :]),
                          (sm["bp_row"][:, :], wda["bp_row"][:, :])])
    emit_load_x(ctx, x_d, rg["r0"])
    stage_l0_attn(ctx, cs, st, sm, wda, x_in=rg["r0"], x_out=rg["r2"],
                  rfm=rg["r1"], rq=rg["r2"], rk=rg["r3"], rvp=rg["rvp"])

    ar.reset()
    sm["cr"] = ar.take([128, 1024], BF16)
    sm["ci"] = ar.take([128, 1024], BF16)
    sm["g1_row"] = ar.take([1, D], BF16)
    emit_arena_consts(ctx, cs, ar, ("c_mask",))
    emit_load_small(ctx, [(sm["g1_row"][:, :], wdb["g1_row"][:, :])])
    stage_l0_ffn_attn(ctx, cs, st, sm, wdb, x_in=rg["r2"], rA=rg["r0"],
                      rB=rg["r1"], rC=rg["r3"], rvp=rg["rvp"])

    ar.reset()
    alloc_moe_smalls(nc, sm, ar)
    sm["g2_row"] = ar.take([1, D], BF16)
    emit_load_small(ctx, [(sm["g2_row"][:, :], wdc["g2_row"][:, :])])
    acc32 = [rg["r0"].bitcast(F32)[:, 0:4096], rg["r0"].bitcast(F32)[:, 4096:8192]]
    stage_l0_moe(ctx, cs, st, sm, wdc, rH=rg["r0"], rX1=rg["r2"], rOut=rg["r1"],
                 rFm=rg["r3"], h_sb=rg["rvp"][:, :16 * 1024], acc32=acc32)

    ar.reset()
    alloc_ttt_smalls(nc, sm, ar)
    emit_arena_consts(ctx, cs, ar, ("c_trils", "c_trius", "c_uincl"))
    stage_ttt(ctx, cs, st, sm, wdd, rX=rg["r1"], rFm=rg["r3"], rTH=rg["r0"],
              rvp=rg["rvp"], wreg=rg["r2"].bitcast(F32),
              extra_gdma=lambda g: emit_l1_precast(ctx, wde, g))

    ar.reset()
    alloc_l1_smalls(nc, sm, ar)
    emit_arena_consts(ctx, cs, ar, ("c_iota_e",))
    emit_load_l1_smalls(ctx, sm, wde)
    stage_l1_ffn(ctx, cs, st, sm, wde, rX=rg["r1"], rLT=rg["r2"],
                 cta=rg["r3"].bitcast(F32), ctb=None,
                 h_sb=rg["rvp"][:, 0:16 * 1024], rvp=rg["rvp"], out_d=out_d,
                 r0=rg["r0"])
    nc.finalize()
    return nc


def host_full_maps(ins):
    cs = host_constants()
    base = dict(cs)
    base.update(host_l0a_weight_maps(ins))
    base.update(host_l0f_attn_weight_maps(ins))
    base.update(host_l0_moe_weight_maps(ins))
    base.update(host_ttt_weight_maps(ins))
    x = np.ascontiguousarray(np.asarray(ins["L0_attn_x"], dtype=np.float32))
    maps = []
    for c in range(N_CORES):
        m = dict(base)
        m.update(host_l1_ffn_weight_maps(ins, c))
        m["x_in"] = np.ascontiguousarray(x[c])
        maps.append(m)
    return maps


def kernel_fused(**inputs):
    nc = _prog("fused")
    maps = host_full_maps(inputs)
    res = run_bass_kernel_spmd(nc, maps, core_ids=list(range(N_CORES)))
    return np.stack([np.ascontiguousarray(np.asarray(r["x_out"], dtype=np.float32))
                     for r in res.results], axis=0)


# version 10
# speedup vs baseline: 1.0827x; 1.0827x over previous
import numpy as np
import concourse.bass as bass
import concourse.mybir as mybir
from concourse.bass_utils import run_bass_kernel_spmd
from concourse.bass_core_rust import IndirectOffsetOnAxis

F32 = mybir.dt.float32
BF16 = mybir.dt.bfloat16
I32 = mybir.dt.int32
AF = mybir.ActivationFunctionType
ALU = mybir.AluOpType
AX = mybir.AxisListType

N_CORES = 8
T = 2048
D = 1024
NH = 16
HD = 64
TCH = T // 128
DCH = D // 128
QT = 512
NQT = T // QT
EPS_LN = 1e-5
EPS_RMS = 1e-6
A_TTT = float(np.float32(0.01) * (np.float32(2.0) / np.float32(1024.0)))
EXP_SHIFT = -14.0
VPW = HD + 1
CAP = 2560


SEM_ALIAS = {
    "efw30": "ropew0", "efw31": "ropew1",
    "efg0": "e6g0", "efg1": "e6g1",
    "egg0": "mw0", "egg1": "mw1",
    "eout0": "e4s", "eout1": "e3s",
    "etpp": "trpt", "etpe": "tret",
    "egv": "lnla", "ega": "lnlv", "egs": "lnlgo", "egr": "lnlrs",
    "ego": "e7v", "egc": "e7tp", "eg2": "e7t2", "epc": "e2w",
}


class Ctx:
    def __init__(self, nc):
        self.nc = nc
        self.sems = {}
        self.cnt = {}

    def sem(self, name):
        name = SEM_ALIAS.get(name, name)
        if name not in self.sems:
            self.sems[name] = self.nc.alloc_semaphore(name)
            self.cnt[name] = 0
        return self.sems[name]

    def inc(self, ins, name, by=1):
        name = SEM_ALIAS.get(name, name)
        ins.then_inc(self.sem(name), by)
        self.cnt[name] += by
        return self.cnt[name]

    def val(self, name):
        name = SEM_ALIAS.get(name, name)
        self.sem(name)
        return self.cnt[name]

    def wait(self, eng, name, val=None):
        name = SEM_ALIAS.get(name, name)
        eng.wait_ge(self.sem(name), self.cnt[name] if val is None else val)

    def breg(self, g, val):
        if not hasattr(self, "bregs"):
            self.bregs = {}
        if val not in self.bregs:
            r = g.alloc_register("bnd%d" % val)
            g.reg_mov(r, val)
            self.bregs[val] = r
        return self.bregs[val]


def edge(ctx, eng, ins, tag=None):
    c = ctx.inc(ins, "ech")
    ctx.wait(eng, "ech", c)



class Arena:

    def __init__(self, nc, name="sma", nbytes=11712):
        self.h = nc.alloc_sbuf_tensor(name, [128, nbytes // 2], BF16)
        self.h32 = self.h.bitcast(F32)
        self.hi = self.h.bitcast(I32)
        self.off = 0

    def reset(self):
        self.off = 0

    def take(self, shape, dt):
        ib = {BF16: 2, F32: 4, I32: 4}[dt]
        self.off = (self.off + 3) & ~3
        cols = shape[1]
        start = self.off // ib
        self.off += cols * ib
        assert self.off <= self.h.shape[1] * 2, "arena overflow %d" % self.off
        src = {BF16: self.h, F32: self.h32, I32: self.hi}[dt]
        return src[0:shape[0], start:start + cols]


def host_constants():
    c = {}
    c["c_ident"] = np.eye(128, dtype=np.float32)
    c["c_identf"] = np.eye(128, dtype=np.float32)
    c["c_ones"] = np.ones((128, 128), dtype=np.float32)
    c["c_onesb"] = np.ones((128, 128), dtype=np.float32)
    m = np.zeros((128, 4, QT), dtype=np.float32)
    for j in range(4):
        for p in range(128):
            m[p, j, j * 128 + p:] = 1.0
    c["c_mask"] = m.reshape(128, 4 * QT)
    tril_s = np.tril(np.ones((128, 128), dtype=np.float32), -1) * np.float32(A_TTT)
    c["c_trils"] = tril_s
    c["c_trius"] = np.ascontiguousarray(tril_s.T)
    c["c_uincl"] = np.ascontiguousarray(np.triu(np.ones((128, 128), dtype=np.float32)))
    c["c_iota_e"] = np.broadcast_to(np.arange(8, dtype=np.float32), (128, 8)).copy()
    return c


CONST_SPECS = {
    "c_ident": ([128, 128], BF16),
    "c_identf": ([128, 128], F32),
    "c_ones": ([128, 128], F32),
    "c_onesb": ([128, 128], BF16),
    "c_mask": ([128, 4 * QT], BF16),
    "c_trils": ([128, 128], F32),
    "c_trius": ([128, 128], F32),
    "c_uincl": ([128, 128], F32),
    "c_iota_e": ([128, 8], F32),
}


RESIDENT_CONSTS = ("c_ident", "c_identf", "c_ones", "c_onesb")


def setup_constants(ctx, resident_only=False):
    nc = ctx.nc
    cs = {}
    drams = {}
    for name, (shape, dt) in CONST_SPECS.items():
        drams[name] = nc.dram_tensor(name, shape, F32, kind="ExternalInput")
        if (not resident_only) or name in RESIDENT_CONSTS:
            cs[name] = nc.alloc_sbuf_tensor(name + "_s", shape, dt)
    names = RESIDENT_CONSTS if resident_only else tuple(CONST_SPECS)
    with nc.Block() as blk:
        @blk.gpsimd
        def _(g):
            for name in names:
                ins = g.dma_start(cs[name][:, :], drams[name][:, :])
                ctx.inc(ins, "cld", 16)
            ctx.wait(g, "cld")
    cs["_drams"] = drams
    return cs


def emit_arena_consts(ctx, cs, ar, names):
    nc = ctx.nc
    for name in names:
        cs[name] = ar.take(*CONST_SPECS[name])
    with nc.Block() as blk:
        @blk.gpsimd
        def _(g):
            for name in names:
                ins = g.dma_start(cs[name][:, :], cs["_drams"][name][:, :])
                ctx.inc(ins, "cld", 16)
            ctx.wait(g, "cld")



_PSUM_UID = [0]


def emit_transposes(ctx, tag, pairs, cs, dtype=BF16):
    nc = ctx.nc
    _PSUM_UID[0] += 1
    uid = str(_PSUM_UID[0])
    with (
        nc.psum_tensor("pt0" + tag + uid, [128, 128], dtype) as p0,
        nc.psum_tensor("pt1" + tag + uid, [128, 128], dtype) as p1,
        nc.Block() as blk,
    ):
        ps = [p0, p1]

        base_p = ctx.val("trp" + tag)
        base_e = ctx.val("tre" + tag)

        @blk.tensor
        def _(t):
            for n, (src, dst) in enumerate(pairs):
                if n >= 2:
                    ctx.wait(t, "tre" + tag, base_e + n - 1)
                ins = t.transpose(ps[n % 2][:, :], src, cs["c_ident"][:, :])
                ctx.inc(ins, "trp" + tag)

        @blk.vector
        def _(v):
            for n, (src, dst) in enumerate(pairs):
                ctx.wait(v, "trp" + tag, base_p + n + 1)
                ins = v.tensor_copy(dst, ps[n % 2][:, :])
                ctx.inc(ins, "tre" + tag)


def attn_scratch(sm):
    return {"E": sm["ws"][:, 0:2048],
            "shift": sm["bc0"][:, 768:769],
            "rcp": sm["bc0"][:, 0:512],
            "bcs": sm["bc0"].bitcast(BF16)[0:64, 1024:1536]}


def emit_tok2fm(ctx, tag, x_tok, x_fm, cs):
    pairs = []
    for i in range(TCH):
        for k in range(DCH):
            pairs.append((x_tok[:, i * D + k * 128:i * D + (k + 1) * 128],
                          x_fm[:, k * T + i * 128:k * T + (i + 1) * 128]))
    emit_transposes(ctx, tag, pairs, cs)


def emit_matmul_tiles(ctx, tag, jobs, psums, evac, dmas=None, job_wreq=None,
                      dma_pereq=None):
    nc = ctx.nc
    nb = len(psums)
    base_p = ctx.val(tag + "p")
    base_e = ctx.val(tag + "e")
    with nc.Block() as blk:
        if dmas:
            @blk.gpsimd
            def _(g):
                for i, (dst, src, semname) in enumerate(dmas):
                    if dma_pereq and dma_pereq[i] > 0:
                        ctx.wait(g, tag + "p", base_p + dma_pereq[i])
                    ins = g.dma_start(dst, src)
                    ctx.inc(ins, semname, 16)

        @blk.tensor
        def _(t):
            for n, job in enumerate(jobs):
                if job_wreq and job_wreq[n]:
                    for semname, val in job_wreq[n]:
                        ctx.wait(t, semname, val)
                if n >= nb:
                    ctx.wait(t, tag + "e", base_e + n - nb + 1)
                ps = psums[n % nb]
                na = len(job)
                for a, (lh, rh) in enumerate(job):
                    ins = t.matmul(ps, lh, rh, start=(a == 0), stop=(a == na - 1))
                ctx.inc(ins, tag + "p")

        @blk.vector
        def _(v):
            for n, job in enumerate(jobs):
                ctx.wait(v, tag + "p", base_p + n + 1)
                ins = evac(n, psums[n % nb], v)
                ctx.inc(ins, tag + "e")


def emit_ln_stats_apply(ctx, tag, src_tok, dst_tok, st, eps_val, rms=False,
                        gamma_bc=None, dst_f32_tiles=None, n_t=TCH, dbg=None):
    nc = ctx.nc
    base_la = ctx.val(tag + "la")
    base_lv = ctx.val(tag + "lv")
    with nc.Block() as blk:
        @blk.scalar
        def _(a):
            for tt in range(n_t):
                if tt >= 2:
                    ctx.wait(a, tag + "lv", base_lv + tt - 1)
                ins = a.activation(st["sq"][:, (tt % 2) * D:(tt % 2 + 1) * D],
                                   src_tok[:, tt * D:(tt + 1) * D], AF.Square)
                ctx.inc(ins, tag + "la")

        @blk.vector
        def _(v):
            v.memset(st["eps"][:, :], eps_val)
            for tt in range(n_t):
                ctx.wait(v, tag + "la", base_la + tt + 1)
                ins = v.tensor_reduce(st["ss"][:, tt:tt + 1],
                                      st["sq"][:, (tt % 2) * D:(tt % 2 + 1) * D],
                                      AX.X, ALU.add)
                ctx.inc(ins, tag + "lv")
                if not rms:
                    v.tensor_reduce(st["mean"][:, tt:tt + 1],
                                    src_tok[:, tt * D:(tt + 1) * D], AX.X, ALU.add)
    with nc.Block() as blk:
        @blk.vector
        def _(v):
            v.tensor_scalar(st["var"][:, :n_t], st["ss"][:, :n_t], 1.0 / D, None,
                            ALU.mult)
            if not rms:
                ins = v.tensor_scalar(st["mean"][:, :n_t], st["mean"][:, :n_t],
                                      1.0 / D, None, ALU.mult)
                edge(ctx, v, ins, tag + "c1")
                ins = v.tensor_tensor(st["msq"][:, :n_t], st["mean"][:, :n_t],
                                      st["mean"][:, :n_t], ALU.mult)
                edge(ctx, v, ins, tag + "c2")
                v.tensor_tensor(st["var"][:, :n_t], st["var"][:, :n_t],
                                st["msq"][:, :n_t], ALU.subtract)
            ins = v.engine_nop()
            ctx.inc(ins, tag + "lgo")
        @blk.scalar
        def _(a):
            ctx.wait(a, tag + "lgo")
            ins = a.activation(st["rstd"][:, :n_t], st["var"][:, :n_t], AF.Ln,
                               bias=st["eps"][:, :])
            edge(ctx, a, ins)
            ins = a.activation(st["rstd"][:, :n_t], st["rstd"][:, :n_t], AF.Exp,
                               scale=-0.5)
            ctx.inc(ins, tag + "lrs")
    with nc.Block() as blk:
        @blk.vector
        def _(v):
            ctx.wait(v, tag + "lrs")
            for tt in range(n_t):
                src = src_tok[:, tt * D:(tt + 1) * D]
                dst = (dst_f32_tiles(tt) if dst_f32_tiles is not None
                       else dst_tok[:, tt * D:(tt + 1) * D])
                if rms:
                    ins = v.tensor_scalar(dst, src, st["rstd"][:, tt:tt + 1], None,
                                          ALU.mult)
                else:
                    ins = v.tensor_scalar(dst, src, st["mean"][:, tt:tt + 1],
                                          st["rstd"][:, tt:tt + 1], ALU.subtract,
                                          ALU.mult)
                if gamma_bc is not None:
                    edge(ctx, v, ins, tag + "c3")
                    v.tensor_tensor(dst, dst, gamma_bc[:, :], ALU.mult)


def alloc_ln_stats(nc, sq_backing=None):
    st = {}
    for nm in ("ss", "mean", "var", "rstd", "msq"):
        st[nm] = nc.alloc_sbuf_tensor("lnst_" + nm, [128, TCH], F32)
    st["eps"] = nc.alloc_sbuf_tensor("lnst_eps", [128, 1], F32)
    if sq_backing is not None:
        st["sq"] = sq_backing
    else:
        st["sq"] = nc.alloc_sbuf_tensor("lnst_sq", [128, 2 * D], BF16)
    return st


def emit_bcast_rows_simple(ctx, tag, items, cs):
    nc = ctx.nc
    _PSUM_UID[0] += 1
    tag2 = tag + "u" + str(_PSUM_UID[0])
    with (
        nc.psum_tensor("bc0" + tag2, [128, 512], F32) as p0,
        nc.psum_tensor("bc1" + tag2, [128, 512], F32) as p1,
        nc.Block() as blk,
    ):
        ps = [p0, p1]

        base_bp = ctx.val(tag + "bp")
        base_be = ctx.val(tag + "be")

        @blk.tensor
        def _(t):
            for n, (row, dst) in enumerate(items):
                w = row.shape[-1] if hasattr(row, "shape") else 512
                if n >= 2:
                    ctx.wait(t, tag + "be", base_be + n - 1)
                ins = t.matmul(ps[n % 2][:, 0:dst.shape[-1]],
                               cs["c_onesb" if row.dtype == BF16 else "c_ones"][0:1, :],
                               row, start=True, stop=True)
                ctx.inc(ins, tag + "bp")

        @blk.vector
        def _(v):
            for n, (row, dst) in enumerate(items):
                ctx.wait(v, tag + "bp", base_bp + n + 1)
                ins = v.tensor_copy(dst, ps[n % 2][:, 0:dst.shape[-1]])
                ctx.inc(ins, tag + "be")



def emit_attention(ctx, tag, qf, kf, vp, yf, cs, scr, scale=0.125):
    nc = ctx.nc
    tiles = []
    groups = []
    n = 0
    for h in range(NH):
        for qt in range(NQT):
            nk = 4 * qt + 4
            groups.append((h, qt, nk, n))
            for kc in range(nk):
                tiles.append((h, qt, kc, nk))
                n += 1
    NT = len(tiles)

    def npartial(idx):
        c = 0
        for i in range(idx + 1):
            h, qt, kc, nk = tiles[i]
            if kc >= nk - 4:
                c += 1
        return c

    pref = []
    c = 0
    for i in range(NT):
        h, qt, kc, nk = tiles[i]
        if kc >= nk - 4:
            c += 1
        pref.append(c)
    group_of_tile = []
    for gi, (h, qt, nk, f) in enumerate(groups):
        group_of_tile += [gi] * nk
    islast = [tiles[i][2] == tiles[i][3] - 1 for i in range(NT)]
    nlast = []
    c2 = 0
    for i in range(NT):
        if islast[i]:
            c2 += 1
        nlast.append(c2)

    E = scr["E"]
    shift = scr["shift"]
    rcp = scr["rcp"]
    bcs = scr["bcs"]
    _PSUM_UID[0] += 1
    tag_u = tag + "u" + str(_PSUM_UID[0])
    bases = {k: ctx.val(k + tag) for k in
             ("ap", "aa", "am", "ad", "aq", "ar", "ab", "av", "ae")}
    with (
        nc.psum_tensor("sc0" + tag_u, [128, QT], F32) as sc0,
        nc.psum_tensor("sc1" + tag_u, [128, QT], F32) as sc1,
        nc.psum_tensor("sc2" + tag_u, [128, QT], F32) as sc2,
        nc.psum_tensor("sc3" + tag_u, [128, QT], F32) as sc3,
        nc.psum_tensor("yp0" + tag_u, [VPW, QT], F32) as yp0,
        nc.psum_tensor("yp1" + tag_u, [VPW, QT], F32) as yp1,
        nc.psum_tensor("bcp" + tag_u, [64, QT], F32) as bcp,
    ):
        with nc.Block() as blk0:
            @blk0.vector
            def _(v):
                v.memset(shift[:, :], EXP_SHIFT)
        scs = [sc0, sc1, sc2, sc3]
        yps = [yp0, yp1]

        def eslice(i):
            return E[:, (i % 4) * QT:(i % 4 + 1) * QT]

        with nc.Block() as blk:
            @blk.tensor
            def _(t):
                def issue_score(i):
                    h, qt, kc, nk = tiles[i]
                    ch, rw = h // 2, (h % 2) * 64
                    if i >= 4:
                        ctx.wait(t, "aa" + tag, bases["aa"] + i - 3)
                    ins = t.matmul(scs[i % 4][:, :],
                                   kf[rw:rw + 64, ch * T + kc * 128:ch * T + (kc + 1) * 128],
                                   qf[rw:rw + 64, ch * T + qt * QT:ch * T + (qt + 1) * QT],
                                   start=True, stop=True)
                    ctx.inc(ins, "ap" + tag)

                def issue_av(i):
                    h, qt, kc, nk = tiles[i]
                    g = group_of_tile[i]
                    ctx.wait(t, "aa" + tag, bases["aa"] + i + 1)
                    if pref[i]:
                        ctx.wait(t, "am" + tag, bases["am"] + pref[i])
                    if kc == 0 and g >= 2:
                        ctx.wait(t, "ae" + tag, bases["ae"] + g - 1)
                    ins = t.matmul(yps[g % 2][:, :],
                                   vp[:, kc * (NH * VPW) + h * VPW:
                                      kc * (NH * VPW) + (h + 1) * VPW],
                                   eslice(i),
                                   start=(kc == 0), stop=(kc == nk - 1))
                    if kc == nk - 1:
                        ctx.inc(ins, "aq" + tag)
                    else:
                        ctx.inc(ins, "ad" + tag)

                def issue_bcast(g):
                    ctx.wait(t, "ar" + tag, bases["ar"] + g + 1)
                    if g >= 1:
                        ctx.wait(t, "av" + tag, bases["av"] + g)
                    ins = t.matmul(bcp[:, :], cs["c_ones"][64:65, 0:64],
                                   rcp[64:65, :], start=True, stop=True)
                    ctx.inc(ins, "ab" + tag)

                bcast_at = {}
                for gi in range(len(groups) - 1):
                    last_g = groups[gi + 1][3] - 1
                    defer = min(3, groups[gi + 1][2] - 4)
                    bcast_at.setdefault(last_g + defer, []).append(gi)

                def after_av(j):
                    for gi in bcast_at.get(j, []):
                        issue_bcast(gi)

                for i in range(NT):
                    issue_score(i)
                    if i >= 2:
                        issue_av(i - 2)
                        after_av(i - 2)
                issue_av(NT - 2)
                after_av(NT - 2)
                issue_av(NT - 1)
                after_av(NT - 1)
                issue_bcast(len(groups) - 1)

            @blk.scalar
            def _(a):
                for i in range(NT):
                    ctx.wait(a, "ap" + tag, bases["ap"] + i + 1)
                    if i >= 4:
                        nl = nlast[i - 4]
                        nn = (i - 3) - nl
                        if nn:
                            ctx.wait(a, "ad" + tag, bases["ad"] + nn)
                        if nl:
                            ctx.wait(a, "aq" + tag, bases["aq"] + nl)
                        if pref[i - 4]:
                            ctx.wait(a, "am" + tag, bases["am"] + pref[i - 4])
                    ins = a.activation(eslice(i), scs[i % 4][:, :], AF.Exp,
                                       bias=shift[:, :], scale=scale)
                    ctx.inc(ins, "aa" + tag)

            @blk.vector
            def _(v):
                for i in range(NT):
                    h, qt, kc, nk = tiles[i]
                    if kc >= nk - 4:
                        ctx.wait(v, "aa" + tag, bases["aa"] + i + 1)
                        j = kc - (nk - 4)
                        ins = v.tensor_tensor(eslice(i), eslice(i),
                                              cs["c_mask"][:, j * QT:(j + 1) * QT],
                                              ALU.mult)
                        ctx.inc(ins, "am" + tag)
                    if kc == nk - 1:
                        g = group_of_tile[i]
                        h2, qt2 = h, qt
                        ch, rw = h2 // 2, (h2 % 2) * 64
                        ctx.wait(v, "aq" + tag, bases["aq"] + g + 1)
                        ins = v.reciprocal(rcp[64:65, :], yps[g % 2][64:65, :])
                        ctx.inc(ins, "ar" + tag)
                        ctx.wait(v, "ab" + tag, bases["ab"] + g + 1)
                        ins = v.tensor_copy(bcs[:, :], bcp[:, :])
                        ctx.inc(ins, "av" + tag)
                        ctx.wait(v, "av" + tag)
                        ins = v.tensor_tensor(
                            yf[rw:rw + 64, ch * T + qt2 * QT:ch * T + (qt2 + 1) * QT],
                            yps[g % 2][0:64, :], bcs[:, :], ALU.mult)
                        ctx.inc(ins, "ae" + tag)



def stage_l0_attn(ctx, cs, st, sm, wd, x_in, x_out, rfm, rq, rk, rvp,
                  dd=None, rg=None):
    nc = ctx.nc
    dd = dd or {}
    ws = sm["ws"]
    if "xa" in dd:
        emit_store_raw(ctx, x_in, dd["xa"], "xa")
    emit_tok2fm(ctx, "t", x_in, rfm, cs)
    if "xfm" in dd:
        emit_store_raw(ctx, rfm, dd["xfm"], "xfm")

    emit_bcast_rows_simple(ctx, "bc", [
        (sm["bv_row"][0:1, j * 512:(j + 1) * 512],
         sm["bc0"][:, j * 512:(j + 1) * 512]) for j in range(2)
    ], cs)

    with (
        nc.psum_tensor("l0a_p0", [128, QT], F32) as p0,
        nc.psum_tensor("l0a_p1", [128, QT], F32) as p1,
        nc.psum_tensor("l0a_p2", [128, QT], F32) as p2,
        nc.psum_tensor("l0a_p3", [128, QT], F32) as p3,
    ):
        jobs, meta, wreq = [], [], []
        dmas, dpereq = [], []
        bw = [ctx.val("mw0"), ctx.val("mw1")]
        for oc in range(16):
            dmas.append((ws[:, (oc % 2) * 1024:(oc % 2) * 1024 + 1024]
                         .rearrange("p (dc c) -> p dc c", c=128),
                         wd["W_attn_ocm"][oc].rearrange("(dc p) c -> p dc c", p=128),
                         "mw" + str(oc % 2)))
            dpereq.append(max(0, (oc - 1) * NQT))
            for tt in range(NQT):
                jobs.append([(ws[:, (oc % 2) * 1024 + dc * 128:(oc % 2) * 1024 + (dc + 1) * 128],
                              rfm[:, dc * T + tt * QT:dc * T + (tt + 1) * QT])
                             for dc in range(DCH)])
                meta.append((oc, tt))
                wreq.append([("mw" + str(oc % 2), bw[oc % 2] + 16 * (oc // 2 + 1))])

        def qk_evac(n, ps, v):
            oc, tt = meta[n]
            dst = rq if oc < 8 else rk
            c = oc % 8
            return v.tensor_scalar(dst[:, c * T + tt * QT:c * T + (tt + 1) * QT],
                                   ps, sm["b3_pp"][:, oc:oc + 1], None, ALU.add)

        emit_matmul_tiles(ctx, "mm", jobs, [p0[:, :], p1[:, :], p2[:, :], p3[:, :]],
                          qk_evac, dmas=dmas, job_wreq=wreq, dma_pereq=dpereq)

        bw = [ctx.val("mw0"), ctx.val("mw1")]
        with nc.Block() as blk:
            @blk.vector
            def _(v):
                v.memset(rvp[:, :TCH * NH * VPW], 0.0)
            @blk.gpsimd
            def _(g):
                for half in range(2):
                    for j in range(4):
                        oc = 16 + half * 4 + j
                        dst = ws[:, half * 4096:(half + 1) * 4096].rearrange(
                            "p (dc j c) -> p dc j c", j=4, c=128)[:, :, j, :]
                        ins = g.dma_start(
                            dst,
                            wd["W_attn_ocm"][oc].rearrange("(dc p) c -> p dc c", p=128))
                        ctx.inc(ins, "mw" + str(half), 16)

        vjobs, vmeta, vwreq = [], [], []
        for half in range(2):
            for tc in range(TCH):
                vjobs.append([(rfm[:, dc * T + tc * 128:dc * T + (tc + 1) * 128],
                               ws[:, half * 4096 + dc * 512:half * 4096 + (dc + 1) * 512])
                              for dc in range(DCH)])
                vmeta.append((half, tc))
                vwreq.append([("mw" + str(half), bw[half] + 64)])

        def v_evac(n, ps, v):
            half, tc = vmeta[n]
            base = tc * (NH * VPW) + half * 8 * VPW
            return v.tensor_add(
                rvp[:, base:base + 8 * VPW].rearrange("p (h w) -> p h w", w=VPW)[:, :, 0:HD],
                ps.rearrange("p (h w) -> p h w", w=HD),
                sm["bc0"][:, half * 512:(half + 1) * 512].rearrange(
                    "p (h w) -> p h w", w=HD))

        emit_matmul_tiles(ctx, "mm", vjobs, [p0[:, :], p1[:, :], p2[:, :], p3[:, :]],
                          v_evac, job_wreq=vwreq)
        with nc.Block() as blk:
            @blk.vector
            def _(v):
                v.memset(rvp[:, :TCH * NH * VPW].rearrange(
                    "p (t h w) -> p t h w", h=NH, w=VPW)[:, :, :, HD:HD + 1], 1.0)

    if "qf" in dd:
        emit_store_raw(ctx, rq, dd["qf"], "qf")
    if "kf" in dd:
        emit_store_raw(ctx, rk, dd["kf"], "kf")
    if "vp" in dd:
        emit_store_raw(ctx, rvp, dd["vp"], "vp")
    emit_attention(ctx, "at", rq, rk, rvp, rfm, cs, attn_scratch(sm))
    if "yf" in dd:
        emit_store_raw(ctx, rfm, dd["yf"], "yf")

    emit_bcast_rows_simple(ctx, "bc", [
        (sm["bp_row"][0:1, j * 512:(j + 1) * 512],
         sm["bc1"][:, j * 512:(j + 1) * 512]) for j in range(2)
    ], cs)

    bw = [ctx.val("mw0"), ctx.val("mw1")]
    with nc.Block() as blk:
        @blk.gpsimd
        def _(g):
            for half in range(2):
                ins = g.dma_start(
                    ws[:, half * 4096:(half + 1) * 4096]
                    .rearrange("p (dc c) -> p dc c", c=512),
                    wd["W_proj_hm"][half].rearrange("(dc p) c -> p dc c", p=128))
                ctx.inc(ins, "mw" + str(half), 16)

    zjobs, zmeta, zwreq = [], [], []
    for half in range(2):
        for tc in range(TCH):
            zjobs.append([(rfm[:, dc * T + tc * 128:dc * T + (tc + 1) * 128],
                           ws[:, half * 4096 + dc * 512:half * 4096 + (dc + 1) * 512])
                          for dc in range(DCH)])
            zmeta.append((half, tc))
            zwreq.append([("mw" + str(half), bw[half] + 16)])

    def z_evac(n, ps, v):
        half, tc = zmeta[n]
        sl = slice(tc * D + half * 512, tc * D + (half + 1) * 512)
        ins = v.tensor_add(x_out[:, sl], ps, sm["bc1"][:, half * 512:(half + 1) * 512])
        edge(ctx, v, ins, "l0azc")
        return v.tensor_tensor(x_out[:, sl], x_out[:, sl], x_in[:, sl], ALU.add)

    with (
        nc.psum_tensor("l0a_z0", [128, QT], F32) as z0,
        nc.psum_tensor("l0a_z1", [128, QT], F32) as z1,
        nc.psum_tensor("l0a_z2", [128, QT], F32) as z2,
        nc.psum_tensor("l0a_z3", [128, QT], F32) as z3,
    ):
        emit_matmul_tiles(ctx, "mm", zjobs, [z0[:, :], z1[:, :], z2[:, :], z3[:, :]],
                          z_evac, job_wreq=zwreq)

    if "xpre" in dd:
        emit_store_raw(ctx, x_out, dd["xpre"], "xpre")
    dbg = None
    if "lnst" in dd:
        dbg = {k: dd["lnst"][:, i * TCH:(i + 1) * TCH]
               for i, k in enumerate(("ss", "mean", "var", "rstd"))}
    emit_ln_stats_apply(ctx, "ln", x_out, x_out, st, EPS_LN, dbg=dbg)



def emit_proj_tok(ctx, tag, src_fm, wname_dram, dst_cb, psums, wsem_offset=0):
    pass


def stage_l0_ffn_attn(ctx, cs, st, sm, wd, x_in, rA, rB, rC, rvp):
    nc = ctx.nc
    ws = sm["ws"]
    emit_bcast_rows_simple(ctx, "bc", [
        (sm["g1_row"][0:1, j * 512:(j + 1) * 512],
         sm["bc0"][:, j * 512:(j + 1) * 512]) for j in range(2)
    ], cs)
    emit_ln_stats_apply(ctx, "ln", x_in, rA, st, EPS_RMS, rms=True,
                        gamma_bc=sm["bc0"])
    emit_tok2fm(ctx, "t", rA, rB, cs)

    def tok_proj(tag, wkey, dst_or_cb):
        jobs, meta, wreq = [], [], []
        bw = [ctx.val("mw0"), ctx.val("mw1")]
        dmas = [(ws[:, half * 4096:(half + 1) * 4096]
                 .rearrange("p (dc c) -> p dc c", c=512),
                 wd[wkey][half].rearrange("(dc p) c -> p dc c", p=128),
                 "mw" + str(half))
                for half in range(2)]
        for half in range(2):
            for tc in range(TCH):
                jobs.append([(rB[:, dc * T + tc * 128:dc * T + (tc + 1) * 128],
                              ws[:, half * 4096 + dc * 512:half * 4096 + (dc + 1) * 512])
                             for dc in range(DCH)])
                meta.append((half, tc))
                wreq.append([("mw" + str(half), bw[half] + 16)])

        def evac(n, ps, v):
            half, tc = meta[n]
            if callable(dst_or_cb):
                return dst_or_cb(half, tc, ps, v)
            return v.tensor_copy(
                dst_or_cb[:, tc * D + half * 512:tc * D + (half + 1) * 512], ps)

        with (
            nc.psum_tensor(tag + "_p0", [128, QT], F32) as p0,
            nc.psum_tensor(tag + "_p1", [128, QT], F32) as p1,
            nc.psum_tensor(tag + "_p2", [128, QT], F32) as p2,
            nc.psum_tensor(tag + "_p3", [128, QT], F32) as p3,
        ):
            emit_matmul_tiles(ctx, "mm", jobs,
                              [p0[:, :], p1[:, :], p2[:, :], p3[:, :]], evac,
                              dmas=dmas, job_wreq=wreq)

    tok_proj("l0fq", "l0f_wq", rA)
    tok_proj("l0fk", "l0f_wk", rC)
    with nc.Block() as blk:
        @blk.vector
        def _(v):
            v.memset(rvp[:, :TCH * NH * VPW], 0.0)

    def v_evac(half, tc, ps, v):
        base = tc * (NH * VPW) + half * 8 * VPW
        return v.tensor_copy(
            rvp[:, base:base + 8 * VPW].rearrange("p (h w) -> p h w", w=VPW)[:, :, 0:HD],
            ps.rearrange("p (h w) -> p h w", w=HD))

    tok_proj("l0fv", "l0f_wv", v_evac)
    with nc.Block() as blk:
        @blk.vector
        def _(v):
            v.memset(rvp[:, :TCH * NH * VPW].rearrange(
                "p (t h w) -> p t h w", h=NH, w=VPW)[:, :, :, HD:HD + 1], 1.0)

    with nc.Block() as blk:
        @blk.gpsimd
        def _(g):
            for tt in range(TCH):
                if tt >= 2:
                    ctx.wait(g, "ropev", (tt - 1) * 2)
                ins = g.dma_start(sm["cr"][:, (tt % 2) * 512:(tt % 2 + 1) * 512],
                                  wd["cr"][tt * 128:(tt + 1) * 128, :])
                ctx.inc(ins, "ropew" + str(tt % 2), 16)
                ins = g.dma_start(sm["ci"][:, (tt % 2) * 512:(tt % 2 + 1) * 512],
                                  wd["ci"][tt * 128:(tt + 1) * 128, :])
                ctx.inc(ins, "ropew" + str(tt % 2), 16)

        @blk.vector
        def _(v):
            t1 = sm["bc0"][:, 0:512]
            t2 = sm["bc0"][:, 512:1024]
            t3 = sm["bc1"][:, 0:512]
            t4 = sm["bc1"][:, 512:1024]
            for tt in range(TCH):
                ctx.wait(v, "ropew" + str(tt % 2), 32 * (tt // 2 + 1))
                cr = sm["cr"][:, (tt % 2) * 512:(tt % 2 + 1) * 512]
                ci = sm["ci"][:, (tt % 2) * 512:(tt % 2 + 1) * 512]
                for reg in (rA, rC):
                    tile = reg[:, tt * D:(tt + 1) * D].rearrange(
                        "p (hi two) -> p hi two", two=2)
                    EV = tile[:, :, 0]
                    OD = tile[:, :, 1]
                    v.tensor_tensor(t1, EV, cr, ALU.mult)
                    v.tensor_tensor(t2, OD, ci, ALU.mult)
                    v.tensor_tensor(t3, OD, cr, ALU.mult)
                    ins = v.tensor_tensor(t4, EV, ci, ALU.mult)
                    edge(ctx, v, ins, "ropec")
                    v.tensor_tensor(EV, t1, t2, ALU.subtract)
                    ins = v.tensor_tensor(OD, t3, t4, ALU.add)
                    edge(ctx, v, ins)
                ins = v.engine_nop()
                ctx.inc(ins, "ropev", 2)

    emit_tok2fm(ctx, "t", rA, rB, cs)
    emit_tok2fm(ctx, "t", rC, rA, cs)

    emit_attention(ctx, "at", rB, rA, rvp, rC, cs, attn_scratch(sm))

    def h_evac(half, tc, ps, v):
        sl = slice(tc * D + half * 512, tc * D + (half + 1) * 512)
        return v.tensor_add(rA[:, sl], ps, x_in[:, sl])

    jobs, meta, wreq = [], [], []
    bw = [ctx.val("mw0"), ctx.val("mw1")]
    dmas = [(ws[:, half * 4096:(half + 1) * 4096]
             .rearrange("p (dc c) -> p dc c", c=512),
             wd["l0f_wo"][half].rearrange("(dc p) c -> p dc c", p=128),
             "mw" + str(half))
            for half in range(2)]
    for half in range(2):
        for tc in range(TCH):
            jobs.append([(rC[:, dc * T + tc * 128:dc * T + (tc + 1) * 128],
                          ws[:, half * 4096 + dc * 512:half * 4096 + (dc + 1) * 512])
                         for dc in range(DCH)])
            meta.append((half, tc))
            wreq.append([("mw" + str(half), bw[half] + 16)])

    def evac(n, ps, v):
        half, tc = meta[n]
        return h_evac(half, tc, ps, v)

    with (
        nc.psum_tensor("l0fo_p0", [128, QT], F32) as p0,
        nc.psum_tensor("l0fo_p1", [128, QT], F32) as p1,
        nc.psum_tensor("l0fo_p2", [128, QT], F32) as p2,
        nc.psum_tensor("l0fo_p3", [128, QT], F32) as p3,
    ):
        emit_matmul_tiles(ctx, "mm", jobs,
                          [p0[:, :], p1[:, :], p2[:, :], p3[:, :]], evac,
                          dmas=dmas, job_wreq=wreq)


def declare_l0f_attn_weights(nc):
    wd = {}
    for wname in ("wq", "wk", "wv", "wo"):
        w = nc.dram_tensor("l0f_" + wname, [2, D, 512], F32, kind="ExternalInput")
        wd["l0f_" + wname] = [w[i] for i in range(2)]
    wd["cr"] = nc.dram_tensor("cr", [T, 512], F32, kind="ExternalInput")
    wd["ci"] = nc.dram_tensor("ci", [T, 512], F32, kind="ExternalInput")
    wd["g1_row"] = nc.dram_tensor("g1_row", [1, D], F32, kind="ExternalInput")
    return wd


def host_l0f_attn_weight_maps(ins):
    m = {}
    for wname in ("wq", "wk", "wv", "wo"):
        W = np.asarray(ins["L0_ffn_" + wname])
        m["l0f_" + wname] = np.ascontiguousarray(
            W.reshape(D, 2, 512).transpose(1, 0, 2))
    fr = np.asarray(ins["L0_ffn_freqs_cis"]).reshape(T, 32, 2)
    m["cr"] = np.ascontiguousarray(np.tile(fr[:, :, 0], (1, NH)))
    m["ci"] = np.ascontiguousarray(np.tile(fr[:, :, 1], (1, NH)))
    m["g1_row"] = np.asarray(ins["L0_ffn_attn_norm_w"]).reshape(1, D)
    return m


def build_stage_b_program():
    nc = bass.Bass("TRN2", target_bir_lowering=False)
    ctx = Ctx(nc)
    x_d = nc.dram_tensor("x_in", [T, D], F32, kind="ExternalInput")
    out_d = nc.dram_tensor("x_out", [T, D], F32, kind="ExternalOutput")
    wd = declare_l0f_attn_weights(nc)
    cs = setup_constants(ctx)
    rg = alloc_regions(nc)
    sm = alloc_smalls(nc)
    sm["g1_row"] = nc.alloc_sbuf_tensor("g1_row_s", [1, D], BF16)
    sm["cr"] = nc.alloc_sbuf_tensor("cr_s", [128, 1024], BF16)
    sm["ci"] = nc.alloc_sbuf_tensor("ci_s", [128, 1024], BF16)
    st = alloc_ln_stats(nc, sq_backing=sm["ws"][:, :2 * D])
    emit_load_small(ctx, [(sm["g1_row"][:, :], wd["g1_row"][:, :])])
    emit_load_x(ctx, x_d, rg["r2"])
    stage_l0_ffn_attn(ctx, cs, st, sm, wd, x_in=rg["r2"], rA=rg["r0"],
                      rB=rg["r1"], rC=rg["r3"], rvp=rg["rvp"])
    emit_store_x(ctx, rg["r0"], out_d)
    nc.finalize()
    return nc



def emit_combine_weights(ctx, tag, rFm, cmbw, sm, cs, rw_key, dram_rw, nacc=DCH,
                         in_stride=T):
    nc = ctx.nc
    with nc.Block() as blk:
        @blk.gpsimd
        def _(g):
            ins = g.dma_start(sm[rw_key][:, :].rearrange("p (dc e) -> p dc e", e=8),
                              dram_rw)
            ctx.inc(ins, tag + "rw", 16)
    with (
        nc.psum_tensor(tag + "_p0", [128, 8], F32) as prt0,
        nc.psum_tensor(tag + "_p1", [128, 8], F32) as prt1,
        nc.Block() as blk,
    ):
        prt = [prt0, prt1]
        @blk.tensor
        def _(t):
            ctx.wait(t, tag + "rw")
            for tt in range(TCH):
                if tt >= 2:
                    ctx.wait(t, tag + "re", tt - 1)
                for dc in range(nacc):
                    ins = t.matmul(prt[tt % 2][:, :],
                                   rFm[:, dc * in_stride + tt * 128:
                                       dc * in_stride + (tt + 1) * 128],
                                   sm[rw_key][:, dc * 8:(dc + 1) * 8],
                                   start=(dc == 0), stop=(dc == nacc - 1))
                ctx.inc(ins, tag + "rp")

        @blk.vector
        def _(v):
            L = sm["cmb_l"]
            m1 = sm["cmb_s"][:, 0:1]
            m2 = sm["cmb_s"][:, 1:2]
            nm1 = sm["cmb_s"][:, 2:3]
            den = sm["cmb_s"][:, 3:4]
            eq = sm["cmb_t"][:, 0:8]
            l2 = sm["cmb_t"][:, 8:16]
            e1 = sm["cmb_t"][:, 16:24]
            ge = sm["cmb_t"][:, 24:32]
            for tt in range(TCH):
                ctx.wait(v, tag + "rp", tt + 1)
                ins = v.tensor_copy(L[:, :], prt[tt % 2][:, :])
                ctx.inc(ins, tag + "re")
                ctx.wait(v, tag + "re")
                ins = v.tensor_reduce(m1[:, :], L[:, :], AX.X, ALU.max)
                edge(ctx, v, ins, tag + "c")
                ins = v.tensor_scalar(eq[:, :], L[:, :], m1[:, :], None, ALU.is_equal)
                edge(ctx, v, ins, tag + "c")
                ins = v.scalar_tensor_tensor(l2[:, :], eq[:, :], -1e30, L[:, :],
                                             ALU.mult, ALU.add)
                edge(ctx, v, ins, tag + "c")
                ins = v.tensor_reduce(m2[:, :], l2[:, :], AX.X, ALU.max)
                edge(ctx, v, ins, tag + "c")
                ins = v.tensor_scalar(nm1[:, :], m1[:, :], -1.0, None, ALU.mult)
                ins = v.tensor_scalar(ge[:, :], L[:, :], m2[:, :], None, ALU.is_ge)
                ctx.inc(ins, tag + "cx")
                ctx.wait(v, tag + "ca", tt + 1)
                ins = v.tensor_tensor(e1[:, :], e1[:, :], ge[:, :], ALU.mult)
                edge(ctx, v, ins, tag + "c")
                ins = v.tensor_reduce(den[:, :], e1[:, :], AX.X, ALU.add)
                edge(ctx, v, ins, tag + "c")
                ins = v.reciprocal(den[:, :], den[:, :])
                edge(ctx, v, ins, tag + "c")
                ins = v.tensor_scalar(cmbw[:, tt * 8:(tt + 1) * 8], e1[:, :],
                                      den[:, :], None, ALU.mult)
                ctx.inc(ins, tag + "cv")

        @blk.scalar
        def _(a):
            L = sm["cmb_l"]
            nm1 = sm["cmb_s"][:, 2:3]
            e1 = sm["cmb_t"][:, 16:24]
            for tt in range(TCH):
                ctx.wait(a, tag + "cx", tt + 1)
                ins = a.activation(e1[:, :], L[:, :], AF.Exp, bias=nm1[:, :])
                ctx.inc(ins, tag + "ca")


def stage_l0_moe(ctx, cs, st, sm, wd, rH, rX1, rOut, rFm, h_sb, acc32):
    nc = ctx.nc
    TH = T // 2

    emit_bcast_rows_simple(ctx, "bc", [
        (sm["g2_row"][0:1, j * 512:(j + 1) * 512],
         sm["bc0"][:, j * 512:(j + 1) * 512]) for j in range(2)
    ], cs)
    emit_ln_stats_apply(ctx, "ln", rH, rOut, st, EPS_RMS, rms=True,
                        gamma_bc=sm["bc0"])
    with nc.Block() as blk:
        @blk.vector
        def _(v):
            for tc in range(TCH):
                v.tensor_tensor(rX1[:, tc * D:(tc + 1) * D],
                                rX1[:, tc * D:(tc + 1) * D],
                                rH[:, tc * D:(tc + 1) * D], ALU.add)
    emit_tok2fm(ctx, "t", rOut, rFm, cs)

    emit_combine_weights(ctx, "l0mr", rFm, sm["cmbw"], sm, cs, "rw",
                         wd["router_w"][:, :].rearrange("(dc p) e -> p dc e", p=128))

    cmbT = sm["cmbT"]
    with (
        nc.psum_tensor("l0mt_p", [8, 128], F32) as ptp,
        nc.Block() as blk,
    ):
        @blk.tensor
        def _(t):
            for tt in range(TCH):
                if tt >= 1:
                    ctx.wait(t, "l0mte", tt)
                ins = t.transpose(ptp[:, :], sm["cmbw"][:, tt * 8:(tt + 1) * 8],
                                  cs["c_identf"][:, :])
                ctx.inc(ins, "l0mtp")

        @blk.vector
        def _(v):
            for tt in range(TCH):
                ctx.wait(v, "l0mtp", tt + 1)
                ins = v.tensor_copy(cmbT[:, tt * 128:(tt + 1) * 128], ptp[:, :])
                ctx.inc(ins, "l0mte")
    with nc.Block() as blk:
        @blk.gpsimd
        def _(g):
            ins = g.dma_start(wd["cmb_d"][:, :], cmbT[:, :])
            ctx.inc(ins, "l0mtd", 16)
            ctx.wait(g, "l0mtd")

    w1s, w3s, w2s, stmp, cbrow = (sm["w1s"], sm["w3s"], sm["w2s"],
                                  sm["stmp"], sm["cbrow"])
    cbh = rOut[:, 8 * 1024:16 * 1024]
    cnt = {"wpair": 0, "w2": 0, "jh": 0, "jy": 0}

    for half in range(2):
        hof = half * TH
        with nc.Block() as blk:
            @blk.vector
            def _(v):
                v.memset(acc32[0][:, :], 0.0)
                v.memset(acc32[1][:, :], 0.0)
        with (
            nc.psum_tensor("l0mcb_p" + str(half), [128, 512], F32) as pcb,
            nc.Block() as blk,
        ):
            @blk.gpsimd
            def _(g):
                for th in range(2):
                    for e in range(8):
                        n = half * 16 + th * 8 + e
                        if n >= 1:
                            ctx.wait(g, "l0mbp", n)
                        src = bass.AP(wd["cmb_d"], e * T + hof + th * 512,
                                      [[1, 1], [1, 512]])
                        ins = g.dma_start(cbrow[:, :], src)
                        ctx.inc(ins, "l0mcw", 16)

            @blk.tensor
            def _(t):
                for th in range(2):
                    for e in range(8):
                        n = half * 16 + th * 8 + e
                        ctx.wait(t, "l0mcw", 16 * (n + 1))
                        if n >= 1:
                            ctx.wait(t, "l0mbe", n)
                        ins = t.matmul(pcb[:, :], cs["c_onesb"][0:1, :],
                                       cbrow[0:1, :],
                                       start=True, stop=True)
                        ctx.inc(ins, "l0mbp")

            @blk.vector
            def _(v):
                for th in range(2):
                    for e in range(8):
                        n = half * 16 + th * 8 + e
                        ctx.wait(v, "l0mbp", n + 1)
                        ins = v.tensor_copy(
                            cbh[:, e * 1024 + th * 512:e * 1024 + (th + 1) * 512],
                            pcb[:, :])
                        ctx.inc(ins, "l0mbe")

        wp0, w20 = cnt["wpair"], cnt["w2"]
        jh0, jy0 = cnt["jh"], cnt["jy"]
        h_end_of_e = {}
        y_end_of_e = {}
        jh, jy, wp, w2i = jh0, jy0, wp0, w20
        for e in range(8):
            jh += 32
            wp += 16
            h_end_of_e[e] = jh
            jy += 16
            w2i += 8
            y_end_of_e[e] = jy

        with (
            nc.psum_tensor("l0mh_a0" + str(half), [128, 512], F32) as ha0,
            nc.psum_tensor("l0mh_b0" + str(half), [128, 512], F32) as hb0,
            nc.psum_tensor("l0mh_a1" + str(half), [128, 512], F32) as ha1,
            nc.psum_tensor("l0mh_b1" + str(half), [128, 512], F32) as hb1,
            nc.psum_tensor("l0mh_a2" + str(half), [128, 512], F32) as ha2,
            nc.psum_tensor("l0mh_b2" + str(half), [128, 512], F32) as hb2,
            nc.psum_tensor("l0my_0" + str(half), [128, 512], F32) as yq0,
            nc.psum_tensor("l0my_1" + str(half), [128, 512], F32) as yq1,
            nc.Block() as blk,
        ):
            hpa = [ha0, ha1, ha2]
            hpb = [hb0, hb1, hb2]
            yps = [yq0, yq1]

            @blk.gpsimd
            def _(g):
                wpx, w2x = wp0, w20
                for e in range(8):
                    for ic in range(16):
                        if wpx >= 2:
                            ctx.wait(g, "mhp", (wpx - 1) * 2)
                        for wkey, dst in (("w1_l", w1s), ("w3_l", w3s)):
                            ins = g.dma_start(
                                dst[:, (wpx % 2) * 1024:(wpx % 2 + 1) * 1024]
                                .rearrange("p (dc c) -> p dc c", c=128),
                                wd[wkey][e * 16 + ic].rearrange(
                                    "(dc p) c -> p dc c", p=128))
                            ctx.inc(ins, "mhw" + str(wpx % 2), 16)
                        wpx += 1
                    for dc in range(8):
                        if w2x >= 2:
                            ctx.wait(g, "myp", (w2x - 1) * 2)
                        ins = g.dma_start(
                            w2s[:, (w2x % 2) * 2048:(w2x % 2 + 1) * 2048]
                            .rearrange("p (ic c) -> p ic c", c=128),
                            wd["w2_l"][e * 8 + dc].rearrange(
                                "(ic p) c -> p ic c", p=128))
                        ctx.inc(ins, "myw" + str(w2x % 2), 16)
                        w2x += 1

            @blk.tensor
            def _(t):
                jhx, jyx = jh0, jy0
                for e in range(8):
                    for ic in range(16):
                        wpair = wp0 + e * 16 + ic
                        ctx.wait(t, "mhw" + str(wpair % 2), 32 * (wpair // 2 + 1))
                        for th in range(2):
                            if jhx >= 3:
                                ctx.wait(t, "mhe", jhx - 2)
                                ctx.wait(t, "mha", jhx - 2)
                            pa, pb = hpa[jhx % 3], hpb[jhx % 3]
                            for dc in range(DCH):
                                t.matmul(pa[:, :],
                                         w1s[:, (wpair % 2) * 1024 + dc * 128:
                                             (wpair % 2) * 1024 + (dc + 1) * 128],
                                         rFm[:, dc * T + hof + th * 512:
                                             dc * T + hof + (th + 1) * 512],
                                         start=(dc == 0), stop=(dc == DCH - 1))
                            for dc in range(DCH):
                                ins = t.matmul(pb[:, :],
                                               w3s[:, (wpair % 2) * 1024 + dc * 128:
                                                   (wpair % 2) * 1024 + (dc + 1) * 128],
                                               rFm[:, dc * T + hof + th * 512:
                                                   dc * T + hof + (th + 1) * 512],
                                               start=(dc == 0), stop=(dc == DCH - 1))
                            ctx.inc(ins, "mhp")
                            jhx += 1
                    for dc in range(8):
                        w2x = w20 + e * 8 + dc
                        ctx.wait(t, "myw" + str(w2x % 2), 16 * (w2x // 2 + 1))
                        for th in range(2):
                            if jyx >= 2:
                                ctx.wait(t, "mye", jyx - 1)
                            ctx.wait(t, "mhe", h_end_of_e[e])
                            for ic in range(16):
                                ins = t.matmul(
                                    yps[jyx % 2][:, :],
                                    w2s[:, (w2x % 2) * 2048 + ic * 128:
                                        (w2x % 2) * 2048 + (ic + 1) * 128],
                                    h_sb[:, ic * 1024 + th * 512:
                                         ic * 1024 + (th + 1) * 512],
                                    start=(ic == 0), stop=(ic == 15))
                            ctx.inc(ins, "myp")
                            jyx += 1

            @blk.scalar
            def _(a):
                jhx = jh0
                for e in range(8):
                    for ic in range(16):
                        for th in range(2):
                            ctx.wait(a, "mhp", jhx + 1)
                            ins = a.activation(
                                stmp[:, (jhx % 3) * 512:(jhx % 3 + 1) * 512],
                                hpa[jhx % 3][:, :], AF.Silu)
                            ctx.inc(ins, "mha")
                            jhx += 1

            @blk.vector
            def _(v):
                jhx, jyx = jh0, jy0
                for e in range(8):
                    for ic in range(16):
                        for th in range(2):
                            ctx.wait(v, "mha", jhx + 1)
                            if ic == 0 and th == 0 and jyx > jy0 - 0 and e > 0:
                                ctx.wait(v, "myp", jyx)
                            elif e == 0 and ic == 0 and th == 0 and jy0 > 0:
                                ctx.wait(v, "myp", jy0)
                            dst = h_sb[:, ic * 1024 + th * 512:
                                       ic * 1024 + (th + 1) * 512]
                            ins = v.tensor_tensor(
                                dst, stmp[:, (jhx % 3) * 512:(jhx % 3 + 1) * 512],
                                hpb[jhx % 3][:, :], ALU.mult)
                            edge(ctx, v, ins, "mhc")
                            ins = v.tensor_tensor(
                                dst, dst,
                                cbh[:, e * 1024 + th * 512:e * 1024 + (th + 1) * 512],
                                ALU.mult)
                            ctx.inc(ins, "mhe")
                            jhx += 1
                    for dc in range(8):
                        for th in range(2):
                            ctx.wait(v, "myp", jyx + 1)
                            sl = acc32[th][:, dc * 512:(dc + 1) * 512]
                            ins = v.tensor_tensor(sl, sl, yps[jyx % 2][:, :], ALU.add)
                            ctx.inc(ins, "mye")
                            jyx += 1
        cnt["wpair"] = wp
        cnt["w2"] = w2i
        cnt["jh"] = jh
        cnt["jy"] = jy

        with (
            nc.psum_tensor("l0ma_p0" + str(half), [128, 128], F32) as tp0,
            nc.psum_tensor("l0ma_p1" + str(half), [128, 128], F32) as tp1,
            nc.Block() as blk,
        ):
            tps = [tp0, tp1]

            base_tp = ctx.val("mtp")
            base_te = ctx.val("mte")

            @blk.tensor
            def _(t):
                n = 0
                for ts in range(8):
                    for dc in range(8):
                        if n >= 2:
                            ctx.wait(t, "mte", base_te + n - 1)
                        ins = t.transpose(
                            tps[n % 2][:, :],
                            acc32[ts // 4]
                            [:, dc * 512 + (ts % 4) * 128:
                             dc * 512 + (ts % 4 + 1) * 128],
                            cs["c_identf"][:, :])
                        ctx.inc(ins, "mtp")
                        n += 1

            @blk.vector
            def _(v):
                n = 0
                for ts in range(8):
                    tc = half * 8 + ts
                    for dc in range(8):
                        ctx.wait(v, "mtp", base_tp + n + 1)
                        ins = v.tensor_add(
                            rOut[:, tc * D + dc * 128:tc * D + (dc + 1) * 128],
                            tps[n % 2][:, :],
                            rX1[:, tc * D + dc * 128:tc * D + (dc + 1) * 128])
                        ctx.inc(ins, "mte")
                        n += 1

    emit_ln_stats_apply(ctx, "ln", rOut, rOut, st, EPS_LN)


def declare_l0_moe_weights(nc):
    wd = {}
    w1 = nc.dram_tensor("w1_l", [8 * 16, D, 128], F32, kind="ExternalInput")
    wd["w1_l"] = [w1[i] for i in range(128)]
    w3 = nc.dram_tensor("w3_l", [8 * 16, D, 128], F32, kind="ExternalInput")
    wd["w3_l"] = [w3[i] for i in range(128)]
    w2 = nc.dram_tensor("w2_l", [8 * 8, 16 * 128, 128], F32, kind="ExternalInput")
    wd["w2_l"] = [w2[i] for i in range(64)]
    wd["router_w"] = nc.dram_tensor("router_w", [D, 8], F32, kind="ExternalInput")
    wd["g2_row"] = nc.dram_tensor("g2_row", [1, D], F32, kind="ExternalInput")
    wd["cmb_d"] = nc.dram_tensor("cmb_d", [8, T], BF16)
    return wd


def host_l0_moe_weight_maps(ins):
    m = {}
    w1 = np.asarray(ins["L0_ffn_w1"])
    m["w1_l"] = np.ascontiguousarray(
        w1.reshape(8, D, 16, 128).transpose(0, 2, 1, 3).reshape(128, D, 128))
    w3 = np.asarray(ins["L0_ffn_w3"])
    m["w3_l"] = np.ascontiguousarray(
        w3.reshape(8, D, 16, 128).transpose(0, 2, 1, 3).reshape(128, D, 128))
    w2 = np.asarray(ins["L0_ffn_w2"])
    m["w2_l"] = np.ascontiguousarray(
        w2.reshape(8, 16, 128, 8, 128).transpose(0, 3, 1, 2, 4)
        .reshape(64, 16 * 128, 128))
    m["router_w"] = np.asarray(ins["L0_ffn_router_w"])
    m["g2_row"] = np.asarray(ins["L0_ffn_ffn_norm_w"]).reshape(1, D)
    return m


def alloc_moe_smalls(nc, sm, ar=None):
    uid = [0]
    def _mk(sh, dt):
        uid[0] += 1
        return nc.alloc_sbuf_tensor("moe_%d" % uid[0], sh, dt)
    A = ar.take if ar else _mk
    sm["rw"] = A([128, 64], BF16)
    sm["cmbw"] = A([128, TCH * 8], F32)
    sm["cmb_l"] = A([128, 8], F32)
    sm["cmb_s"] = A([128, 4], F32)
    sm["cmb_t"] = A([128, 32], F32)
    sm["cmbT"] = A([8, T], BF16)
    ws = sm["ws"]
    sm["w1s"] = ws[:, 0:2048]
    sm["w3s"] = ws[:, 2048:4096]
    sm["w2s"] = ws[:, 4096:8192]
    sm["stmp"] = sm["bc0"].bitcast(BF16)[:, 0:1536]
    sm["cbrow"] = (ar.take([1, 512], BF16) if ar else
                   nc.alloc_sbuf_tensor("cbrow", [1, 512], BF16))
    return sm


def emit_load_x2(ctx, x_d, xa):
    nc = ctx.nc
    with nc.Block() as blk:
        @blk.gpsimd
        def _(g):
            for tc in range(TCH):
                ins = g.dma_start(xa[:, tc * D:(tc + 1) * D],
                                  x_d[tc * 128:(tc + 1) * 128, :])
                ctx.inc(ins, "xld2", 16)
            ctx.wait(g, "xld2")


def build_stage_c_program():
    nc = bass.Bass("TRN2", target_bir_lowering=False)
    ctx = Ctx(nc)
    h_d = nc.dram_tensor("h_in", [T, D], F32, kind="ExternalInput")
    x1_d = nc.dram_tensor("x1_in", [T, D], F32, kind="ExternalInput")
    out_d = nc.dram_tensor("x_out", [T, D], F32, kind="ExternalOutput")
    wd = declare_l0_moe_weights(nc)
    cs = setup_constants(ctx)
    rg = alloc_regions(nc)
    sm = alloc_smalls(nc)
    sm["g2_row"] = nc.alloc_sbuf_tensor("g2_row_s", [1, D], BF16)
    alloc_moe_smalls(nc, sm)
    st = alloc_ln_stats(nc, sq_backing=sm["ws"][:, :2 * D])
    emit_load_small(ctx, [(sm["g2_row"][:, :], wd["g2_row"][:, :])])
    emit_load_x(ctx, h_d, rg["r0"])
    emit_load_x2(ctx, x1_d, rg["r2"])
    acc32 = [rg["r0"].bitcast(F32)[:, 0:4096], rg["r0"].bitcast(F32)[:, 4096:8192]]
    stage_l0_moe(ctx, cs, st, sm, wd, rH=rg["r0"], rX1=rg["r2"], rOut=rg["r1"],
                 rFm=rg["r3"], h_sb=rg["rvp"][:, :16 * 1024], acc32=acc32)
    emit_store_x(ctx, rg["r1"], out_d)
    nc.finalize()
    return nc



def stage_ttt(ctx, cs, st, sm, wd, rX, rFm, rTH, rvp, wreg, extra_gdma=None):
    nc = ctx.nc
    ws = sm["ws"]
    ktok = ws[:, 0:1024]
    vtok = ws[:, 1024:2048]
    qtok = ws[:, 2048:3072]
    kfm = ws[:, 3072:4096]
    qfm = ws[:, 4096:5120]
    ubf = ws[:, 5120:6144]
    Xs = [ws[:, 6144:6272], ws[:, 6400:6528]]
    Ys = [ws[:, 6272:6400], ws[:, 6528:6656]]
    Msb = ws[:, 6656:6784]
    sq2 = ws[:, 6784:7808]
    wbf = rvp[:, 0:8192]
    thq = rvp[:, 8192:16384]
    bbc = sm["bc1"]
    tmpO = sm["ttO"]
    tmpB = sm["bc0"]
    brow = sm["tt_brow"]
    Uf = sm["ttU"]
    st2 = dict(st)
    st2["sq"] = sq2

    emit_tok2fm(ctx, "t", rX, rFm, cs)

    with nc.Block() as blk:
        @blk.gpsimd
        def _(g):
            for dc in range(DCH):
                for key, dst in (("thK", rTH[:, dc * 1024:(dc + 1) * 1024]),
                                 ("thV", rTH[:, 8192 + dc * 1024:8192 + (dc + 1) * 1024]),
                                 ("thQ", thq[:, dc * 1024:(dc + 1) * 1024])):
                    ins = g.dma_start(dst, wd[key][dc])
                    ctx.inc(ins, "ttw", 16)
            for ec in range(DCH):
                ins = g.dma_start(wreg[:, ec * 1024:(ec + 1) * 1024], wd["w0T"][ec])
                ctx.inc(ins, "ttw", 16)
                ins = g.dma_start(wbf[:, ec * 1024:(ec + 1) * 1024], wd["w0T"][ec])
                ctx.inc(ins, "ttw", 16)
            ins = g.dma_start(brow[:, :], wd["b0"][:, :])
            ctx.inc(ins, "ttw", 16)
            if extra_gdma is not None:
                extra_gdma(g)
            ctx.wait(g, "ttw")
    emit_bcast_rows_simple(ctx, "bc", [
        (brow[0:1, j * 512:(j + 1) * 512], bbc[:, j * 512:(j + 1) * 512])
        for j in range(2)], cs)

    for c in range(TCH):
        tag = "tt"
        jobs, meta = [], []
        for src, dst in ((rTH, ktok), (None, vtok), (None, qtok)):
            pass
        proj_srcs = [(0, ktok), (1, vtok), (2, qtok)]
        jobs = []
        dsts = []
        for wi, dstt in proj_srcs:
            for it in range(2):
                job = []
                for dc in range(DCH):
                    if wi == 0:
                        rhs = rTH[:, dc * 1024 + it * 512:dc * 1024 + (it + 1) * 512]
                    elif wi == 1:
                        rhs = rTH[:, 8192 + dc * 1024 + it * 512:
                                  8192 + dc * 1024 + (it + 1) * 512]
                    else:
                        rhs = thq[:, dc * 1024 + it * 512:dc * 1024 + (it + 1) * 512]
                    job.append((rFm[:, dc * T + c * 128:dc * T + (c + 1) * 128], rhs))
                jobs.append(job)
                dsts.append((dstt, it))

        def proj_evac(n, ps, v):
            dstt, it = dsts[n]
            return v.tensor_copy(dstt[:, it * 512:(it + 1) * 512], ps)

        with (
            nc.psum_tensor(tag + str(c) + "p0", [128, 512], F32) as p0,
            nc.psum_tensor(tag + str(c) + "p1", [128, 512], F32) as p1,
            nc.psum_tensor(tag + str(c) + "p2", [128, 512], F32) as p2,
            nc.psum_tensor(tag + str(c) + "p3", [128, 512], F32) as p3,
        ):
            emit_matmul_tiles(ctx, "mm", jobs,
                              [p0[:, :], p1[:, :], p2[:, :], p3[:, :]], proj_evac)

        pairs = []
        for ec in range(DCH):
            pairs.append((ktok[:, ec * 128:(ec + 1) * 128],
                          kfm[:, ec * 128:(ec + 1) * 128]))
            pairs.append((qtok[:, ec * 128:(ec + 1) * 128],
                          qfm[:, ec * 128:(ec + 1) * 128]))
        emit_transposes(ctx, "t", pairs, cs)

        with (
            nc.psum_tensor(tag + str(c) + "B0", [128, 512], F32) as B0,
            nc.psum_tensor(tag + str(c) + "B1", [128, 512], F32) as B1,
            nc.psum_tensor(tag + str(c) + "A", [128, 128], F32) as Aps,
            nc.psum_tensor(tag + str(c) + "KQ", [128, 128], F32) as KQps,
            nc.psum_tensor(tag + str(c) + "O0", [128, 512], F32) as O0,
            nc.psum_tensor(tag + str(c) + "O1", [128, 512], F32) as O1,
            nc.psum_tensor(tag + str(c) + "S0", [128, 512], F32) as S0,
            nc.psum_tensor(tag + str(c) + "S1", [128, 512], F32) as S1,
        ):
            Bps = [B0, B1]
            Ops = [O0, O1]
            Sps = [S0, S1]
            with nc.Block() as blk:
                @blk.tensor
                def _(t):
                    for it in range(2):
                        for ec in range(DCH):
                            t.matmul(Bps[it][:, :],
                                     kfm[:, ec * 128:(ec + 1) * 128],
                                     wbf[:, ec * 1024 + it * 512:ec * 1024 + (it + 1) * 512],
                                     start=(ec == 0), stop=(ec == DCH - 1))
                    for ec in range(DCH):
                        t.matmul(Aps[:, :], kfm[:, ec * 128:(ec + 1) * 128],
                                 kfm[:, ec * 128:(ec + 1) * 128],
                                 start=(ec == 0), stop=(ec == DCH - 1))
                    for ec in range(DCH):
                        t.matmul(KQps[:, :], kfm[:, ec * 128:(ec + 1) * 128],
                                 qfm[:, ec * 128:(ec + 1) * 128],
                                 start=(ec == 0), stop=(ec == DCH - 1))
                    for it in range(2):
                        for ec in range(DCH):
                            ins = t.matmul(Ops[it][:, :],
                                           qfm[:, ec * 128:(ec + 1) * 128],
                                           wbf[:, ec * 1024 + it * 512:
                                               ec * 1024 + (it + 1) * 512],
                                           start=(ec == 0), stop=False)
                    ctx.inc(ins, tag + "mm1")
            with nc.Block() as blk:
                @blk.vector
                def _(v):
                    ctx.wait(v, tag + "mm1")
                    for it in range(2):
                        sl = slice(it * 512, (it + 1) * 512)
                        ins = v.tensor_add(tmpB[:, sl], Bps[it][:, :], bbc[:, sl])
                        edge(ctx, v, ins, tag + "c")
                        ins = v.tensor_tensor(tmpB[:, sl], tmpB[:, sl], vtok[:, sl],
                                              ALU.subtract)
                        edge(ctx, v, ins, tag + "c")
                        ins = v.tensor_scalar(Uf[:, sl], tmpB[:, sl], -A_TTT, None,
                                              ALU.mult)
                        edge(ctx, v, ins, tag + "c")
                        v.tensor_copy(ubf[:, sl], Uf[:, sl])
                    ins = v.scalar_tensor_tensor(Xs[0][:, :], Aps[:, :], 1.0,
                                                 cs["c_trils"][:, :], ALU.add, ALU.mult)
                    ins = v.scalar_tensor_tensor(Ys[0][:, :], Aps[:, :], 1.0,
                                                 cs["c_trius"][:, :], ALU.add, ALU.mult)
                    ins = v.scalar_tensor_tensor(Msb[:, :], KQps[:, :], 1.0,
                                                 cs["c_uincl"][:, :], ALU.add, ALU.mult)

            base_svu = ctx.val(tag + "svu")
            with nc.Block() as blk:
                @blk.tensor
                def _(t):
                    for lvl in range(7):
                        cur = lvl % 2
                        if lvl >= 1:
                            ctx.wait(t, tag + "svu", base_svu + lvl)
                        for it in range(2):
                            ins = t.matmul(Sps[it][:, :], Ys[cur][:, :],
                                           ubf[:, it * 512:(it + 1) * 512],
                                           start=True, stop=True)
                        ctx.inc(ins, tag + "sva")
                        if lvl < 6:
                            t.matmul(KQps[:, :], Ys[cur][:, :], Xs[cur][:, :],
                                     start=True, stop=True)
                            ins = t.matmul(Aps[:, :], Xs[cur][:, :], Ys[cur][:, :],
                                           start=True, stop=True)
                            ctx.inc(ins, tag + "svb")

                @blk.vector
                def _(v):
                    for lvl in range(7):
                        ctx.wait(v, tag + "sva", ctx.val(tag + "sva") - 6 + lvl)
                        op = ALU.subtract if lvl == 0 else ALU.add
                        for it in range(2):
                            sl = slice(it * 512, (it + 1) * 512)
                            ins = v.tensor_tensor(Uf[:, sl], Uf[:, sl],
                                                  Sps[it][:, :], op)
                            if it == 0:
                                edge(ctx, v, ins, tag + "c")
                        ctx.inc(ins, tag + "svd")

                @blk.scalar
                def _(a):
                    for lvl in range(7):
                        nxt = (lvl + 1) % 2
                        ctx.wait(a, tag + "svd", ctx.val(tag + "svd") - 6 + lvl)
                        for it in range(2):
                            sl = slice(it * 512, (it + 1) * 512)
                            ins = a.activation(ubf[:, sl], Uf[:, sl], AF.Copy)
                        if lvl < 6:
                            ctx.wait(a, tag + "svb", ctx.val(tag + "svb") - 5 + lvl)
                            a.activation(Xs[nxt][:, :], KQps[:, :], AF.Copy)
                            ins = a.activation(Ys[nxt][:, :], Aps[:, :], AF.Copy)
                        ctx.inc(ins, tag + "svu")

            with nc.Block() as blk:
                @blk.tensor
                def _(t):
                    for it in range(2):
                        ins = t.matmul(Ops[it][:, :], Msb[:, :],
                                       ubf[:, it * 512:(it + 1) * 512],
                                       start=False, stop=True)
                    ctx.inc(ins, tag + "o2")
                @blk.vector
                def _(v):
                    ctx.wait(v, tag + "o2")
                    for it in range(2):
                        sl = slice(it * 512, (it + 1) * 512)
                        ins = v.tensor_add(tmpO[:, sl], Ops[it][:, :], bbc[:, sl])
                        edge(ctx, v, ins, tag + "c")
                        v.tensor_tensor(tmpO[:, sl], tmpO[:, sl],
                                        rX[:, c * D + it * 512:c * D + (it + 1) * 512],
                                        ALU.add)

            emit_ln_stats_apply(ctx, "ln", tmpO, rX[:, c * D:(c + 1) * D],
                                st2, EPS_LN, n_t=1)

            base_we = ctx.val(tag + "we")
            base_wp = ctx.val(tag + "wp")
            with nc.Block() as blk:
                @blk.tensor
                def _(t):
                    n = 0
                    for ec in range(DCH):
                        for it in range(2):
                            if n >= 2:
                                ctx.wait(t, tag + "we", base_we + n - 1)
                            ins = t.matmul(Sps[n % 2][:, :],
                                           ktok[:, ec * 128:(ec + 1) * 128],
                                           ubf[:, it * 512:(it + 1) * 512],
                                           start=True, stop=True)
                            ctx.inc(ins, tag + "wp")
                            n += 1
                    for it in range(2):
                        ins = t.matmul(Bps[it][0:1, :], cs["c_onesb"][:, 0:1],
                                       ubf[:, it * 512:(it + 1) * 512],
                                       start=True, stop=True)
                    ctx.inc(ins, tag + "bp")
                @blk.vector
                def _(v):
                    n = 0
                    for ec in range(DCH):
                        for it in range(2):
                            ctx.wait(v, tag + "wp", base_wp + n + 1)
                            sl = slice(ec * 1024 + it * 512, ec * 1024 + (it + 1) * 512)
                            ins = v.tensor_tensor(wreg[:, sl], wreg[:, sl],
                                                  Sps[n % 2][:, :], ALU.add)
                            edge(ctx, v, ins, tag + "c")
                            ins = v.tensor_copy(wbf[:, sl], wreg[:, sl])
                            ctx.inc(ins, tag + "we")
                            n += 1
                    ctx.wait(v, tag + "bp")
                    for it in range(2):
                        sl = slice(it * 512, (it + 1) * 512)
                        ins = v.tensor_tensor(brow[0:1, sl], brow[0:1, sl],
                                              Bps[it][0:1, :], ALU.add)
                    ctx.inc(ins, tag + "bv")
            with nc.Block() as blk:
                @blk.tensor
                def _(t):
                    ctx.wait(t, tag + "bv")
                    for it in range(2):
                        ins = t.matmul(Sps[it][:, :], cs["c_ones"][0:1, :],
                                       brow[0:1, it * 512:(it + 1) * 512],
                                       start=True, stop=True)
                    ctx.inc(ins, tag + "b2")
                @blk.vector
                def _(v):
                    ctx.wait(v, tag + "b2")
                    for it in range(2):
                        v.tensor_copy(bbc[:, it * 512:(it + 1) * 512],
                                      Sps[it][:, :])


def declare_ttt_weights(nc):
    wd = {}
    for key, src in (("thK", "L1_attn_theta_K"), ("thV", "L1_attn_theta_V"),
                     ("thQ", "L1_attn_theta_Q")):
        w = nc.dram_tensor(key, [DCH, 128, D], F32, kind="ExternalInput")
        wd[key] = [w[i] for i in range(DCH)]
    w = nc.dram_tensor("w0T", [DCH, 128, D], F32, kind="ExternalInput")
    wd["w0T"] = [w[i] for i in range(DCH)]
    wd["b0"] = nc.dram_tensor("b0", [1, D], F32, kind="ExternalInput")
    return wd


def host_ttt_weight_maps(ins):
    m = {}
    for key, src in (("thK", "L1_attn_theta_K"), ("thV", "L1_attn_theta_V"),
                     ("thQ", "L1_attn_theta_Q")):
        W = np.asarray(ins[src])
        m[key] = np.ascontiguousarray(W.T.reshape(DCH, 128, D))
    W0 = np.asarray(ins["L1_attn_W0"])
    m["w0T"] = np.ascontiguousarray(W0.T.reshape(DCH, 128, D))
    m["b0"] = np.asarray(ins["L1_attn_b0"]).reshape(1, D)
    return m


def alloc_ttt_smalls(nc, sm, ar=None):
    A = ar.take if ar else (lambda sh, dt: nc.alloc_sbuf_tensor(
        "ttt_%d_%d" % (sh[0], sh[1]), sh, dt))
    sm["ttO"] = sm["bc0"]
    sm["ttU"] = A([128, D], F32)
    sm["tt_brow"] = A([1, D], F32)
    return sm


def build_stage_d_program():
    nc = bass.Bass("TRN2", target_bir_lowering=False)
    ctx = Ctx(nc)
    x_d = nc.dram_tensor("x_in", [T, D], F32, kind="ExternalInput")
    out_d = nc.dram_tensor("x_out", [T, D], F32, kind="ExternalOutput")
    wd = declare_ttt_weights(nc)
    cs = setup_constants(ctx)
    rg = alloc_regions(nc)
    sm = alloc_smalls(nc)
    alloc_ttt_smalls(nc, sm)
    st = alloc_ln_stats(nc, sq_backing=sm["ws"][:, :2 * D])
    emit_load_x(ctx, x_d, rg["r1"])
    stage_ttt(ctx, cs, st, sm, wd, rX=rg["r1"], rFm=rg["r3"], rTH=rg["r0"],
              rvp=rg["rvp"], wreg=rg["r2"].bitcast(F32))
    emit_store_x(ctx, rg["r1"], out_d)
    nc.finalize()
    return nc



CAPSL = 20


def stage_l1_ffn(ctx, cs, st, sm, wd, rX, rLT, cta, ctb, h_sb, rvp, out_d,
                 r0=None):
    nc = ctx.nc
    ws = sm["ws"]
    sm["lcum"] = cta[0:8, 4104:4104 + T]

    with nc.Block() as blk:
        @blk.sync
        def _(s):
            ins = s.dma_start(
                wd["xg_loc"][:, :].rearrange("(tt p) d -> p tt d", p=128),
                rX[:, :].rearrange("p (tt d) -> p tt d", d=D))
            ctx.inc(ins, "e4s", 16)
        @blk.gpsimd
        def _(g):
            g.wait_ge(ctx.sem("e4s"), 16)
            ins = g.collective_compute(
                "AllGather", ALU.bypass, replica_groups=[list(range(N_CORES))],
                ins=[wd["xg_loc"].ap().opt()], outs=[wd["xg_all"].ap().opt()])
            ctx.inc(ins, "ccx")

    emit_tok2fm(ctx, "t", rX, rLT, cs)

    emit_bcast_rows_simple(ctx, "bc", [(sm["bg_row"][0:1, 0:8], sm["bg_bc"][:, 0:8])],
                           cs)
    lg_tok = sm["lg_tok"]
    with nc.Block() as blk:
        @blk.gpsimd
        def _(g):
            ins = g.dma_start(sm["gw"][:, :].rearrange("p (dc e) -> p dc e", e=8),
                              wd["Wg"][:, :].rearrange("(dc p) e -> p dc e", p=128))
            ctx.inc(ins, "e2w", 16)
    with (
        nc.psum_tensor("e2_p0", [128, 8], F32) as pg0,
        nc.psum_tensor("e2_p1", [128, 8], F32) as pg1,
        nc.Block() as blk,
    ):
        pgs = [pg0, pg1]
        @blk.tensor
        def _(t):
            ctx.wait(t, "e2w")
            for tt in range(TCH):
                if tt >= 2:
                    ctx.wait(t, "e2e", tt - 1)
                for dc in range(DCH):
                    ins = t.matmul(pgs[tt % 2][:, :],
                                   rLT[:, dc * T + tt * 128:dc * T + (tt + 1) * 128],
                                   sm["gw"][:, dc * 8:(dc + 1) * 8],
                                   start=(dc == 0), stop=(dc == DCH - 1))
                ctx.inc(ins, "e2p")
        @blk.vector
        def _(v):
            for tt in range(TCH):
                ctx.wait(v, "e2p", tt + 1)
                ins = v.tensor_add(lg_tok[:, tt * 8:(tt + 1) * 8], pgs[tt % 2][:, :],
                                   sm["bg_bc"][:, 0:8])
                ctx.inc(ins, "e2e")
    L3 = lg_tok[:, :].rearrange("p (t e) -> p t e", e=8)
    m1, m2 = sm["e_m1"], sm["e_m2"]
    eqm1, eqm2, gem = sm["e_eq1"], sm["e_eq2"], sm["e_ge"]
    w1t, w2t = sm["e_w1"], sm["e_w2"]
    with nc.Block() as blk:
        @blk.vector
        def _(v):
            ins = v.tensor_reduce(m1[:, :], L3, AX.X, ALU.max)
            edge(ctx, v, ins, "ec")
            ins = v.tensor_tensor(eqm1[:, :].rearrange("p (t e) -> p t e", e=8),
                                  L3, m1[:, :].unsqueeze(2).broadcast_to((128, 16, 8)),
                                  ALU.is_equal)
            edge(ctx, v, ins, "ec")
            ins = v.scalar_tensor_tensor(eqm2[:, :], eqm1[:, :], -1e30,
                                         lg_tok[:, :], ALU.mult, ALU.add)
            edge(ctx, v, ins, "ec")
            ins = v.tensor_reduce(m2[:, :], eqm2[:, :].rearrange(
                "p (t e) -> p t e", e=8), AX.X, ALU.max)
            edge(ctx, v, ins, "ec")
            ins = v.tensor_tensor(gem[:, :].rearrange("p (t e) -> p t e", e=8),
                                  L3, m2[:, :].unsqueeze(2).broadcast_to((128, 16, 8)),
                                  ALU.is_ge)
            edge(ctx, v, ins, "ec")
            ins = v.tensor_tensor(eqm2[:, :].rearrange("p (t e) -> p t e", e=8),
                                  L3, m2[:, :].unsqueeze(2).broadcast_to((128, 16, 8)),
                                  ALU.is_equal)
            edge(ctx, v, ins, "ec")
            ins = v.tensor_tensor(w2t[:, :], m2[:, :], m1[:, :], ALU.subtract)
            ctx.inc(ins, "ecx")
        @blk.scalar
        def _(a):
            ctx.wait(a, "ecx")
            ins = a.activation(w2t[:, :], w2t[:, :], AF.Exp)
            ctx.inc(ins, "eca")
        @blk.vector
        def _(v):
            ctx.wait(v, "eca")
            ins = v.tensor_scalar(w1t[:, :], w2t[:, :], 1.0, None, ALU.add)
            edge(ctx, v, ins, "ec")
            ins = v.reciprocal(w1t[:, :], w1t[:, :])
            edge(ctx, v, ins, "ec")
            ins = v.tensor_tensor(w2t[:, :], w2t[:, :], w1t[:, :], ALU.mult)
            edge(ctx, v, ins, "ec")

    labl = sm["labl"]
    with (
        nc.psum_tensor("e5t0", [8, 128], F32) as pt0,
        nc.psum_tensor("e5t1", [8, 128], F32) as pt1,
        nc.Block() as blk,
    ):
        pts = [pt0, pt1]
        @blk.tensor
        def _(t):
            for n in range(TCH):
                if n >= 2:
                    ctx.wait(t, "e5e", n - 1)
                ins = t.transpose(pts[n % 2][:, :], gem[:, n * 8:(n + 1) * 8],
                                  cs["c_identf"][:, :])
                ctx.inc(ins, "e5p")
        @blk.vector
        def _(v):
            for n in range(TCH):
                ctx.wait(v, "e5p", n + 1)
                ins = v.tensor_copy(labl[0:8, n * 128:(n + 1) * 128], pts[n % 2][:, :])
                ctx.inc(ins, "e5e")
    with nc.Block() as blk:
        @blk.sync
        def _(s):
            ins = s.dma_start(wd["lab_loc"][:, :], labl[0:8, :])
            ctx.inc(ins, "e3s", 16)
        @blk.gpsimd
        def _(g):
            g.wait_ge(ctx.sem("e3s"), 16)
            ins = g.collective_compute(
                "AllGather", ALU.bypass, replica_groups=[list(range(N_CORES))],
                ins=[wd["lab_loc"].ap().opt()], outs=[wd["lab_all"].ap().opt()])
            ctx.inc(ins, "ccl")
    with nc.Block() as blk:
        @blk.gpsimd
        def _(g):
            ctx.wait(g, "ccl")
            src = bass.AP(wd["lab_all"], 0, [[T, 8], [8 * T, 8], [1, T]])
            ins = g.dma_start(
                rLT[0:8, 0:N_CORES * T].rearrange("e (k t) -> e k t", t=T), src)
            ctx.inc(ins, "e5w", 16)
            ctx.wait(g, "e5w")

    strip = cta[0:8, 0:4096]
    lastc = cta[0:8, 4096:4097]
    with nc.Block() as blk:
        @blk.vector
        def _(v):
            v.memset(lastc, 0.0)
    for q in range(4):
        with nc.Block() as blk:
            @blk.vector
            def _(v):
                if q >= 1:
                    ctx.wait(v, "e6s", 16 * q)
                ins = v.tensor_tensor_scan(strip, rLT[0:8, q * 4096:(q + 1) * 4096],
                                           rLT[0:8, q * 4096:(q + 1) * 4096],
                                           lastc, ALU.add, ALU.bypass)
                edge(ctx, v, ins, "ec")
                v.tensor_copy(lastc, strip[:, 4095:4096])
        with (
            nc.psum_tensor("e6_p0q%d" % q, [1, 512], F32) as pc0,
            nc.psum_tensor("e6_p1q%d" % q, [1, 512], F32) as pc1,
            nc.Block() as blk,
        ):
            pcs = [pc0, pc1]
            base6e = ctx.val("e6e")
            base6p = ctx.val("e6p")
            base6g = ctx.val("e6g")
            @blk.sync
            def _(s):
                ins = s.dma_start(
                    wd["c_all"][:, :].rearrange("(e k) t -> e k t", k=8)
                    [:, 2 * q:2 * q + 2, :],
                    strip.rearrange("e (k t) -> e k t", t=2048))
                ctx.inc(ins, "e6s", 16)
            @blk.tensor
            def _(t):
                for j in range(8):
                    if j >= 2:
                        ctx.wait(t, "e6e", base6e + j - 1)
                    ins = t.matmul(pcs[j % 2][:, :], sm["myhot"][0:8, 0:1],
                                   strip[:, j * 512:(j + 1) * 512],
                                   start=True, stop=True)
                    ctx.inc(ins, "e6p")
            base_g0 = ctx.val("e6g0")
            base_g1 = ctx.val("e6g1")

            @blk.vector
            def _(v):
                for j in range(8):
                    ctx.wait(v, "e6p", base6p + j + 1)
                    if j >= 2:
                        ctx.wait(v, "e6g" + str(j % 2),
                                 (base_g0 if j % 2 == 0 else base_g1) + 16 * (j // 2))
                    ins = v.tensor_copy(
                        sm["crow2"][0:1, (j % 2) * 512:(j % 2 + 1) * 512],
                        pcs[j % 2][:, :])
                    ctx.inc(ins, "e6e")
            @blk.gpsimd
            def _(g):
                for j in range(8):
                    ctx.wait(g, "e6e", base6e + j + 1)
                    ins = g.dma_start(
                        bass.AP(wd["cseg_d"], q * 4096 + j * 512, [[1, 1], [1, 512]]),
                        sm["crow2"][0:1, (j % 2) * 512:(j % 2 + 1) * 512])
                    ctx.inc(ins, "e6g" + str(j % 2), 16)
                ctx.wait(g, "e6g0")
                ctx.wait(g, "e6g1")

    with nc.Block() as blk:
        @blk.gpsimd
        def _(g):
            ins = g.dma_start(sm["cseg_s"][:, :], wd["cseg_d"][:, :])
            ctx.inc(ins, "e7w", 16)
            g.iota(sm["svals"][:, :], [[0, 1]], base=0, channel_multiplier=1,
                   allow_small_or_imprecise_dtypes=True)
    with (
        nc.psum_tensor("e7_t", [1, 128], F32) as ptr,
        nc.psum_tensor("e7_b0", [128, 128], F32) as pb0,
        nc.psum_tensor("e7_b1", [128, 128], F32) as pb1,
    ):
        with nc.Block() as blk:
            @blk.tensor
            def _(t):
                ctx.wait(t, "e7w")
                ins = t.transpose(ptr[:, :], sm["cseg_s"][:, 127:128],
                                  cs["c_identf"][:, :])
                ctx.inc(ins, "e7tp")
            @blk.vector
            def _(v):
                ctx.wait(v, "e7tp")
                ins = v.tensor_copy(sm["prow"][0:1, :], ptr[:, :])
                ctx.inc(ins, "e7v")
                ctx.wait(v, "e7v")
                ins = v.tensor_copy(sm["crow"][0:1, 1:128], sm["prow"][0:1, 0:127])
                edge(ctx, v, ins, "ec")
                ins = v.memset(sm["crow"][0:1, 0:1], 0.0)
                edge(ctx, v, ins, "ec")
                ins = v.tensor_tensor(sm["crow"][0:1, 0:128], sm["prow"][0:1, :],
                                      sm["crow"][0:1, 0:128], ALU.subtract)
                ctx.inc(ins, "e7v")
        with nc.Block() as blk:
            @blk.tensor
            def _(t):
                ins = t.matmul(pb0[:, :], cs["c_ones"][0:1, :], sm["prow"][0:1, :],
                               start=True, stop=True)
                ins = t.matmul(pb1[:, :], cs["c_ones"][0:1, :],
                               sm["crow"][0:1, 0:128], start=True, stop=True)
                ctx.inc(ins, "e7t2")
            @blk.vector
            def _(v):
                ctx.wait(v, "e7t2")
                v.tensor_copy(sm["pbc"][:, :], pb0[:, :])
                v.tensor_copy(sm["cbc"][:, :], pb1[:, :])
    with nc.Block() as blk:
        @blk.vector
        def _(v):
            for s_t in range(CAPSL):
                ins = v.tensor_scalar(sm["sv2"][:, s_t:s_t + 1], sm["svals"][:, :],
                                      float(s_t * 128), None, ALU.add)
                edge(ctx, v, ins, "ec")
                ins = v.tensor_scalar(sm["msk"][:, :], sm["pbc"][:, :],
                                      sm["sv2"][:, s_t:s_t + 1], None, ALU.is_le)
                edge(ctx, v, ins, "ec")
                ins = v.tensor_reduce(sm["gval"][:, s_t:s_t + 1], sm["msk"][:, :],
                                      AX.X, ALU.add)
                edge(ctx, v, ins, "ec")
            ins = v.tensor_copy(sm["gidx"][:, :], sm["gval"][:, :])
            edge(ctx, v, ins, "ec")
            v.memset(sm["cg"][:, :], 0.0)
    for s_t in range(CAPSL):
        with nc.Block() as blk:
            @blk.gpsimd
            def _(g):
                ins = g.indirect_dma_start(
                    sm["cg"][:, :], None, wd["cseg_d"][:, :],
                    IndirectOffsetOnAxis(sm["gidx"][:, s_t:s_t + 1], 0),
                    bounds_check=ctx.breg(g, 127), oob_is_err=False)
                ctx.inc(ins, "e8g", 16)
                ctx.wait(g, "e8g")
        with nc.Block() as blk:
            @blk.vector
            def _(v):
                ins = v.tensor_scalar(sm["msk"][:, :], sm["cg"][:, :],
                                      sm["sv2"][:, s_t:s_t + 1], None, ALU.is_le)
                edge(ctx, v, ins, "ec")
                ins = v.tensor_reduce(sm["tok1"][:, :], sm["msk"][:, :], AX.X,
                                      ALU.add)
                edge(ctx, v, ins, "ec")
                ins = v.scalar_tensor_tensor(sm["tok1"][:, :],
                                             sm["gval"][:, s_t:s_t + 1], 128.0,
                                             sm["tok1"][:, :], ALU.mult, ALU.add)
                edge(ctx, v, ins, "ec")
                ins = v.tensor_copy(sm["tokidx"][:, s_t:s_t + 1], sm["tok1"][:, :])
                edge(ctx, v, ins, "ec")
                v.memset(sm["cg"][:, :], 0.0)

    with nc.Block() as blk:
        @blk.gpsimd
        def _(g):
            ctx.wait(g, "e6s")
            ins = g.indirect_dma_start(
                sm["lcum"][0:8, :], None, wd["c_all"][:, :],
                IndirectOffsetOnAxis(sm["lcum_idx"][0:8, 0:1], 0),
                bounds_check=ctx.breg(g, 63), oob_is_err=False)
            ctx.inc(ins, "e9g", 16)
            ctx.wait(g, "e9g")
    with (
        nc.psum_tensor("e9t0", [128, 8], F32) as pq0,
        nc.psum_tensor("e9t1", [128, 8], F32) as pq1,
        nc.Block() as blk,
    ):
        pqs = [pq0, pq1]
        @blk.tensor
        def _(t):
            for tt in range(TCH):
                if tt >= 2:
                    ctx.wait(t, "e9e", tt - 1)
                ins = t.transpose(pqs[tt % 2][:, :],
                                  sm["lcum"][0:8, tt * 128:(tt + 1) * 128],
                                  cs["c_identf"][0:8, 0:8])
                ctx.inc(ins, "e9p")
        @blk.vector
        def _(v):
            for tt in range(TCH):
                ctx.wait(v, "e9p", tt + 1)
                ins = v.tensor_copy(sm["seqt"][:, tt * 8:(tt + 1) * 8],
                                    pqs[tt % 2][:, :])
                ctx.inc(ins, "e9e")
    with nc.Block() as blk:
        @blk.vector
        def _(v):
            ins = v.tensor_scalar(sm["vmat"][:, :], sm["seqt"][:, :], float(8 * 320),
                                  None, ALU.is_le)
            edge(ctx, v, ins, "ec")
            ins = v.tensor_tensor(sm["vmat"][:, :], sm["vmat"][:, :], gem[:, :],
                                  ALU.mult)
            edge(ctx, v, ins, "ec")
            ins = v.tensor_scalar(sm["fmat"][:, :], sm["seqt"][:, :], -1.0, None,
                                  ALU.add)
            edge(ctx, v, ins, "ec")
            ins = v.tensor_scalar(sm["cg"][:, :], sm["fmat"][:, :], 512.0, None,
                                  ALU.is_ge)
            edge(ctx, v, ins, "ec")
            for thr in (1024.0, 1536.0, 2048.0):
                ins = v.tensor_scalar(sm["msk"][:, :], sm["fmat"][:, :], thr,
                                      None, ALU.is_ge)
                edge(ctx, v, ins, "ec")
                ins = v.tensor_tensor(sm["cg"][:, :], sm["cg"][:, :],
                                      sm["msk"][:, :], ALU.add)
                edge(ctx, v, ins, "ec")
            ins = v.scalar_tensor_tensor(sm["fmat"][:, :], sm["cg"][:, :],
                                         3584.0, sm["fmat"][:, :], ALU.mult,
                                         ALU.add)
            edge(ctx, v, ins, "ec")
            for tt in range(TCH):
                ins = v.scalar_tensor_tensor(
                    sm["fmat"][:, tt * 8:(tt + 1) * 8], cs["c_iota_e"][:, :],
                    512.0, sm["fmat"][:, tt * 8:(tt + 1) * 8], ALU.mult, ALU.add)
            edge(ctx, v, ins, "ec")
            ins = v.tensor_tensor(sm["fmat"][:, :], sm["fmat"][:, :], sm["vmat"][:, :],
                                  ALU.mult)
            edge(ctx, v, ins, "ec")
            ins = v.tensor_scalar(sm["vmat"][:, :], sm["vmat"][:, :], -20480.0, 20480.0,
                                  ALU.mult, ALU.add)
            edge(ctx, v, ins, "ec")
            ins = v.tensor_tensor(sm["fmat"][:, :], sm["fmat"][:, :], sm["vmat"][:, :],
                                  ALU.add)
            edge(ctx, v, ins, "ec")
            ins = v.tensor_tensor(sm["msk"][:, :], eqm1[:, :], sm["fmat"][:, :],
                                  ALU.mult)
            edge(ctx, v, ins, "ec")
            ins = v.tensor_reduce(sm["fl1"][:, :], sm["msk"][:, :].rearrange(
                "p (t e) -> p t e", e=8), AX.X, ALU.add)
            edge(ctx, v, ins, "ec")
            ins = v.tensor_tensor(sm["msk"][:, :], eqm2[:, :], sm["fmat"][:, :],
                                  ALU.mult)
            edge(ctx, v, ins, "ec")
            ins = v.tensor_reduce(sm["fl2"][:, :], sm["msk"][:, :].rearrange(
                "p (t e) -> p t e", e=8), AX.X, ALU.add)
            edge(ctx, v, ins, "ec")
            ins = v.tensor_copy(sm["fl1i"][:, :], sm["fl1"][:, :])
            ins = v.tensor_copy(sm["fl2i"][:, :], sm["fl2"][:, :])

    bt = [rLT[:, 0:4096], r0[:, 0:4096]]
    bf = [rLT[:, 4096:8192], rLT[:, 8192:12288]]
    y_fm = rLT[:, 12288:16384]
    y_tok = r0[:, 4096:8192]
    w3sl = [r0[:, 8192:12288], r0[:, 12288:16384]]
    w1s = ws[:, 0:2048]
    w2s = ws[:, 2048:4096]
    stmp = sm["bc0"].bitcast(BF16)[:, 0:1024]

    tp_done = {}
    ytp_done = {}
    bS = {n: ctx.val(n) for n in
          ("etpp", "etpe", "efg0", "efg1", "efw10", "efw11", "efw20", "efw21",
           "efw30", "efw31", "efs")}
    epc_t = ctx.val("epc")

    with (
        nc.psum_tensor("effh0", [128, 512], F32) as hp0,
        nc.psum_tensor("effh1", [128, 512], F32) as hp1,
        nc.psum_tensor("effh2", [128, 512], F32) as hp2,
        nc.psum_tensor("effh3", [128, 512], F32) as hp3,
        nc.psum_tensor("effy0", [128, 512], F32) as yq0,
        nc.psum_tensor("effy1", [128, 512], F32) as yq1,
        nc.psum_tensor("efftp0", [128, 128], BF16) as tpq0,
        nc.psum_tensor("efftp1", [128, 128], BF16) as tpq1,
        nc.Block() as blk,
    ):
        hpa = [hp0, hp2]
        hpb = [hp1, hp3]
        yqs = [yq0, yq1]
        tpq = [tpq0, tpq1]

        def t_pairs(ft):
            return [(bt[ft % 2][:, j * 1024 + dc * 128:j * 1024 + (dc + 1) * 128],
                     bf[ft % 2][:, dc * 512 + j * 128:dc * 512 + (j + 1) * 128])
                    for j in range(4) for dc in range(DCH)]

        def yt_pairs(ft):
            return [(y_fm[:, dc * 512 + j * 128:dc * 512 + (j + 1) * 128],
                     y_tok[:, j * 1024 + dc * 128:j * 1024 + (dc + 1) * 128])
                    for dc in range(DCH) for j in range(4)]

        @blk.tensor
        def _(t):
            tpn = [0]

            def run_tp(pairs, pre=None):
                for i, (src, dst) in enumerate(pairs):
                    n = tpn[0]
                    if i == 0 and pre:
                        for semname, val in pre:
                            ctx.wait(t, semname, bS.get(semname, 0) + val)
                    if n >= 2:
                        ctx.wait(t, "etpe", bS["etpe"] + n - 1)
                    ins = t.transpose(tpq[n % 2][:, :],
                                      src, cs["c_ident"][:, :])
                    ctx.inc(ins, "etpp")
                    tpn[0] += 1

            jh = 0
            jy = 0
            for ft in range(5):
                if ft == 0:
                    run_tp(t_pairs(0), pre=[("efg0", 64)])
                    tp_done[0] = ctx.val("etpp")
                for hc in range(32):
                    if hc == 0:
                        ctx.wait(t, "etpe", tp_done[ft] - bS["etpp"] + bS["etpe"])
                    ctx.wait(t, "efw1" + str(jh % 2),
                             bS["efw1" + str(jh % 2)] + 16 * (jh // 2 + 1))
                    ctx.wait(t, "efw2" + str(jh % 2),
                             bS["efw2" + str(jh % 2)] + 16 * (jh // 2 + 1))
                    if jh >= 2:
                        ctx.wait(t, "efhe", jh - 1)
                        ctx.wait(t, "efha", jh - 1)
                    pa, pb = hpa[jh % 2], hpb[jh % 2]
                    for dc in range(DCH):
                        t.matmul(pa[:, :],
                                 w1s[:, (jh % 2) * 1024 + dc * 128:
                                     (jh % 2) * 1024 + (dc + 1) * 128],
                                 bf[ft % 2][:, dc * 512:(dc + 1) * 512],
                                 start=(dc == 0), stop=(dc == DCH - 1))
                    for dc in range(DCH):
                        ins = t.matmul(pb[:, :],
                                       w2s[:, (jh % 2) * 1024 + dc * 128:
                                           (jh % 2) * 1024 + (dc + 1) * 128],
                                       bf[ft % 2][:, dc * 512:(dc + 1) * 512],
                                       start=(dc == 0), stop=(dc == DCH - 1))
                    ctx.inc(ins, "efhp")
                    jh += 1
                if ft < 4:
                    run_tp(t_pairs(ft + 1),
                           pre=[("efg" + str((ft + 1) % 2),
                                 64 * ((ft + 1) // 2 + 1))])
                    tp_done[ft + 1] = ctx.val("etpp")
                for dc in range(8):
                    ctx.wait(t, "efw3" + str(jy % 2),
                             bS["efw3" + str(jy % 2)] + 16 * (jy // 2 + 1))
                    if jy >= 2:
                        ctx.wait(t, "efye", jy - 1)
                    if dc == 0:
                        ctx.wait(t, "efhe", 32 * (ft + 1))
                    for hc in range(32):
                        ins = t.matmul(yqs[jy % 2][:, :],
                                       w3sl[jy % 2][:, hc * 128:(hc + 1) * 128],
                                       h_sb[:, hc * 512:(hc + 1) * 512],
                                       start=(hc == 0), stop=(hc == 31))
                    ctx.inc(ins, "efyp")
                    jy += 1
                run_tp(yt_pairs(ft), pre=[("efye", 8 * (ft + 1)),
                                          ("efs", 16 * ft)])

        @blk.vector
        def _(v):
            tpe = [0]

            def evac_tp(pairs):
                for src, dst in pairs:
                    n = tpe[0]
                    ctx.wait(v, "etpp", bS["etpp"] + n + 1)
                    ins = v.tensor_copy(dst, tpq[n % 2][:, :])
                    ctx.inc(ins, "etpe")
                    tpe[0] += 1

            jh = 0
            jy = 0
            for ft in range(5):
                if ft == 0:
                    evac_tp(t_pairs(0))
                for hc in range(32):
                    ctx.wait(v, "efha", jh + 1)
                    if hc == 0 and ft > 0:
                        ctx.wait(v, "efyp", 8 * ft)
                    ins = v.tensor_tensor(
                        h_sb[:, hc * 512:(hc + 1) * 512],
                        stmp[:, (jh % 2) * 512:(jh % 2 + 1) * 512],
                        hpb[jh % 2][:, :], ALU.mult)
                    ctx.inc(ins, "efhe")
                    jh += 1
                if ft < 4:
                    evac_tp(t_pairs(ft + 1))
                for dc in range(8):
                    ctx.wait(v, "efyp", jy + 1)
                    ins = v.tensor_copy(y_fm[:, dc * 512:(dc + 1) * 512],
                                        yqs[jy % 2][:, :])
                    ctx.inc(ins, "efye")
                    jy += 1
                evac_tp(yt_pairs(ft))
                ytp_done[ft] = ctx.val("etpe")

        @blk.scalar
        def _(a):
            ctx.wait(a, "epc", epc_t)
            for hw in range(2):
                ins = a.dma_start(w1s[:, hw * 1024:(hw + 1) * 1024],
                                  wd["W1b"][:, hw * 1024:(hw + 1) * 1024])
                ctx.inc(ins, "efw1" + str(hw % 2), 16)
            jh = 0
            for ft in range(5):
                for hc in range(32):
                    hw = jh + 2
                    if hw < 160:
                        ctx.wait(a, "efhp", hw - 1)
                        hc2 = hw % 32
                        ins = a.dma_start(
                            w1s[:, (hw % 2) * 1024:(hw % 2 + 1) * 1024],
                            wd["W1b"][:, hc2 * 1024:(hc2 + 1) * 1024])
                        ctx.inc(ins, "efw1" + str(hw % 2), 16)
                    ctx.wait(a, "efhp", jh + 1)
                    ins = a.activation(stmp[:, (jh % 2) * 512:(jh % 2 + 1) * 512],
                                       hpa[jh % 2][:, :], AF.Silu)
                    ctx.inc(ins, "efha")
                    jh += 1

        @blk.sync
        def _(s):
            ctx.wait(s, "epc", epc_t)
            for hw in range(2):
                ins = s.dma_start(w2s[:, hw * 1024:(hw + 1) * 1024],
                                  wd["W2b"][:, hw * 1024:(hw + 1) * 1024])
                ctx.inc(ins, "efw2" + str(hw % 2), 16)
            jh = 0
            for ft in range(5):
                if ft > 0:
                    ctx.wait(s, "etpe", ytp_done[ft - 1])
                    ins = s.dma_start(
                        wd["ye_loc"][(ft - 1) * 512:ft * 512, :]
                        .rearrange("(j p) d -> p j d", p=128),
                        y_tok[:, :].rearrange("p (j d) -> p j d", d=D))
                    ctx.inc(ins, "efs", 16)
                for hc in range(32):
                    hw = jh + 2
                    if hw < 160:
                        ctx.wait(s, "efhp", hw - 1)
                        hc2 = hw % 32
                        ins = s.dma_start(
                            w2s[:, (hw % 2) * 1024:(hw % 2 + 1) * 1024],
                            wd["W2b"][:, hc2 * 1024:(hc2 + 1) * 1024])
                        ctx.inc(ins, "efw2" + str(hw % 2), 16)
                    jh += 1
            ctx.wait(s, "etpe", ytp_done[4])
            ins = s.dma_start(
                wd["ye_loc"][4 * 512:5 * 512, :]
                .rearrange("(j p) d -> p j d", p=128),
                y_tok[:, :].rearrange("p (j d) -> p j d", d=D))
            ctx.inc(ins, "efs", 16)

        @blk.gpsimd
        def _(g):
            ctx.wait(g, "epc", epc_t)
            ctx.wait(g, "ccx")

            def gather(ft):
                if ft >= 2:
                    ctx.wait(g, "etpp", tp_done[ft - 2])
                for j in range(4):
                    ins = g.indirect_dma_start(
                        bt[ft % 2][:, j * 1024:(j + 1) * 1024], None,
                        wd["xg_all"][:, :],
                        IndirectOffsetOnAxis(
                            sm["tokidx"][:, ft * 4 + j:ft * 4 + j + 1], 0),
                        bounds_check=ctx.breg(g, N_CORES * T - 1),
                        oob_is_err=False)
                    ctx.inc(ins, "efg" + str(ft % 2), 16)

            def w3_loads(ft):
                for dc in range(8):
                    jy = ft * 8 + dc
                    if jy >= 2:
                        ctx.wait(g, "efyp", jy - 1)
                    ins = g.dma_start(w3sl[jy % 2][:, :],
                                      wd["W3b"][:, dc * 4096:(dc + 1) * 4096])
                    ctx.inc(ins, "efw3" + str(jy % 2), 16)

            base_efs = bS["efs"]

            def ye_ag(ft):
                ctx.wait(g, "efs", base_efs + 16 * (ft + 1))
                src = bass.AP(wd["ye_loc"], ft * 512 * D, [[1, 512 * D]])
                dst = bass.AP(wd["ye_all"], ft * N_CORES * 512 * D,
                              [[1, N_CORES * 512 * D]])
                ins = g.collective_compute(
                    "AllGather", ALU.bypass,
                    replica_groups=[list(range(N_CORES))],
                    ins=[src.opt()], outs=[dst.opt()])
                ctx.inc(ins, "ccy")

            gather(0)
            gather(1)
            gather(2)
            w3_loads(0)
            gather(3)
            w3_loads(1)
            gather(4)
            w3_loads(2)
            ye_ag(0)
            w3_loads(3)
            ye_ag(1)
            w3_loads(4)
            ye_ag(2)
            ye_ag(3)
            ye_ag(4)

    rows = [(rvp[:, 0:1024], rvp[:, 1024:2048]),
            (rvp[:, 2048:3072], rvp[:, 3072:4096])]
    ocmb3 = [rvp[:, 4096 + 2048 * i:4096 + 2048 * (i + 1)].bitcast(F32)
             for i in range(3)]
    sqs = rvp[:, 10240:12288]
    outs = [sm["bc1"], sm["bc0"]]
    with nc.Block() as blk:
        @blk.vector
        def _(v):
            ins = v.memset(st["eps"][:, :], EPS_LN)
            edge(ctx, v, ins, "ec")
            v.memset(rvp[:, 0:4096].bitcast(F32), 0.0)

    bC = {n: ctx.val(n) for n in
          ("egc", "egv", "ega", "egs", "egr", "ego", "egg0", "egg1",
           "eout0", "eout1")}
    ccy_t = ctx.val("ccy")
    with nc.Block() as blk:
        @blk.gpsimd
        def _(g):
            ctx.wait(g, "ccy", ccy_t)
            for tt in range(TCH):
                if tt >= 2:
                    ctx.wait(g, "egc", bC["egc"] + tt - 1)
                r1, r2 = rows[tt % 2]
                ins = g.indirect_dma_start(
                    r1, None, wd["ye_all"][:, :],
                    IndirectOffsetOnAxis(sm["fl1i"][:, tt:tt + 1], 0),
                    bounds_check=ctx.breg(g, N_CORES * CAP - 1), oob_is_err=False)
                ctx.inc(ins, "egg" + str(tt % 2), 16)
                ins = g.indirect_dma_start(
                    r2, None, wd["ye_all"][:, :],
                    IndirectOffsetOnAxis(sm["fl2i"][:, tt:tt + 1], 0),
                    bounds_check=ctx.breg(g, N_CORES * CAP - 1), oob_is_err=False)
                ctx.inc(ins, "egg" + str(tt % 2), 16)

        @blk.vector
        def _(v):
            for r in range(TCH + 2):
                if r < TCH:
                    tt = r
                    p = tt % 2
                    oc = ocmb3[tt % 3]
                    r1, r2 = rows[p]
                    ctx.wait(v, "egg" + str(p),
                             bC["egg" + str(p)] + 32 * (tt // 2 + 1))
                    if tt >= 3:
                        ctx.wait(v, "egs", bC["egs"] + tt - 2)
                        ctx.wait(v, "ego", bC["ego"] + tt - 2)
                    ins = v.tensor_scalar(oc[:, :], r1, w1t[:, tt:tt + 1],
                                          None, ALU.mult)
                    edge(ctx, v, ins, "ec")
                    ins = v.scalar_tensor_tensor(oc[:, :], r2, w2t[:, tt:tt + 1],
                                                 oc[:, :], ALU.mult, ALU.add)
                    edge(ctx, v, ins, "ec")
                    ins = v.tensor_tensor(oc[:, :], oc[:, :],
                                          rX[:, tt * D:(tt + 1) * D], ALU.add)
                    ctx.inc(ins, "egv")
                    ins = v.memset(r1, 0.0)
                    edge(ctx, v, ins, "ec")
                    ins = v.memset(r2, 0.0)
                    ctx.inc(ins, "egc")
                if 1 <= r <= TCH:
                    tt = r - 1
                    oc = ocmb3[tt % 3]
                    ctx.wait(v, "ega", bC["ega"] + tt + 1)
                    ins = v.tensor_reduce(st["ss"][:, tt:tt + 1],
                                          sqs[:, (tt % 2) * 1024:(tt % 2 + 1) * 1024],
                                          AX.X, ALU.add)
                    edge(ctx, v, ins, "ec")
                    ins = v.tensor_reduce(st["mean"][:, tt:tt + 1], oc[:, :],
                                          AX.X, ALU.add)
                    edge(ctx, v, ins, "ec")
                    ins = v.tensor_scalar(st["var"][:, tt:tt + 1],
                                          st["ss"][:, tt:tt + 1], 1.0 / D, None,
                                          ALU.mult)
                    edge(ctx, v, ins, "ec")
                    ins = v.tensor_scalar(st["mean"][:, tt:tt + 1],
                                          st["mean"][:, tt:tt + 1], 1.0 / D, None,
                                          ALU.mult)
                    edge(ctx, v, ins, "ec")
                    ins = v.tensor_tensor(st["msq"][:, tt:tt + 1],
                                          st["mean"][:, tt:tt + 1],
                                          st["mean"][:, tt:tt + 1], ALU.mult)
                    edge(ctx, v, ins, "ec")
                    ins = v.tensor_tensor(st["var"][:, tt:tt + 1],
                                          st["var"][:, tt:tt + 1],
                                          st["msq"][:, tt:tt + 1], ALU.subtract)
                    ctx.inc(ins, "egs")
                if r >= 2:
                    tt = r - 2
                    oc = ocmb3[tt % 3]
                    ctx.wait(v, "egr", bC["egr"] + tt + 1)
                    if tt >= 2:
                        ctx.wait(v, "eout" + str(tt % 2),
                                 bC["eout" + str(tt % 2)] + 16 * (tt // 2))
                    ins = v.tensor_scalar(outs[tt % 2][:, :], oc[:, :],
                                          st["mean"][:, tt:tt + 1],
                                          st["rstd"][:, tt:tt + 1],
                                          ALU.subtract, ALU.mult)
                    ctx.inc(ins, "ego")

        @blk.scalar
        def _(a):
            for r in range(TCH + 1):
                if r < TCH:
                    tt = r
                    ctx.wait(a, "egv", bC["egv"] + tt + 1)
                    if tt >= 2:
                        ctx.wait(a, "egs", bC["egs"] + tt - 1)
                    ins = a.activation(sqs[:, (tt % 2) * 1024:(tt % 2 + 1) * 1024],
                                       ocmb3[tt % 3][:, :], AF.Square)
                    ctx.inc(ins, "ega")
                if r >= 1:
                    tt = r - 1
                    ctx.wait(a, "egs", bC["egs"] + tt + 1)
                    ins = a.activation(st["rstd"][:, tt:tt + 1],
                                       st["var"][:, tt:tt + 1], AF.Ln,
                                       bias=st["eps"][:, :])
                    c2 = ctx.inc(ins, "eg2")
                    ctx.wait(a, "eg2", c2)
                    ins = a.activation(st["rstd"][:, tt:tt + 1],
                                       st["rstd"][:, tt:tt + 1], AF.Exp,
                                       scale=-0.5)
                    ctx.inc(ins, "egr")

        @blk.sync
        def _(s):
            for tt in range(TCH):
                ctx.wait(s, "ego", bC["ego"] + tt + 1)
                ins = s.dma_start(out_d[tt * 128:(tt + 1) * 128, :],
                                  outs[tt % 2][:, :])
                ctx.inc(ins, "eout" + str(tt % 2), 16)
            ctx.wait(s, "eout0")
            ctx.wait(s, "eout1")


def declare_l1_ffn_weights(nc):
    wd = {}
    for key, n in (("W1e", 32), ("W2e", 32)):
        w = nc.dram_tensor(key, [n, D, 128], F32, kind="ExternalInput")
        wd[key] = [w[i] for i in range(n)]
        wd[key + "_t"] = w
    w = nc.dram_tensor("W3e", [8, 32 * 128, 128], F32, kind="ExternalInput")
    wd["W3e"] = [w[i] for i in range(8)]
    wd["W3e_t"] = w
    wd["W1b"] = nc.dram_tensor("W1b", [128, 32768], BF16)
    wd["W2b"] = nc.dram_tensor("W2b", [128, 32768], BF16)
    wd["W3b"] = nc.dram_tensor("W3b", [128, 32768], BF16)
    wd["Wg"] = nc.dram_tensor("Wg", [D, 8], F32, kind="ExternalInput")
    wd["bg_row"] = nc.dram_tensor("bg_row", [1, 8], F32, kind="ExternalInput")
    wd["myhot"] = nc.dram_tensor("myhot", [8, 1], F32, kind="ExternalInput")
    wd["lcum_idx"] = nc.dram_tensor("lcum_idx", [8, 1], I32, kind="ExternalInput")
    wd["xg_loc"] = nc.dram_tensor("xg_loc", [T, D], BF16)
    wd["xg_all"] = nc.dram_tensor("xg_all", [N_CORES * T, D], BF16)
    wd["lab_loc"] = nc.dram_tensor("lab_loc", [8, T], BF16)
    wd["lab_all"] = nc.dram_tensor("lab_all", [N_CORES * 8, T], BF16)
    wd["c_all"] = nc.dram_tensor("c_all", [64, T], F32)
    wd["cseg_d"] = nc.dram_tensor("cseg_d", [128, 128], F32)
    wd["ye_loc"] = nc.dram_tensor("ye_loc", [CAP, D], BF16)
    wd["ye_all"] = nc.dram_tensor("ye_all", [N_CORES * CAP, D], BF16)
    return wd


def host_l1_ffn_weight_maps(ins, core):
    m = {}
    W1 = np.asarray(ins["L1_ffn_W1"])[core]
    m["W1e"] = np.ascontiguousarray(W1.reshape(D, 32, 128).transpose(1, 0, 2))
    W2 = np.asarray(ins["L1_ffn_W2"])[core]
    m["W2e"] = np.ascontiguousarray(W2.reshape(D, 32, 128).transpose(1, 0, 2))
    W3 = np.asarray(ins["L1_ffn_W3"])[core]
    m["W3e"] = np.ascontiguousarray(
        W3.reshape(32, 128, 8, 128).transpose(2, 0, 1, 3).reshape(8, 32 * 128, 128))
    m["Wg"] = np.asarray(ins["L1_ffn_Wg"])
    m["bg_row"] = np.asarray(ins["L1_ffn_bg"]).reshape(1, 8)
    hot = np.zeros((8, 1), np.float32)
    hot[core, 0] = 1.0
    m["myhot"] = hot
    m["lcum_idx"] = (np.arange(8, dtype=np.int32) * 8 + core).reshape(8, 1)
    return m


def emit_l1_precast(ctx, wd, g):
    for key, bkey in (("W1e_t", "W1b"), ("W2e_t", "W2b")):
        w = wd[key]
        tw = wd[bkey]
        for h0 in range(0, 32, 8):
            src = w[h0:h0 + 8].rearrange("hc (dc p) c -> (hc dc) p c", p=128)
            dst = tw[:, h0 * 1024:(h0 + 8) * 1024].rearrange(
                "p (hc dc c) -> (hc dc) p c", hc=8, c=128)
            ins = g.dma_start(dst, src)
            ctx.inc(ins, "epc", 16)
    w = wd["W3e_t"]
    tw = wd["W3b"]
    for d0 in range(0, 8, 2):
        src = w[d0:d0 + 2].rearrange("dc (hc p) c -> (dc hc) p c", p=128)
        dst = tw[:, d0 * 4096:(d0 + 2) * 4096].rearrange(
            "p (dc hc c) -> (dc hc) p c", dc=2, c=128)
        ins = g.dma_start(dst, src)
        ctx.inc(ins, "epc", 16)
    wd["_epc"] = ctx.val("epc")


def alloc_l1_smalls(nc, sm, ar=None):
    uid = [0]
    def _mk(sh, dt):
        uid[0] += 1
        return nc.alloc_sbuf_tensor("l1_%d" % uid[0], sh, dt)
    A = ar.take if ar else _mk
    sm["gw"] = A([128, 64], BF16)
    sm["bg_row"] = A([1, 8], F32)
    sm["bg_bc"] = A([128, 8], F32)
    sm["lg_tok"] = A([128, 128], F32)
    sm["e_m1"] = A([128, 16], F32)
    sm["e_m2"] = A([128, 16], F32)
    sm["e_eq1"] = A([128, 128], F32)
    sm["e_eq2"] = A([128, 128], F32)
    sm["e_ge"] = A([128, 128], F32)
    sm["e_w1"] = A([128, 16], F32)
    sm["e_w2"] = A([128, 16], F32)
    sm["labl"] = sm["ws"][0:8, 0:T]
    sm["myhot"] = A([8, 1], F32)
    sm["crow2"] = A([1, 1024], F32)
    sm["crow"] = sm["crow2"][0:1, 0:128]
    sm["prow"] = sm["crow2"][0:1, 128:256]
    sm["cseg_s"] = A([128, 128], F32)
    sm["svals"] = A([128, 1], F32)
    sm["sv2"] = A([128, CAPSL], F32)
    sm["pbc"] = A([128, 128], F32)
    sm["cbc"] = A([128, 128], F32)
    sm["msk"] = A([128, 128], F32)
    sm["gval"] = A([128, CAPSL], F32)
    sm["gidx"] = A([128, CAPSL], I32)
    sm["cg"] = A([128, 128], F32)
    sm["tok1"] = A([128, 1], F32)
    sm["tokidx"] = A([128, CAPSL], I32)
    sm["lcum_idx"] = A([8, 1], I32)
    sm["seqt"] = A([128, 128], F32)
    sm["vmat"] = A([128, 128], F32)
    sm["fmat"] = A([128, 128], F32)
    sm["fl1"] = A([128, 16], F32)
    sm["fl2"] = A([128, 16], F32)
    sm["fl1i"] = A([128, 16], I32)
    sm["fl2i"] = A([128, 16], I32)
    return sm


def emit_load_l1_smalls(ctx, sm, wd):
    emit_load_small(ctx, [
        (sm["bg_row"][:, :], wd["bg_row"][:, :]),
        (sm["myhot"][:, :], wd["myhot"][:, :]),
        (sm["lcum_idx"][:, :], wd["lcum_idx"][:, :]),
    ])


def build_stage_e_program():
    nc = bass.Bass("TRN2", target_bir_lowering=False,
                   dynamic_dma_scratch_size=26624)
    ctx = Ctx(nc)
    x_d = nc.dram_tensor("x_in", [T, D], F32, kind="ExternalInput")
    out_d = nc.dram_tensor("x_out", [T, D], F32, kind="ExternalOutput")
    wd = declare_l1_ffn_weights(nc)
    cs = setup_constants(ctx, resident_only=True)
    rg = alloc_regions(nc, names=("r0", "r1", "r2", "r3", "rvp"))
    sm = alloc_smalls(nc, skip_rows=True)
    ar = Arena(nc)
    alloc_l1_smalls(nc, sm, ar)
    emit_arena_consts(ctx, cs, ar, ("c_iota_e",))
    st = alloc_ln_stats(nc, sq_backing=sm["ws"][:, :2 * D])
    with nc.Block() as blk:
        @blk.gpsimd
        def _(g):
            emit_l1_precast(ctx, wd, g)
    emit_load_l1_smalls(ctx, sm, wd)
    emit_load_x(ctx, x_d, rg["r1"])
    stage_l1_ffn(ctx, cs, st, sm, wd, rX=rg["r1"], rLT=rg["r2"],
                 cta=rg["r3"].bitcast(F32), ctb=None,
                 h_sb=rg["rvp"][:, 0:16 * 1024], rvp=rg["rvp"],
                 out_d=out_d, r0=rg["r0"])
    nc.finalize()
    return nc



def alloc_regions(nc, names=("r0", "r1", "r2", "r3", "rvp")):
    b = {}
    for nm in names:
        if nm == "rvp":
            b[nm] = nc.alloc_sbuf_tensor(nm, [128, TCH * NH * VPW], BF16)
        else:
            b[nm] = nc.alloc_sbuf_tensor(nm, [128, TCH * D], BF16)
    return b


def alloc_smalls(nc, skip_rows=False):
    sm = {}
    sm["ws"] = nc.alloc_sbuf_tensor("ws", [128, 8192], BF16)
    sm["bc0"] = nc.alloc_sbuf_tensor("bc0", [128, D], F32)
    sm["bc1"] = nc.alloc_sbuf_tensor("bc1", [128, D], F32)
    if not skip_rows:
        sm["b3_pp"] = nc.alloc_sbuf_tensor("b3_pp_s", [128, 24], F32)
        sm["bv_row"] = nc.alloc_sbuf_tensor("bv_row_s", [1, D], BF16)
        sm["bp_row"] = nc.alloc_sbuf_tensor("bp_row_s", [1, D], BF16)
    return sm


def emit_load_x(ctx, x_d, xa):
    nc = ctx.nc
    with nc.Block() as blk:
        @blk.gpsimd
        def _(g):
            for tc in range(TCH):
                ins = g.dma_start(xa[:, tc * D:(tc + 1) * D],
                                  x_d[tc * 128:(tc + 1) * 128, :])
                ctx.inc(ins, "xld", 16)
            ctx.wait(g, "xld")


def emit_store_raw(ctx, sb, out_d, tag):
    nc = ctx.nc
    with nc.Block() as blk:
        @blk.gpsimd
        def _(g):
            ins = g.dma_start(out_d[:, :], sb[:, :])
            ctx.inc(ins, "dmp" + tag, 16)
            ctx.wait(g, "dmp" + tag)


def emit_store_x(ctx, x_tok, out_d):
    nc = ctx.nc
    with nc.Block() as blk:
        @blk.gpsimd
        def _(g):
            for tc in range(TCH):
                ins = g.dma_start(out_d[tc * 128:(tc + 1) * 128, :],
                                  x_tok[:, tc * D:(tc + 1) * D])
                ctx.inc(ins, "xst", 16)
            ctx.wait(g, "xst")


def emit_load_small(ctx, items):
    nc = ctx.nc
    with nc.Block() as blk:
        @blk.gpsimd
        def _(s):
            for dst, src in items:
                ins = s.dma_start(dst, src)
                ctx.inc(ins, "sld", 16)
            ctx.wait(s, "sld")


def declare_l0a_weights(nc):
    wd = {}
    wa = nc.dram_tensor("W_attn_ocm", [24, D, 128], F32, kind="ExternalInput")
    wd["W_attn_ocm"] = [wa[i] for i in range(24)]
    wp = nc.dram_tensor("W_proj_hm", [2, D, 512], F32, kind="ExternalInput")
    wd["W_proj_hm"] = [wp[i] for i in range(2)]
    wd["b3_pp"] = nc.dram_tensor("b3_pp", [128, 24], F32, kind="ExternalInput")
    wd["bv_row"] = nc.dram_tensor("bv_row", [1, D], F32, kind="ExternalInput")
    wd["bp_row"] = nc.dram_tensor("bp_row", [1, D], F32, kind="ExternalInput")
    return wd


def host_l0a_weight_maps(ins):
    m = {}
    W = np.asarray(ins["L0_attn_W_attn"])
    m["W_attn_ocm"] = np.ascontiguousarray(
        W.reshape(D, 24, 128).transpose(1, 0, 2))
    Wp = np.asarray(ins["L0_attn_W_proj"])
    m["W_proj_hm"] = np.ascontiguousarray(
        Wp.reshape(D, 2, 512).transpose(1, 0, 2))
    m["b3_pp"] = np.ascontiguousarray(ins["L0_attn_b_attn"].reshape(24, 128).T)
    m["bv_row"] = np.ascontiguousarray(ins["L0_attn_b_attn"][2048:].reshape(1, D))
    m["bp_row"] = np.ascontiguousarray(ins["L0_attn_b_proj"].reshape(1, D))
    return m


def build_stage_a_program(dumps=()):
    nc = bass.Bass("TRN2", target_bir_lowering=False)
    ctx = Ctx(nc)
    x_d = nc.dram_tensor("x_in", [T, D], F32, kind="ExternalInput")
    out_d = nc.dram_tensor("x_out", [T, D], F32, kind="ExternalOutput")
    wd = declare_l0a_weights(nc)
    cs = setup_constants(ctx)
    rg = alloc_regions(nc)
    sm = alloc_smalls(nc)
    st = alloc_ln_stats(nc, sq_backing=sm['ws'][:, :2 * D])
    emit_load_small(ctx, [(sm["b3_pp"][:, :], wd["b3_pp"][:, :]),
                          (sm["bv_row"][:, :], wd["bv_row"][:, :]),
                          (sm["bp_row"][:, :], wd["bp_row"][:, :])])
    emit_load_x(ctx, x_d, rg["r0"])
    dd = {}
    for name in dumps:
        w = TCH * NH * VPW if name == "vp" else (4 * TCH if name == "lnst" else TCH * D)
        dd[name] = nc.dram_tensor("dump_" + name, [128, w], F32,
                                  kind="ExternalOutput")
    stage_l0_attn(ctx, cs, st, sm, wd, x_in=rg["r0"], x_out=rg["r2"],
                  rfm=rg["r1"], rq=rg["r2"], rk=rg["r3"], rvp=rg["rvp"],
                  dd=dd, rg=rg)
    emit_store_x(ctx, rg["r2"], out_d)
    nc.finalize()
    return nc


_PROGS = {}


def _prog(name):
    if name not in _PROGS:
        _PROGS[name] = {
            "a": build_stage_a_program,
            "b": build_stage_b_program,
            "c": build_stage_c_program,
            "d": build_stage_d_program,
            "e": build_stage_e_program,
            "fused": build_fused_program,
        }[name]()
    return _PROGS[name]


def kernel(**inputs):
    return kernel_fused(**inputs)


def kernel_multi(**inputs):
    ins = inputs
    cs = host_constants()
    x = np.ascontiguousarray(np.asarray(ins["L0_attn_x"], dtype=np.float32))

    def run(name, per_core_maps):
        nc = _prog(name)
        res = run_bass_kernel_spmd(nc, per_core_maps, core_ids=list(range(N_CORES)))
        return [np.ascontiguousarray(np.asarray(r["x_out"], dtype=np.float32))
                for r in res.results]

    wa = host_l0a_weight_maps(ins)
    maps = [dict(cs, **wa, x_in=np.ascontiguousarray(x[c])) for c in range(N_CORES)]
    x1 = run("a", maps)

    wb = host_l0f_attn_weight_maps(ins)
    maps = [dict(cs, **wb, x_in=x1[c]) for c in range(N_CORES)]
    h = run("b", maps)

    wc = host_l0_moe_weight_maps(ins)
    maps = [dict(cs, **wc, h_in=h[c], x1_in=x1[c]) for c in range(N_CORES)]
    x2 = run("c", maps)

    wdt = host_ttt_weight_maps(ins)
    maps = [dict(cs, **wdt, x_in=x2[c]) for c in range(N_CORES)]
    x3 = run("d", maps)

    maps = [dict(cs, **host_l1_ffn_weight_maps(ins, c), x_in=x3[c])
            for c in range(N_CORES)]
    out = run("e", maps)
    return np.stack(out, axis=0)


def build_fused_program():
    nc = bass.Bass("TRN2", target_bir_lowering=False,
                   dynamic_dma_scratch_size=26624)
    ctx = Ctx(nc)
    x_d = nc.dram_tensor("x_in", [T, D], F32, kind="ExternalInput")
    out_d = nc.dram_tensor("x_out", [T, D], F32, kind="ExternalOutput")
    wda = declare_l0a_weights(nc)
    wdb = declare_l0f_attn_weights(nc)
    wdc = declare_l0_moe_weights(nc)
    wdd = declare_ttt_weights(nc)
    wde = declare_l1_ffn_weights(nc)
    cs = setup_constants(ctx, resident_only=True)
    rg = alloc_regions(nc)
    sm = alloc_smalls(nc, skip_rows=True)
    ar = Arena(nc)
    st = alloc_ln_stats(nc, sq_backing=sm["ws"][:, :2 * D])
    sm["b3_pp"] = ar.take([128, 24], F32)
    sm["bv_row"] = ar.take([1, D], BF16)
    sm["bp_row"] = ar.take([1, D], BF16)
    emit_arena_consts(ctx, cs, ar, ("c_mask",))
    emit_load_small(ctx, [(sm["b3_pp"][:, :], wda["b3_pp"][:, :]),
                          (sm["bv_row"][:, :], wda["bv_row"][:, :]),
                          (sm["bp_row"][:, :], wda["bp_row"][:, :])])
    emit_load_x(ctx, x_d, rg["r0"])
    stage_l0_attn(ctx, cs, st, sm, wda, x_in=rg["r0"], x_out=rg["r2"],
                  rfm=rg["r1"], rq=rg["r2"], rk=rg["r3"], rvp=rg["rvp"])

    ar.reset()
    sm["cr"] = ar.take([128, 1024], BF16)
    sm["ci"] = ar.take([128, 1024], BF16)
    sm["g1_row"] = ar.take([1, D], BF16)
    emit_arena_consts(ctx, cs, ar, ("c_mask",))
    emit_load_small(ctx, [(sm["g1_row"][:, :], wdb["g1_row"][:, :])])
    stage_l0_ffn_attn(ctx, cs, st, sm, wdb, x_in=rg["r2"], rA=rg["r0"],
                      rB=rg["r1"], rC=rg["r3"], rvp=rg["rvp"])

    ar.reset()
    alloc_moe_smalls(nc, sm, ar)
    sm["g2_row"] = ar.take([1, D], BF16)
    emit_load_small(ctx, [(sm["g2_row"][:, :], wdc["g2_row"][:, :])])
    acc32 = [rg["r0"].bitcast(F32)[:, 0:4096], rg["r0"].bitcast(F32)[:, 4096:8192]]
    stage_l0_moe(ctx, cs, st, sm, wdc, rH=rg["r0"], rX1=rg["r2"], rOut=rg["r1"],
                 rFm=rg["r3"], h_sb=rg["rvp"][:, :16 * 1024], acc32=acc32)

    ar.reset()
    alloc_ttt_smalls(nc, sm, ar)
    emit_arena_consts(ctx, cs, ar, ("c_trils", "c_trius", "c_uincl"))
    stage_ttt(ctx, cs, st, sm, wdd, rX=rg["r1"], rFm=rg["r3"], rTH=rg["r0"],
              rvp=rg["rvp"], wreg=rg["r2"].bitcast(F32),
              extra_gdma=lambda g: emit_l1_precast(ctx, wde, g))

    ar.reset()
    alloc_l1_smalls(nc, sm, ar)
    emit_arena_consts(ctx, cs, ar, ("c_iota_e",))
    emit_load_l1_smalls(ctx, sm, wde)
    stage_l1_ffn(ctx, cs, st, sm, wde, rX=rg["r1"], rLT=rg["r2"],
                 cta=rg["r3"].bitcast(F32), ctb=None,
                 h_sb=rg["rvp"][:, 0:16 * 1024], rvp=rg["rvp"], out_d=out_d,
                 r0=rg["r0"])
    nc.finalize()
    return nc


def host_full_maps(ins):
    cs = host_constants()
    base = dict(cs)
    base.update(host_l0a_weight_maps(ins))
    base.update(host_l0f_attn_weight_maps(ins))
    base.update(host_l0_moe_weight_maps(ins))
    base.update(host_ttt_weight_maps(ins))
    x = np.ascontiguousarray(np.asarray(ins["L0_attn_x"], dtype=np.float32))
    maps = []
    for c in range(N_CORES):
        m = dict(base)
        m.update(host_l1_ffn_weight_maps(ins, c))
        m["x_in"] = np.ascontiguousarray(x[c])
        maps.append(m)
    return maps


def kernel_fused(**inputs):
    nc = _prog("fused")
    maps = host_full_maps(inputs)
    res = run_bass_kernel_spmd(nc, maps, core_ids=list(range(N_CORES)))
    return np.stack([np.ascontiguousarray(np.asarray(r["x_out"], dtype=np.float32))
                     for r in res.results], axis=0)


# version 11
# speedup vs baseline: 1.0926x; 1.0092x over previous
import numpy as np
import concourse.bass as bass
import concourse.mybir as mybir
from concourse.bass_utils import run_bass_kernel_spmd
from concourse.bass_core_rust import IndirectOffsetOnAxis

F32 = mybir.dt.float32
BF16 = mybir.dt.bfloat16
I32 = mybir.dt.int32
AF = mybir.ActivationFunctionType
ALU = mybir.AluOpType
AX = mybir.AxisListType

N_CORES = 8
T = 2048
D = 1024
NH = 16
HD = 64
TCH = T // 128
DCH = D // 128
QT = 512
NQT = T // QT
EPS_LN = 1e-5
EPS_RMS = 1e-6
A_TTT = float(np.float32(0.01) * (np.float32(2.0) / np.float32(1024.0)))
EXP_SHIFT = -14.0
VPW = HD + 1
CAP = 2560


SEM_ALIAS = {
    "efw30": "ropew0", "efw31": "ropew1",
    "efg0": "e6g0", "efg1": "e6g1",
    "egg0": "mw0", "egg1": "mw1",
    "eout0": "e4s", "eout1": "e3s",
    "etpp": "trpt", "etpe": "tret",
    "egv": "lnla", "ega": "lnlv", "egs": "lnlgo", "egr": "lnlrs",
    "ego": "e7v", "egc": "e7tp", "eg2": "e7t2", "epc": "e2w",
}


class Ctx:
    def __init__(self, nc):
        self.nc = nc
        self.sems = {}
        self.cnt = {}

    def sem(self, name):
        name = SEM_ALIAS.get(name, name)
        if name not in self.sems:
            self.sems[name] = self.nc.alloc_semaphore(name)
            self.cnt[name] = 0
        return self.sems[name]

    def inc(self, ins, name, by=1):
        name = SEM_ALIAS.get(name, name)
        ins.then_inc(self.sem(name), by)
        self.cnt[name] += by
        return self.cnt[name]

    def val(self, name):
        name = SEM_ALIAS.get(name, name)
        self.sem(name)
        return self.cnt[name]

    def wait(self, eng, name, val=None):
        name = SEM_ALIAS.get(name, name)
        eng.wait_ge(self.sem(name), self.cnt[name] if val is None else val)

    def breg(self, g, val):
        if not hasattr(self, "bregs"):
            self.bregs = {}
        if val not in self.bregs:
            r = g.alloc_register("bnd%d" % val)
            g.reg_mov(r, val)
            self.bregs[val] = r
        return self.bregs[val]


def edge(ctx, eng, ins, tag=None):
    c = ctx.inc(ins, "ech")
    ctx.wait(eng, "ech", c)



class Arena:

    def __init__(self, nc, name="sma", nbytes=11712):
        self.h = nc.alloc_sbuf_tensor(name, [128, nbytes // 2], BF16)
        self.h32 = self.h.bitcast(F32)
        self.hi = self.h.bitcast(I32)
        self.off = 0

    def reset(self):
        self.off = 0

    def take(self, shape, dt):
        ib = {BF16: 2, F32: 4, I32: 4}[dt]
        self.off = (self.off + 3) & ~3
        cols = shape[1]
        start = self.off // ib
        self.off += cols * ib
        assert self.off <= self.h.shape[1] * 2, "arena overflow %d" % self.off
        src = {BF16: self.h, F32: self.h32, I32: self.hi}[dt]
        return src[0:shape[0], start:start + cols]


def host_constants():
    c = {}
    c["c_ident"] = np.eye(128, dtype=np.float32)
    c["c_identf"] = np.eye(128, dtype=np.float32)
    c["c_ones"] = np.ones((128, 128), dtype=np.float32)
    c["c_onesb"] = np.ones((128, 128), dtype=np.float32)
    m = np.zeros((128, 4, QT), dtype=np.float32)
    for j in range(4):
        for p in range(128):
            m[p, j, j * 128 + p:] = 1.0
    c["c_mask"] = m.reshape(128, 4 * QT)
    tril_s = np.tril(np.ones((128, 128), dtype=np.float32), -1) * np.float32(A_TTT)
    c["c_trils"] = tril_s
    c["c_trius"] = np.ascontiguousarray(tril_s.T)
    c["c_uincl"] = np.ascontiguousarray(np.triu(np.ones((128, 128), dtype=np.float32)))
    c["c_iota_e"] = np.broadcast_to(np.arange(8, dtype=np.float32), (128, 8)).copy()
    return c


CONST_SPECS = {
    "c_ident": ([128, 128], BF16),
    "c_identf": ([128, 128], F32),
    "c_ones": ([128, 128], F32),
    "c_onesb": ([128, 128], BF16),
    "c_mask": ([128, 4 * QT], BF16),
    "c_trils": ([128, 128], F32),
    "c_trius": ([128, 128], F32),
    "c_uincl": ([128, 128], F32),
    "c_iota_e": ([128, 8], F32),
}


RESIDENT_CONSTS = ("c_ident", "c_identf", "c_ones", "c_onesb")


def setup_constants(ctx, resident_only=False):
    nc = ctx.nc
    cs = {}
    drams = {}
    for name, (shape, dt) in CONST_SPECS.items():
        drams[name] = nc.dram_tensor(name, shape, F32, kind="ExternalInput")
        if (not resident_only) or name in RESIDENT_CONSTS:
            cs[name] = nc.alloc_sbuf_tensor(name + "_s", shape, dt)
    names = RESIDENT_CONSTS if resident_only else tuple(CONST_SPECS)
    with nc.Block() as blk:
        @blk.gpsimd
        def _(g):
            for name in names:
                ins = g.dma_start(cs[name][:, :], drams[name][:, :])
                ctx.inc(ins, "cld", 16)
            ctx.wait(g, "cld")
    cs["_drams"] = drams
    return cs


def emit_arena_consts(ctx, cs, ar, names):
    nc = ctx.nc
    for name in names:
        cs[name] = ar.take(*CONST_SPECS[name])
    with nc.Block() as blk:
        @blk.gpsimd
        def _(g):
            for name in names:
                ins = g.dma_start(cs[name][:, :], cs["_drams"][name][:, :])
                ctx.inc(ins, "cld", 16)
            ctx.wait(g, "cld")



_PSUM_UID = [0]


def emit_transposes(ctx, tag, pairs, cs, dtype=BF16):
    nc = ctx.nc
    _PSUM_UID[0] += 1
    uid = str(_PSUM_UID[0])
    with (
        nc.psum_tensor("pt0" + tag + uid, [128, 128], dtype) as p0,
        nc.psum_tensor("pt1" + tag + uid, [128, 128], dtype) as p1,
        nc.Block() as blk,
    ):
        ps = [p0, p1]

        base_p = ctx.val("trp" + tag)
        base_e = ctx.val("tre" + tag)

        @blk.tensor
        def _(t):
            for n, (src, dst) in enumerate(pairs):
                if n >= 2:
                    ctx.wait(t, "tre" + tag, base_e + n - 1)
                ins = t.transpose(ps[n % 2][:, :], src, cs["c_ident"][:, :])
                ctx.inc(ins, "trp" + tag)

        @blk.vector
        def _(v):
            for n, (src, dst) in enumerate(pairs):
                ctx.wait(v, "trp" + tag, base_p + n + 1)
                ins = v.tensor_copy(dst, ps[n % 2][:, :])
                ctx.inc(ins, "tre" + tag)


def attn_scratch(sm):
    return {"E": sm["ws"][:, 0:2048],
            "shift": sm["bc0"][:, 768:769],
            "rcp": sm["bc0"][:, 0:512],
            "bcs": sm["bc0"].bitcast(BF16)[0:64, 1024:1536]}


def emit_tok2fm(ctx, tag, x_tok, x_fm, cs):
    pairs = []
    for i in range(TCH):
        for k in range(DCH):
            pairs.append((x_tok[:, i * D + k * 128:i * D + (k + 1) * 128],
                          x_fm[:, k * T + i * 128:k * T + (i + 1) * 128]))
    emit_transposes(ctx, tag, pairs, cs)


def emit_matmul_tiles(ctx, tag, jobs, psums, evac, dmas=None, job_wreq=None,
                      dma_pereq=None):
    nc = ctx.nc
    nb = len(psums)
    base_p = ctx.val(tag + "p")
    base_e = ctx.val(tag + "e")
    with nc.Block() as blk:
        if dmas:
            @blk.gpsimd
            def _(g):
                for i, (dst, src, semname) in enumerate(dmas):
                    if dma_pereq and dma_pereq[i] > 0:
                        ctx.wait(g, tag + "p", base_p + dma_pereq[i])
                    ins = g.dma_start(dst, src)
                    ctx.inc(ins, semname, 16)

        @blk.tensor
        def _(t):
            for n, job in enumerate(jobs):
                if job_wreq and job_wreq[n]:
                    for semname, val in job_wreq[n]:
                        ctx.wait(t, semname, val)
                if n >= nb:
                    ctx.wait(t, tag + "e", base_e + n - nb + 1)
                ps = psums[n % nb]
                na = len(job)
                for a, (lh, rh) in enumerate(job):
                    ins = t.matmul(ps, lh, rh, start=(a == 0), stop=(a == na - 1))
                ctx.inc(ins, tag + "p")

        @blk.vector
        def _(v):
            for n, job in enumerate(jobs):
                ctx.wait(v, tag + "p", base_p + n + 1)
                ins = evac(n, psums[n % nb], v)
                ctx.inc(ins, tag + "e")


def emit_ln_stats_apply(ctx, tag, src_tok, dst_tok, st, eps_val, rms=False,
                        gamma_bc=None, dst_f32_tiles=None, n_t=TCH, dbg=None):
    nc = ctx.nc
    base_la = ctx.val(tag + "la")
    base_lv = ctx.val(tag + "lv")
    with nc.Block() as blk:
        @blk.scalar
        def _(a):
            for tt in range(n_t):
                if tt >= 2:
                    ctx.wait(a, tag + "lv", base_lv + tt - 1)
                ins = a.activation(st["sq"][:, (tt % 2) * D:(tt % 2 + 1) * D],
                                   src_tok[:, tt * D:(tt + 1) * D], AF.Square)
                ctx.inc(ins, tag + "la")

        @blk.vector
        def _(v):
            v.memset(st["eps"][:, :], eps_val)
            for tt in range(n_t):
                ctx.wait(v, tag + "la", base_la + tt + 1)
                ins = v.tensor_reduce(st["ss"][:, tt:tt + 1],
                                      st["sq"][:, (tt % 2) * D:(tt % 2 + 1) * D],
                                      AX.X, ALU.add)
                ctx.inc(ins, tag + "lv")
                if not rms:
                    v.tensor_reduce(st["mean"][:, tt:tt + 1],
                                    src_tok[:, tt * D:(tt + 1) * D], AX.X, ALU.add)
    with nc.Block() as blk:
        @blk.vector
        def _(v):
            v.tensor_scalar(st["var"][:, :n_t], st["ss"][:, :n_t], 1.0 / D, None,
                            ALU.mult)
            if not rms:
                ins = v.tensor_scalar(st["mean"][:, :n_t], st["mean"][:, :n_t],
                                      1.0 / D, None, ALU.mult)
                edge(ctx, v, ins, tag + "c1")
                ins = v.tensor_tensor(st["msq"][:, :n_t], st["mean"][:, :n_t],
                                      st["mean"][:, :n_t], ALU.mult)
                edge(ctx, v, ins, tag + "c2")
                v.tensor_tensor(st["var"][:, :n_t], st["var"][:, :n_t],
                                st["msq"][:, :n_t], ALU.subtract)
            ins = v.engine_nop()
            ctx.inc(ins, tag + "lgo")
        @blk.scalar
        def _(a):
            ctx.wait(a, tag + "lgo")
            ins = a.activation(st["rstd"][:, :n_t], st["var"][:, :n_t], AF.Ln,
                               bias=st["eps"][:, :])
            edge(ctx, a, ins)
            ins = a.activation(st["rstd"][:, :n_t], st["rstd"][:, :n_t], AF.Exp,
                               scale=-0.5)
            ctx.inc(ins, tag + "lrs")
    with nc.Block() as blk:
        @blk.vector
        def _(v):
            ctx.wait(v, tag + "lrs")
            for tt in range(n_t):
                src = src_tok[:, tt * D:(tt + 1) * D]
                dst = (dst_f32_tiles(tt) if dst_f32_tiles is not None
                       else dst_tok[:, tt * D:(tt + 1) * D])
                if rms:
                    ins = v.tensor_scalar(dst, src, st["rstd"][:, tt:tt + 1], None,
                                          ALU.mult)
                else:
                    ins = v.tensor_scalar(dst, src, st["mean"][:, tt:tt + 1],
                                          st["rstd"][:, tt:tt + 1], ALU.subtract,
                                          ALU.mult)
                if gamma_bc is not None:
                    edge(ctx, v, ins, tag + "c3")
                    v.tensor_tensor(dst, dst, gamma_bc[:, :], ALU.mult)


def alloc_ln_stats(nc, sq_backing=None):
    st = {}
    for nm in ("ss", "mean", "var", "rstd", "msq"):
        st[nm] = nc.alloc_sbuf_tensor("lnst_" + nm, [128, TCH], F32)
    st["eps"] = nc.alloc_sbuf_tensor("lnst_eps", [128, 1], F32)
    if sq_backing is not None:
        st["sq"] = sq_backing
    else:
        st["sq"] = nc.alloc_sbuf_tensor("lnst_sq", [128, 2 * D], BF16)
    return st


def emit_bcast_rows_simple(ctx, tag, items, cs):
    nc = ctx.nc
    _PSUM_UID[0] += 1
    tag2 = tag + "u" + str(_PSUM_UID[0])
    with (
        nc.psum_tensor("bc0" + tag2, [128, 512], F32) as p0,
        nc.psum_tensor("bc1" + tag2, [128, 512], F32) as p1,
        nc.Block() as blk,
    ):
        ps = [p0, p1]

        base_bp = ctx.val(tag + "bp")
        base_be = ctx.val(tag + "be")

        @blk.tensor
        def _(t):
            for n, (row, dst) in enumerate(items):
                w = row.shape[-1] if hasattr(row, "shape") else 512
                if n >= 2:
                    ctx.wait(t, tag + "be", base_be + n - 1)
                ins = t.matmul(ps[n % 2][:, 0:dst.shape[-1]],
                               cs["c_onesb" if row.dtype == BF16 else "c_ones"][0:1, :],
                               row, start=True, stop=True)
                ctx.inc(ins, tag + "bp")

        @blk.vector
        def _(v):
            for n, (row, dst) in enumerate(items):
                ctx.wait(v, tag + "bp", base_bp + n + 1)
                ins = v.tensor_copy(dst, ps[n % 2][:, 0:dst.shape[-1]])
                ctx.inc(ins, tag + "be")



def emit_attention(ctx, tag, qf, kf, vp, yf, cs, scr, scale=0.125):
    nc = ctx.nc
    tiles = []
    groups = []
    n = 0
    for h in range(NH):
        for qt in range(NQT):
            nk = 4 * qt + 4
            groups.append((h, qt, nk, n))
            for kc in range(nk):
                tiles.append((h, qt, kc, nk))
                n += 1
    NT = len(tiles)

    def npartial(idx):
        c = 0
        for i in range(idx + 1):
            h, qt, kc, nk = tiles[i]
            if kc >= nk - 4:
                c += 1
        return c

    pref = []
    c = 0
    for i in range(NT):
        h, qt, kc, nk = tiles[i]
        if kc >= nk - 4:
            c += 1
        pref.append(c)
    group_of_tile = []
    for gi, (h, qt, nk, f) in enumerate(groups):
        group_of_tile += [gi] * nk
    islast = [tiles[i][2] == tiles[i][3] - 1 for i in range(NT)]
    nlast = []
    c2 = 0
    for i in range(NT):
        if islast[i]:
            c2 += 1
        nlast.append(c2)

    E = scr["E"]
    shift = scr["shift"]
    rcp = scr["rcp"]
    bcs = scr["bcs"]
    _PSUM_UID[0] += 1
    tag_u = tag + "u" + str(_PSUM_UID[0])
    bases = {k: ctx.val(k + tag) for k in
             ("ap", "aa", "am", "ad", "aq", "ar", "ab", "av", "ae")}
    with (
        nc.psum_tensor("sc0" + tag_u, [128, QT], F32) as sc0,
        nc.psum_tensor("sc1" + tag_u, [128, QT], F32) as sc1,
        nc.psum_tensor("sc2" + tag_u, [128, QT], F32) as sc2,
        nc.psum_tensor("sc3" + tag_u, [128, QT], F32) as sc3,
        nc.psum_tensor("yp0" + tag_u, [VPW, QT], F32) as yp0,
        nc.psum_tensor("yp1" + tag_u, [VPW, QT], F32) as yp1,
        nc.psum_tensor("bcp" + tag_u, [64, QT], F32) as bcp,
    ):
        with nc.Block() as blk0:
            @blk0.vector
            def _(v):
                v.memset(shift[:, :], EXP_SHIFT)
        scs = [sc0, sc1, sc2, sc3]
        yps = [yp0, yp1]

        def eslice(i):
            return E[:, (i % 4) * QT:(i % 4 + 1) * QT]

        with nc.Block() as blk:
            @blk.tensor
            def _(t):
                def issue_score(i):
                    h, qt, kc, nk = tiles[i]
                    ch, rw = h // 2, (h % 2) * 64
                    if i >= 4:
                        ctx.wait(t, "aa" + tag, bases["aa"] + i - 3)
                    ins = t.matmul(scs[i % 4][:, :],
                                   kf[rw:rw + 64, ch * T + kc * 128:ch * T + (kc + 1) * 128],
                                   qf[rw:rw + 64, ch * T + qt * QT:ch * T + (qt + 1) * QT],
                                   start=True, stop=True)
                    ctx.inc(ins, "ap" + tag)

                def issue_av(i):
                    h, qt, kc, nk = tiles[i]
                    g = group_of_tile[i]
                    ctx.wait(t, "aa" + tag, bases["aa"] + i + 1)
                    if pref[i]:
                        ctx.wait(t, "am" + tag, bases["am"] + pref[i])
                    if kc == 0 and g >= 2:
                        ctx.wait(t, "ae" + tag, bases["ae"] + g - 1)
                    ins = t.matmul(yps[g % 2][:, :],
                                   vp[:, kc * (NH * VPW) + h * VPW:
                                      kc * (NH * VPW) + (h + 1) * VPW],
                                   eslice(i),
                                   start=(kc == 0), stop=(kc == nk - 1))
                    if kc == nk - 1:
                        ctx.inc(ins, "aq" + tag)
                    else:
                        ctx.inc(ins, "ad" + tag)

                def issue_bcast(g):
                    ctx.wait(t, "ar" + tag, bases["ar"] + g + 1)
                    if g >= 1:
                        ctx.wait(t, "av" + tag, bases["av"] + g)
                    ins = t.matmul(bcp[:, :], cs["c_ones"][64:65, 0:64],
                                   rcp[64:65, :], start=True, stop=True)
                    ctx.inc(ins, "ab" + tag)

                bcast_at = {}
                for gi in range(len(groups) - 1):
                    last_g = groups[gi + 1][3] - 1
                    defer = min(3, groups[gi + 1][2] - 4)
                    bcast_at.setdefault(last_g + defer, []).append(gi)

                def after_av(j):
                    for gi in bcast_at.get(j, []):
                        issue_bcast(gi)

                for i in range(NT):
                    issue_score(i)
                    if i >= 2:
                        issue_av(i - 2)
                        after_av(i - 2)
                issue_av(NT - 2)
                after_av(NT - 2)
                issue_av(NT - 1)
                after_av(NT - 1)
                issue_bcast(len(groups) - 1)

            @blk.scalar
            def _(a):
                for i in range(NT):
                    ctx.wait(a, "ap" + tag, bases["ap"] + i + 1)
                    if i >= 4:
                        nl = nlast[i - 4]
                        nn = (i - 3) - nl
                        if nn:
                            ctx.wait(a, "ad" + tag, bases["ad"] + nn)
                        if nl:
                            ctx.wait(a, "aq" + tag, bases["aq"] + nl)
                        if pref[i - 4]:
                            ctx.wait(a, "am" + tag, bases["am"] + pref[i - 4])
                    ins = a.activation(eslice(i), scs[i % 4][:, :], AF.Exp,
                                       bias=shift[:, :], scale=scale)
                    ctx.inc(ins, "aa" + tag)

            @blk.vector
            def _(v):
                for i in range(NT):
                    h, qt, kc, nk = tiles[i]
                    if kc >= nk - 4:
                        ctx.wait(v, "aa" + tag, bases["aa"] + i + 1)
                        j = kc - (nk - 4)
                        ins = v.tensor_tensor(eslice(i), eslice(i),
                                              cs["c_mask"][:, j * QT:(j + 1) * QT],
                                              ALU.mult)
                        ctx.inc(ins, "am" + tag)
                    if kc == nk - 1:
                        g = group_of_tile[i]
                        h2, qt2 = h, qt
                        ch, rw = h2 // 2, (h2 % 2) * 64
                        ctx.wait(v, "aq" + tag, bases["aq"] + g + 1)
                        ins = v.reciprocal(rcp[64:65, :], yps[g % 2][64:65, :])
                        ctx.inc(ins, "ar" + tag)
                        ctx.wait(v, "ab" + tag, bases["ab"] + g + 1)
                        ins = v.tensor_copy(bcs[:, :], bcp[:, :])
                        ctx.inc(ins, "av" + tag)
                        ctx.wait(v, "av" + tag)
                        ins = v.tensor_tensor(
                            yf[rw:rw + 64, ch * T + qt2 * QT:ch * T + (qt2 + 1) * QT],
                            yps[g % 2][0:64, :], bcs[:, :], ALU.mult)
                        ctx.inc(ins, "ae" + tag)



def stage_l0_attn(ctx, cs, st, sm, wd, x_in, x_out, rfm, rq, rk, rvp,
                  dd=None, rg=None):
    nc = ctx.nc
    dd = dd or {}
    ws = sm["ws"]
    if "xa" in dd:
        emit_store_raw(ctx, x_in, dd["xa"], "xa")
    emit_tok2fm(ctx, "t", x_in, rfm, cs)
    if "xfm" in dd:
        emit_store_raw(ctx, rfm, dd["xfm"], "xfm")

    emit_bcast_rows_simple(ctx, "bc", [
        (sm["bv_row"][0:1, j * 512:(j + 1) * 512],
         sm["bc0"][:, j * 512:(j + 1) * 512]) for j in range(2)
    ], cs)

    with (
        nc.psum_tensor("l0a_p0", [128, QT], F32) as p0,
        nc.psum_tensor("l0a_p1", [128, QT], F32) as p1,
        nc.psum_tensor("l0a_p2", [128, QT], F32) as p2,
        nc.psum_tensor("l0a_p3", [128, QT], F32) as p3,
    ):
        jobs, meta, wreq = [], [], []
        dmas, dpereq = [], []
        bw = [ctx.val("mw0"), ctx.val("mw1")]
        for oc in range(16):
            dmas.append((ws[:, (oc % 2) * 1024:(oc % 2) * 1024 + 1024]
                         .rearrange("p (dc c) -> p dc c", c=128),
                         wd["W_attn_ocm"][oc].rearrange("(dc p) c -> p dc c", p=128),
                         "mw" + str(oc % 2)))
            dpereq.append(max(0, (oc - 1) * NQT))
            for tt in range(NQT):
                jobs.append([(ws[:, (oc % 2) * 1024 + dc * 128:(oc % 2) * 1024 + (dc + 1) * 128],
                              rfm[:, dc * T + tt * QT:dc * T + (tt + 1) * QT])
                             for dc in range(DCH)])
                meta.append((oc, tt))
                wreq.append([("mw" + str(oc % 2), bw[oc % 2] + 16 * (oc // 2 + 1))])

        def qk_evac(n, ps, v):
            oc, tt = meta[n]
            dst = rq if oc < 8 else rk
            c = oc % 8
            return v.tensor_scalar(dst[:, c * T + tt * QT:c * T + (tt + 1) * QT],
                                   ps, sm["b3_pp"][:, oc:oc + 1], None, ALU.add)

        emit_matmul_tiles(ctx, "mm", jobs, [p0[:, :], p1[:, :], p2[:, :], p3[:, :]],
                          qk_evac, dmas=dmas, job_wreq=wreq, dma_pereq=dpereq)

        bw = [ctx.val("mw0"), ctx.val("mw1")]
        with nc.Block() as blk:
            @blk.vector
            def _(v):
                v.memset(rvp[:, :TCH * NH * VPW], 0.0)
            @blk.gpsimd
            def _(g):
                for half in range(2):
                    for j in range(4):
                        oc = 16 + half * 4 + j
                        dst = ws[:, half * 4096:(half + 1) * 4096].rearrange(
                            "p (dc j c) -> p dc j c", j=4, c=128)[:, :, j, :]
                        ins = g.dma_start(
                            dst,
                            wd["W_attn_ocm"][oc].rearrange("(dc p) c -> p dc c", p=128))
                        ctx.inc(ins, "mw" + str(half), 16)

        vjobs, vmeta, vwreq = [], [], []
        for half in range(2):
            for tc in range(TCH):
                vjobs.append([(rfm[:, dc * T + tc * 128:dc * T + (tc + 1) * 128],
                               ws[:, half * 4096 + dc * 512:half * 4096 + (dc + 1) * 512])
                              for dc in range(DCH)])
                vmeta.append((half, tc))
                vwreq.append([("mw" + str(half), bw[half] + 64)])

        def v_evac(n, ps, v):
            half, tc = vmeta[n]
            base = tc * (NH * VPW) + half * 8 * VPW
            return v.tensor_add(
                rvp[:, base:base + 8 * VPW].rearrange("p (h w) -> p h w", w=VPW)[:, :, 0:HD],
                ps.rearrange("p (h w) -> p h w", w=HD),
                sm["bc0"][:, half * 512:(half + 1) * 512].rearrange(
                    "p (h w) -> p h w", w=HD))

        emit_matmul_tiles(ctx, "mm", vjobs, [p0[:, :], p1[:, :], p2[:, :], p3[:, :]],
                          v_evac, job_wreq=vwreq)
        with nc.Block() as blk:
            @blk.vector
            def _(v):
                v.memset(rvp[:, :TCH * NH * VPW].rearrange(
                    "p (t h w) -> p t h w", h=NH, w=VPW)[:, :, :, HD:HD + 1], 1.0)

    if "qf" in dd:
        emit_store_raw(ctx, rq, dd["qf"], "qf")
    if "kf" in dd:
        emit_store_raw(ctx, rk, dd["kf"], "kf")
    if "vp" in dd:
        emit_store_raw(ctx, rvp, dd["vp"], "vp")
    emit_attention(ctx, "at", rq, rk, rvp, rfm, cs, attn_scratch(sm))
    if "yf" in dd:
        emit_store_raw(ctx, rfm, dd["yf"], "yf")

    emit_bcast_rows_simple(ctx, "bc", [
        (sm["bp_row"][0:1, j * 512:(j + 1) * 512],
         sm["bc1"][:, j * 512:(j + 1) * 512]) for j in range(2)
    ], cs)

    bw = [ctx.val("mw0"), ctx.val("mw1")]
    with nc.Block() as blk:
        @blk.gpsimd
        def _(g):
            for half in range(2):
                ins = g.dma_start(
                    ws[:, half * 4096:(half + 1) * 4096]
                    .rearrange("p (dc c) -> p dc c", c=512),
                    wd["W_proj_hm"][half].rearrange("(dc p) c -> p dc c", p=128))
                ctx.inc(ins, "mw" + str(half), 16)

    zjobs, zmeta, zwreq = [], [], []
    for half in range(2):
        for tc in range(TCH):
            zjobs.append([(rfm[:, dc * T + tc * 128:dc * T + (tc + 1) * 128],
                           ws[:, half * 4096 + dc * 512:half * 4096 + (dc + 1) * 512])
                          for dc in range(DCH)])
            zmeta.append((half, tc))
            zwreq.append([("mw" + str(half), bw[half] + 16)])

    def z_evac(n, ps, v):
        half, tc = zmeta[n]
        sl = slice(tc * D + half * 512, tc * D + (half + 1) * 512)
        ins = v.tensor_add(x_out[:, sl], ps, sm["bc1"][:, half * 512:(half + 1) * 512])
        edge(ctx, v, ins, "l0azc")
        return v.tensor_tensor(x_out[:, sl], x_out[:, sl], x_in[:, sl], ALU.add)

    with (
        nc.psum_tensor("l0a_z0", [128, QT], F32) as z0,
        nc.psum_tensor("l0a_z1", [128, QT], F32) as z1,
        nc.psum_tensor("l0a_z2", [128, QT], F32) as z2,
        nc.psum_tensor("l0a_z3", [128, QT], F32) as z3,
    ):
        emit_matmul_tiles(ctx, "mm", zjobs, [z0[:, :], z1[:, :], z2[:, :], z3[:, :]],
                          z_evac, job_wreq=zwreq)

    if "xpre" in dd:
        emit_store_raw(ctx, x_out, dd["xpre"], "xpre")
    dbg = None
    if "lnst" in dd:
        dbg = {k: dd["lnst"][:, i * TCH:(i + 1) * TCH]
               for i, k in enumerate(("ss", "mean", "var", "rstd"))}
    emit_ln_stats_apply(ctx, "ln", x_out, x_out, st, EPS_LN, dbg=dbg)



def emit_proj_tok(ctx, tag, src_fm, wname_dram, dst_cb, psums, wsem_offset=0):
    pass


def stage_l0_ffn_attn(ctx, cs, st, sm, wd, x_in, rA, rB, rC, rvp):
    nc = ctx.nc
    ws = sm["ws"]
    emit_bcast_rows_simple(ctx, "bc", [
        (sm["g1_row"][0:1, j * 512:(j + 1) * 512],
         sm["bc0"][:, j * 512:(j + 1) * 512]) for j in range(2)
    ], cs)
    emit_ln_stats_apply(ctx, "ln", x_in, rA, st, EPS_RMS, rms=True,
                        gamma_bc=sm["bc0"])
    emit_tok2fm(ctx, "t", rA, rB, cs)

    def tok_proj(tag, wkey, dst_or_cb):
        jobs, meta, wreq = [], [], []
        bw = [ctx.val("mw0"), ctx.val("mw1")]
        dmas = [(ws[:, half * 4096:(half + 1) * 4096]
                 .rearrange("p (dc c) -> p dc c", c=512),
                 wd[wkey][half].rearrange("(dc p) c -> p dc c", p=128),
                 "mw" + str(half))
                for half in range(2)]
        for half in range(2):
            for tc in range(TCH):
                jobs.append([(rB[:, dc * T + tc * 128:dc * T + (tc + 1) * 128],
                              ws[:, half * 4096 + dc * 512:half * 4096 + (dc + 1) * 512])
                             for dc in range(DCH)])
                meta.append((half, tc))
                wreq.append([("mw" + str(half), bw[half] + 16)])

        def evac(n, ps, v):
            half, tc = meta[n]
            if callable(dst_or_cb):
                return dst_or_cb(half, tc, ps, v)
            return v.tensor_copy(
                dst_or_cb[:, tc * D + half * 512:tc * D + (half + 1) * 512], ps)

        with (
            nc.psum_tensor(tag + "_p0", [128, QT], F32) as p0,
            nc.psum_tensor(tag + "_p1", [128, QT], F32) as p1,
            nc.psum_tensor(tag + "_p2", [128, QT], F32) as p2,
            nc.psum_tensor(tag + "_p3", [128, QT], F32) as p3,
        ):
            emit_matmul_tiles(ctx, "mm", jobs,
                              [p0[:, :], p1[:, :], p2[:, :], p3[:, :]], evac,
                              dmas=dmas, job_wreq=wreq)

    tok_proj("l0fq", "l0f_wq", rA)
    tok_proj("l0fk", "l0f_wk", rC)
    with nc.Block() as blk:
        @blk.vector
        def _(v):
            v.memset(rvp[:, :TCH * NH * VPW], 0.0)

    def v_evac(half, tc, ps, v):
        base = tc * (NH * VPW) + half * 8 * VPW
        return v.tensor_copy(
            rvp[:, base:base + 8 * VPW].rearrange("p (h w) -> p h w", w=VPW)[:, :, 0:HD],
            ps.rearrange("p (h w) -> p h w", w=HD))

    tok_proj("l0fv", "l0f_wv", v_evac)
    with nc.Block() as blk:
        @blk.vector
        def _(v):
            v.memset(rvp[:, :TCH * NH * VPW].rearrange(
                "p (t h w) -> p t h w", h=NH, w=VPW)[:, :, :, HD:HD + 1], 1.0)

    with nc.Block() as blk:
        @blk.gpsimd
        def _(g):
            for tt in range(TCH):
                if tt >= 2:
                    ctx.wait(g, "ropev", (tt - 1) * 2)
                ins = g.dma_start(sm["cr"][:, (tt % 2) * 512:(tt % 2 + 1) * 512],
                                  wd["cr"][tt * 128:(tt + 1) * 128, :])
                ctx.inc(ins, "ropew" + str(tt % 2), 16)
                ins = g.dma_start(sm["ci"][:, (tt % 2) * 512:(tt % 2 + 1) * 512],
                                  wd["ci"][tt * 128:(tt + 1) * 128, :])
                ctx.inc(ins, "ropew" + str(tt % 2), 16)

        @blk.vector
        def _(v):
            t1 = sm["bc0"][:, 0:512]
            t2 = sm["bc0"][:, 512:1024]
            t3 = sm["bc1"][:, 0:512]
            t4 = sm["bc1"][:, 512:1024]
            for tt in range(TCH):
                ctx.wait(v, "ropew" + str(tt % 2), 32 * (tt // 2 + 1))
                cr = sm["cr"][:, (tt % 2) * 512:(tt % 2 + 1) * 512]
                ci = sm["ci"][:, (tt % 2) * 512:(tt % 2 + 1) * 512]
                for reg in (rA, rC):
                    tile = reg[:, tt * D:(tt + 1) * D].rearrange(
                        "p (hi two) -> p hi two", two=2)
                    EV = tile[:, :, 0]
                    OD = tile[:, :, 1]
                    v.tensor_tensor(t1, EV, cr, ALU.mult)
                    v.tensor_tensor(t2, OD, ci, ALU.mult)
                    v.tensor_tensor(t3, OD, cr, ALU.mult)
                    ins = v.tensor_tensor(t4, EV, ci, ALU.mult)
                    edge(ctx, v, ins, "ropec")
                    v.tensor_tensor(EV, t1, t2, ALU.subtract)
                    ins = v.tensor_tensor(OD, t3, t4, ALU.add)
                    edge(ctx, v, ins)
                ins = v.engine_nop()
                ctx.inc(ins, "ropev", 2)

    emit_tok2fm(ctx, "t", rA, rB, cs)
    emit_tok2fm(ctx, "t", rC, rA, cs)

    emit_attention(ctx, "at", rB, rA, rvp, rC, cs, attn_scratch(sm))

    def h_evac(half, tc, ps, v):
        sl = slice(tc * D + half * 512, tc * D + (half + 1) * 512)
        return v.tensor_add(rA[:, sl], ps, x_in[:, sl])

    jobs, meta, wreq = [], [], []
    bw = [ctx.val("mw0"), ctx.val("mw1")]
    dmas = [(ws[:, half * 4096:(half + 1) * 4096]
             .rearrange("p (dc c) -> p dc c", c=512),
             wd["l0f_wo"][half].rearrange("(dc p) c -> p dc c", p=128),
             "mw" + str(half))
            for half in range(2)]
    for half in range(2):
        for tc in range(TCH):
            jobs.append([(rC[:, dc * T + tc * 128:dc * T + (tc + 1) * 128],
                          ws[:, half * 4096 + dc * 512:half * 4096 + (dc + 1) * 512])
                         for dc in range(DCH)])
            meta.append((half, tc))
            wreq.append([("mw" + str(half), bw[half] + 16)])

    def evac(n, ps, v):
        half, tc = meta[n]
        return h_evac(half, tc, ps, v)

    with (
        nc.psum_tensor("l0fo_p0", [128, QT], F32) as p0,
        nc.psum_tensor("l0fo_p1", [128, QT], F32) as p1,
        nc.psum_tensor("l0fo_p2", [128, QT], F32) as p2,
        nc.psum_tensor("l0fo_p3", [128, QT], F32) as p3,
    ):
        emit_matmul_tiles(ctx, "mm", jobs,
                          [p0[:, :], p1[:, :], p2[:, :], p3[:, :]], evac,
                          dmas=dmas, job_wreq=wreq)


def declare_l0f_attn_weights(nc):
    wd = {}
    for wname in ("wq", "wk", "wv", "wo"):
        w = nc.dram_tensor("l0f_" + wname, [2, D, 512], F32, kind="ExternalInput")
        wd["l0f_" + wname] = [w[i] for i in range(2)]
    wd["cr"] = nc.dram_tensor("cr", [T, 512], F32, kind="ExternalInput")
    wd["ci"] = nc.dram_tensor("ci", [T, 512], F32, kind="ExternalInput")
    wd["g1_row"] = nc.dram_tensor("g1_row", [1, D], F32, kind="ExternalInput")
    return wd


def host_l0f_attn_weight_maps(ins):
    m = {}
    for wname in ("wq", "wk", "wv", "wo"):
        W = np.asarray(ins["L0_ffn_" + wname])
        m["l0f_" + wname] = np.ascontiguousarray(
            W.reshape(D, 2, 512).transpose(1, 0, 2))
    fr = np.asarray(ins["L0_ffn_freqs_cis"]).reshape(T, 32, 2)
    m["cr"] = np.ascontiguousarray(np.tile(fr[:, :, 0], (1, NH)))
    m["ci"] = np.ascontiguousarray(np.tile(fr[:, :, 1], (1, NH)))
    m["g1_row"] = np.asarray(ins["L0_ffn_attn_norm_w"]).reshape(1, D)
    return m


def build_stage_b_program():
    nc = bass.Bass("TRN2", target_bir_lowering=False)
    ctx = Ctx(nc)
    x_d = nc.dram_tensor("x_in", [T, D], F32, kind="ExternalInput")
    out_d = nc.dram_tensor("x_out", [T, D], F32, kind="ExternalOutput")
    wd = declare_l0f_attn_weights(nc)
    cs = setup_constants(ctx)
    rg = alloc_regions(nc)
    sm = alloc_smalls(nc)
    sm["g1_row"] = nc.alloc_sbuf_tensor("g1_row_s", [1, D], BF16)
    sm["cr"] = nc.alloc_sbuf_tensor("cr_s", [128, 1024], BF16)
    sm["ci"] = nc.alloc_sbuf_tensor("ci_s", [128, 1024], BF16)
    st = alloc_ln_stats(nc, sq_backing=sm["ws"][:, :2 * D])
    emit_load_small(ctx, [(sm["g1_row"][:, :], wd["g1_row"][:, :])])
    emit_load_x(ctx, x_d, rg["r2"])
    stage_l0_ffn_attn(ctx, cs, st, sm, wd, x_in=rg["r2"], rA=rg["r0"],
                      rB=rg["r1"], rC=rg["r3"], rvp=rg["rvp"])
    emit_store_x(ctx, rg["r0"], out_d)
    nc.finalize()
    return nc



def emit_combine_weights(ctx, tag, rFm, cmbw, sm, cs, rw_key, dram_rw, nacc=DCH,
                         in_stride=T):
    nc = ctx.nc
    with nc.Block() as blk:
        @blk.gpsimd
        def _(g):
            ins = g.dma_start(sm[rw_key][:, :].rearrange("p (dc e) -> p dc e", e=8),
                              dram_rw)
            ctx.inc(ins, tag + "rw", 16)
    with (
        nc.psum_tensor(tag + "_p0", [128, 8], F32) as prt0,
        nc.psum_tensor(tag + "_p1", [128, 8], F32) as prt1,
        nc.Block() as blk,
    ):
        prt = [prt0, prt1]
        @blk.tensor
        def _(t):
            ctx.wait(t, tag + "rw")
            for tt in range(TCH):
                if tt >= 2:
                    ctx.wait(t, tag + "re", tt - 1)
                for dc in range(nacc):
                    ins = t.matmul(prt[tt % 2][:, :],
                                   rFm[:, dc * in_stride + tt * 128:
                                       dc * in_stride + (tt + 1) * 128],
                                   sm[rw_key][:, dc * 8:(dc + 1) * 8],
                                   start=(dc == 0), stop=(dc == nacc - 1))
                ctx.inc(ins, tag + "rp")

        @blk.vector
        def _(v):
            L = sm["cmb_l"]
            m1 = sm["cmb_s"][:, 0:1]
            m2 = sm["cmb_s"][:, 1:2]
            nm1 = sm["cmb_s"][:, 2:3]
            den = sm["cmb_s"][:, 3:4]
            eq = sm["cmb_t"][:, 0:8]
            l2 = sm["cmb_t"][:, 8:16]
            e1 = sm["cmb_t"][:, 16:24]
            ge = sm["cmb_t"][:, 24:32]
            for tt in range(TCH):
                ctx.wait(v, tag + "rp", tt + 1)
                ins = v.tensor_copy(L[:, :], prt[tt % 2][:, :])
                ctx.inc(ins, tag + "re")
                ctx.wait(v, tag + "re")
                ins = v.tensor_reduce(m1[:, :], L[:, :], AX.X, ALU.max)
                edge(ctx, v, ins, tag + "c")
                ins = v.tensor_scalar(eq[:, :], L[:, :], m1[:, :], None, ALU.is_equal)
                edge(ctx, v, ins, tag + "c")
                ins = v.scalar_tensor_tensor(l2[:, :], eq[:, :], -1e30, L[:, :],
                                             ALU.mult, ALU.add)
                edge(ctx, v, ins, tag + "c")
                ins = v.tensor_reduce(m2[:, :], l2[:, :], AX.X, ALU.max)
                edge(ctx, v, ins, tag + "c")
                ins = v.tensor_scalar(nm1[:, :], m1[:, :], -1.0, None, ALU.mult)
                ins = v.tensor_scalar(ge[:, :], L[:, :], m2[:, :], None, ALU.is_ge)
                ctx.inc(ins, tag + "cx")
                ctx.wait(v, tag + "ca", tt + 1)
                ins = v.tensor_tensor(e1[:, :], e1[:, :], ge[:, :], ALU.mult)
                edge(ctx, v, ins, tag + "c")
                ins = v.tensor_reduce(den[:, :], e1[:, :], AX.X, ALU.add)
                edge(ctx, v, ins, tag + "c")
                ins = v.reciprocal(den[:, :], den[:, :])
                edge(ctx, v, ins, tag + "c")
                ins = v.tensor_scalar(cmbw[:, tt * 8:(tt + 1) * 8], e1[:, :],
                                      den[:, :], None, ALU.mult)
                ctx.inc(ins, tag + "cv")

        @blk.scalar
        def _(a):
            L = sm["cmb_l"]
            nm1 = sm["cmb_s"][:, 2:3]
            e1 = sm["cmb_t"][:, 16:24]
            for tt in range(TCH):
                ctx.wait(a, tag + "cx", tt + 1)
                ins = a.activation(e1[:, :], L[:, :], AF.Exp, bias=nm1[:, :])
                ctx.inc(ins, tag + "ca")


def stage_l0_moe(ctx, cs, st, sm, wd, rH, rX1, rOut, rFm, h_sb, acc32):
    nc = ctx.nc
    TH = T // 2

    emit_bcast_rows_simple(ctx, "bc", [
        (sm["g2_row"][0:1, j * 512:(j + 1) * 512],
         sm["bc0"][:, j * 512:(j + 1) * 512]) for j in range(2)
    ], cs)
    emit_ln_stats_apply(ctx, "ln", rH, rOut, st, EPS_RMS, rms=True,
                        gamma_bc=sm["bc0"])
    with nc.Block() as blk:
        @blk.vector
        def _(v):
            for tc in range(TCH):
                v.tensor_tensor(rX1[:, tc * D:(tc + 1) * D],
                                rX1[:, tc * D:(tc + 1) * D],
                                rH[:, tc * D:(tc + 1) * D], ALU.add)
    emit_tok2fm(ctx, "t", rOut, rFm, cs)

    emit_combine_weights(ctx, "l0mr", rFm, sm["cmbw"], sm, cs, "rw",
                         wd["router_w"][:, :].rearrange("(dc p) e -> p dc e", p=128))

    cmbT = sm["cmbT"]
    with (
        nc.psum_tensor("l0mt_p", [8, 128], F32) as ptp,
        nc.Block() as blk,
    ):
        @blk.tensor
        def _(t):
            for tt in range(TCH):
                if tt >= 1:
                    ctx.wait(t, "l0mte", tt)
                ins = t.transpose(ptp[:, :], sm["cmbw"][:, tt * 8:(tt + 1) * 8],
                                  cs["c_identf"][:, :])
                ctx.inc(ins, "l0mtp")

        @blk.vector
        def _(v):
            for tt in range(TCH):
                ctx.wait(v, "l0mtp", tt + 1)
                ins = v.tensor_copy(cmbT[:, tt * 128:(tt + 1) * 128], ptp[:, :])
                ctx.inc(ins, "l0mte")
    with nc.Block() as blk:
        @blk.gpsimd
        def _(g):
            ins = g.dma_start(wd["cmb_d"][:, :], cmbT[:, :])
            ctx.inc(ins, "l0mtd", 16)
            ctx.wait(g, "l0mtd")

    w1s, w3s, w2s, stmp, cbrow = (sm["w1s"], sm["w3s"], sm["w2s"],
                                  sm["stmp"], sm["cbrow"])
    cbh = rOut[:, 8 * 1024:16 * 1024]
    cnt = {"wpair": 0, "w2": 0, "jh": 0, "jy": 0}

    for half in range(2):
        hof = half * TH
        with nc.Block() as blk:
            @blk.vector
            def _(v):
                v.memset(acc32[0][:, :], 0.0)
                v.memset(acc32[1][:, :], 0.0)
        with (
            nc.psum_tensor("l0mcb_p" + str(half), [128, 512], F32) as pcb,
            nc.Block() as blk,
        ):
            @blk.gpsimd
            def _(g):
                for th in range(2):
                    for e in range(8):
                        n = half * 16 + th * 8 + e
                        if n >= 1:
                            ctx.wait(g, "l0mbp", n)
                        src = bass.AP(wd["cmb_d"], e * T + hof + th * 512,
                                      [[1, 1], [1, 512]])
                        ins = g.dma_start(cbrow[:, :], src)
                        ctx.inc(ins, "l0mcw", 16)

            @blk.tensor
            def _(t):
                for th in range(2):
                    for e in range(8):
                        n = half * 16 + th * 8 + e
                        ctx.wait(t, "l0mcw", 16 * (n + 1))
                        if n >= 1:
                            ctx.wait(t, "l0mbe", n)
                        ins = t.matmul(pcb[:, :], cs["c_onesb"][0:1, :],
                                       cbrow[0:1, :],
                                       start=True, stop=True)
                        ctx.inc(ins, "l0mbp")

            @blk.vector
            def _(v):
                for th in range(2):
                    for e in range(8):
                        n = half * 16 + th * 8 + e
                        ctx.wait(v, "l0mbp", n + 1)
                        ins = v.tensor_copy(
                            cbh[:, e * 1024 + th * 512:e * 1024 + (th + 1) * 512],
                            pcb[:, :])
                        ctx.inc(ins, "l0mbe")

        wp0, w20 = cnt["wpair"], cnt["w2"]
        jh0, jy0 = cnt["jh"], cnt["jy"]
        h_end_of_e = {}
        y_end_of_e = {}
        jh, jy, wp, w2i = jh0, jy0, wp0, w20
        for e in range(8):
            jh += 32
            wp += 16
            h_end_of_e[e] = jh
            jy += 16
            w2i += 8
            y_end_of_e[e] = jy

        with (
            nc.psum_tensor("l0mh_a0" + str(half), [128, 512], F32) as ha0,
            nc.psum_tensor("l0mh_b0" + str(half), [128, 512], F32) as hb0,
            nc.psum_tensor("l0mh_a1" + str(half), [128, 512], F32) as ha1,
            nc.psum_tensor("l0mh_b1" + str(half), [128, 512], F32) as hb1,
            nc.psum_tensor("l0mh_a2" + str(half), [128, 512], F32) as ha2,
            nc.psum_tensor("l0mh_b2" + str(half), [128, 512], F32) as hb2,
            nc.psum_tensor("l0my_0" + str(half), [128, 512], F32) as yq0,
            nc.psum_tensor("l0my_1" + str(half), [128, 512], F32) as yq1,
            nc.Block() as blk,
        ):
            hpa = [ha0, ha1, ha2]
            hpb = [hb0, hb1, hb2]
            yps = [yq0, yq1]

            @blk.gpsimd
            def _(g):
                wpx, w2x = wp0, w20
                for e in range(8):
                    for ic in range(16):
                        if wpx >= 2:
                            ctx.wait(g, "mhp", (wpx - 1) * 2)
                        for wkey, dst in (("w1_l", w1s), ("w3_l", w3s)):
                            ins = g.dma_start(
                                dst[:, (wpx % 2) * 1024:(wpx % 2 + 1) * 1024]
                                .rearrange("p (dc c) -> p dc c", c=128),
                                wd[wkey][e * 16 + ic].rearrange(
                                    "(dc p) c -> p dc c", p=128))
                            ctx.inc(ins, "mhw" + str(wpx % 2), 16)
                        wpx += 1
                    for dc in range(8):
                        if w2x >= 2:
                            ctx.wait(g, "myp", (w2x - 1) * 2)
                        ins = g.dma_start(
                            w2s[:, (w2x % 2) * 2048:(w2x % 2 + 1) * 2048]
                            .rearrange("p (ic c) -> p ic c", c=128),
                            wd["w2_l"][e * 8 + dc].rearrange(
                                "(ic p) c -> p ic c", p=128))
                        ctx.inc(ins, "myw" + str(w2x % 2), 16)
                        w2x += 1

            @blk.tensor
            def _(t):
                jhx, jyx = jh0, jy0
                for e in range(8):
                    for ic in range(16):
                        wpair = wp0 + e * 16 + ic
                        ctx.wait(t, "mhw" + str(wpair % 2), 32 * (wpair // 2 + 1))
                        for th in range(2):
                            if jhx >= 3:
                                ctx.wait(t, "mhe", jhx - 2)
                                ctx.wait(t, "mha", jhx - 2)
                            pa, pb = hpa[jhx % 3], hpb[jhx % 3]
                            for dc in range(DCH):
                                t.matmul(pa[:, :],
                                         w1s[:, (wpair % 2) * 1024 + dc * 128:
                                             (wpair % 2) * 1024 + (dc + 1) * 128],
                                         rFm[:, dc * T + hof + th * 512:
                                             dc * T + hof + (th + 1) * 512],
                                         start=(dc == 0), stop=(dc == DCH - 1))
                            for dc in range(DCH):
                                ins = t.matmul(pb[:, :],
                                               w3s[:, (wpair % 2) * 1024 + dc * 128:
                                                   (wpair % 2) * 1024 + (dc + 1) * 128],
                                               rFm[:, dc * T + hof + th * 512:
                                                   dc * T + hof + (th + 1) * 512],
                                               start=(dc == 0), stop=(dc == DCH - 1))
                            ctx.inc(ins, "mhp")
                            jhx += 1
                    for dc in range(8):
                        w2x = w20 + e * 8 + dc
                        ctx.wait(t, "myw" + str(w2x % 2), 16 * (w2x // 2 + 1))
                        for th in range(2):
                            if jyx >= 2:
                                ctx.wait(t, "mye", jyx - 1)
                            ctx.wait(t, "mhe", h_end_of_e[e])
                            for ic in range(16):
                                ins = t.matmul(
                                    yps[jyx % 2][:, :],
                                    w2s[:, (w2x % 2) * 2048 + ic * 128:
                                        (w2x % 2) * 2048 + (ic + 1) * 128],
                                    h_sb[:, ic * 1024 + th * 512:
                                         ic * 1024 + (th + 1) * 512],
                                    start=(ic == 0), stop=(ic == 15))
                            ctx.inc(ins, "myp")
                            jyx += 1

            @blk.scalar
            def _(a):
                jhx = jh0
                for e in range(8):
                    for ic in range(16):
                        for th in range(2):
                            ctx.wait(a, "mhp", jhx + 1)
                            ins = a.activation(
                                stmp[:, (jhx % 3) * 512:(jhx % 3 + 1) * 512],
                                hpa[jhx % 3][:, :], AF.Silu)
                            ctx.inc(ins, "mha")
                            jhx += 1

            @blk.vector
            def _(v):
                jhx, jyx = jh0, jy0
                for e in range(8):
                    for ic in range(16):
                        for th in range(2):
                            ctx.wait(v, "mha", jhx + 1)
                            if ic == 0 and th == 0 and jyx > jy0 - 0 and e > 0:
                                ctx.wait(v, "myp", jyx)
                            elif e == 0 and ic == 0 and th == 0 and jy0 > 0:
                                ctx.wait(v, "myp", jy0)
                            dst = h_sb[:, ic * 1024 + th * 512:
                                       ic * 1024 + (th + 1) * 512]
                            ins = v.tensor_tensor(
                                dst, stmp[:, (jhx % 3) * 512:(jhx % 3 + 1) * 512],
                                hpb[jhx % 3][:, :], ALU.mult)
                            edge(ctx, v, ins, "mhc")
                            ins = v.tensor_tensor(
                                dst, dst,
                                cbh[:, e * 1024 + th * 512:e * 1024 + (th + 1) * 512],
                                ALU.mult)
                            ctx.inc(ins, "mhe")
                            jhx += 1
                    for dc in range(8):
                        for th in range(2):
                            ctx.wait(v, "myp", jyx + 1)
                            sl = acc32[th][:, dc * 512:(dc + 1) * 512]
                            ins = v.tensor_tensor(sl, sl, yps[jyx % 2][:, :], ALU.add)
                            ctx.inc(ins, "mye")
                            jyx += 1
        cnt["wpair"] = wp
        cnt["w2"] = w2i
        cnt["jh"] = jh
        cnt["jy"] = jy

        with (
            nc.psum_tensor("l0ma_p0" + str(half), [128, 128], F32) as tp0,
            nc.psum_tensor("l0ma_p1" + str(half), [128, 128], F32) as tp1,
            nc.Block() as blk,
        ):
            tps = [tp0, tp1]

            base_tp = ctx.val("mtp")
            base_te = ctx.val("mte")

            @blk.tensor
            def _(t):
                n = 0
                for ts in range(8):
                    for dc in range(8):
                        if n >= 2:
                            ctx.wait(t, "mte", base_te + n - 1)
                        ins = t.transpose(
                            tps[n % 2][:, :],
                            acc32[ts // 4]
                            [:, dc * 512 + (ts % 4) * 128:
                             dc * 512 + (ts % 4 + 1) * 128],
                            cs["c_identf"][:, :])
                        ctx.inc(ins, "mtp")
                        n += 1

            @blk.vector
            def _(v):
                n = 0
                for ts in range(8):
                    tc = half * 8 + ts
                    for dc in range(8):
                        ctx.wait(v, "mtp", base_tp + n + 1)
                        ins = v.tensor_add(
                            rOut[:, tc * D + dc * 128:tc * D + (dc + 1) * 128],
                            tps[n % 2][:, :],
                            rX1[:, tc * D + dc * 128:tc * D + (dc + 1) * 128])
                        ctx.inc(ins, "mte")
                        n += 1

    emit_ln_stats_apply(ctx, "ln", rOut, rOut, st, EPS_LN)


def declare_l0_moe_weights(nc):
    wd = {}
    w1 = nc.dram_tensor("w1_l", [8 * 16, D, 128], F32, kind="ExternalInput")
    wd["w1_l"] = [w1[i] for i in range(128)]
    w3 = nc.dram_tensor("w3_l", [8 * 16, D, 128], F32, kind="ExternalInput")
    wd["w3_l"] = [w3[i] for i in range(128)]
    w2 = nc.dram_tensor("w2_l", [8 * 8, 16 * 128, 128], F32, kind="ExternalInput")
    wd["w2_l"] = [w2[i] for i in range(64)]
    wd["router_w"] = nc.dram_tensor("router_w", [D, 8], F32, kind="ExternalInput")
    wd["g2_row"] = nc.dram_tensor("g2_row", [1, D], F32, kind="ExternalInput")
    wd["cmb_d"] = nc.dram_tensor("cmb_d", [8, T], BF16)
    return wd


def host_l0_moe_weight_maps(ins):
    m = {}
    w1 = np.asarray(ins["L0_ffn_w1"])
    m["w1_l"] = np.ascontiguousarray(
        w1.reshape(8, D, 16, 128).transpose(0, 2, 1, 3).reshape(128, D, 128))
    w3 = np.asarray(ins["L0_ffn_w3"])
    m["w3_l"] = np.ascontiguousarray(
        w3.reshape(8, D, 16, 128).transpose(0, 2, 1, 3).reshape(128, D, 128))
    w2 = np.asarray(ins["L0_ffn_w2"])
    m["w2_l"] = np.ascontiguousarray(
        w2.reshape(8, 16, 128, 8, 128).transpose(0, 3, 1, 2, 4)
        .reshape(64, 16 * 128, 128))
    m["router_w"] = np.asarray(ins["L0_ffn_router_w"])
    m["g2_row"] = np.asarray(ins["L0_ffn_ffn_norm_w"]).reshape(1, D)
    return m


def alloc_moe_smalls(nc, sm, ar=None):
    uid = [0]
    def _mk(sh, dt):
        uid[0] += 1
        return nc.alloc_sbuf_tensor("moe_%d" % uid[0], sh, dt)
    A = ar.take if ar else _mk
    sm["rw"] = A([128, 64], BF16)
    sm["cmbw"] = A([128, TCH * 8], F32)
    sm["cmb_l"] = A([128, 8], F32)
    sm["cmb_s"] = A([128, 4], F32)
    sm["cmb_t"] = A([128, 32], F32)
    sm["cmbT"] = A([8, T], BF16)
    ws = sm["ws"]
    sm["w1s"] = ws[:, 0:2048]
    sm["w3s"] = ws[:, 2048:4096]
    sm["w2s"] = ws[:, 4096:8192]
    sm["stmp"] = sm["bc0"].bitcast(BF16)[:, 0:1536]
    sm["cbrow"] = (ar.take([1, 512], BF16) if ar else
                   nc.alloc_sbuf_tensor("cbrow", [1, 512], BF16))
    return sm


def emit_load_x2(ctx, x_d, xa):
    nc = ctx.nc
    with nc.Block() as blk:
        @blk.gpsimd
        def _(g):
            for tc in range(TCH):
                ins = g.dma_start(xa[:, tc * D:(tc + 1) * D],
                                  x_d[tc * 128:(tc + 1) * 128, :])
                ctx.inc(ins, "xld2", 16)
            ctx.wait(g, "xld2")


def build_stage_c_program():
    nc = bass.Bass("TRN2", target_bir_lowering=False)
    ctx = Ctx(nc)
    h_d = nc.dram_tensor("h_in", [T, D], F32, kind="ExternalInput")
    x1_d = nc.dram_tensor("x1_in", [T, D], F32, kind="ExternalInput")
    out_d = nc.dram_tensor("x_out", [T, D], F32, kind="ExternalOutput")
    wd = declare_l0_moe_weights(nc)
    cs = setup_constants(ctx)
    rg = alloc_regions(nc)
    sm = alloc_smalls(nc)
    sm["g2_row"] = nc.alloc_sbuf_tensor("g2_row_s", [1, D], BF16)
    alloc_moe_smalls(nc, sm)
    st = alloc_ln_stats(nc, sq_backing=sm["ws"][:, :2 * D])
    emit_load_small(ctx, [(sm["g2_row"][:, :], wd["g2_row"][:, :])])
    emit_load_x(ctx, h_d, rg["r0"])
    emit_load_x2(ctx, x1_d, rg["r2"])
    acc32 = [rg["r0"].bitcast(F32)[:, 0:4096], rg["r0"].bitcast(F32)[:, 4096:8192]]
    stage_l0_moe(ctx, cs, st, sm, wd, rH=rg["r0"], rX1=rg["r2"], rOut=rg["r1"],
                 rFm=rg["r3"], h_sb=rg["rvp"][:, :16 * 1024], acc32=acc32)
    emit_store_x(ctx, rg["r1"], out_d)
    nc.finalize()
    return nc



def stage_ttt(ctx, cs, st, sm, wd, rX, rFm, rTH, rvp, wreg, extra_gdma=None):
    nc = ctx.nc
    ws = sm["ws"]
    ktok = ws[:, 0:1024]
    vtok = ws[:, 1024:2048]
    qtok = ws[:, 2048:3072]
    kfm = ws[:, 3072:4096]
    qfm = ws[:, 4096:5120]
    ubf = ws[:, 5120:6144]
    Xs = [ws[:, 6144:6272], ws[:, 6400:6528]]
    Ys = [ws[:, 6272:6400], ws[:, 6528:6656]]
    Msb = ws[:, 6656:6784]
    sq2 = ws[:, 6784:7808]
    wbf = rvp[:, 0:8192]
    thq = rvp[:, 8192:16384]
    bbc = sm["bc1"]
    tmpO = sm["ttO"]
    tmpB = sm["bc0"]
    brow = sm["tt_brow"]
    Uf = sm["ttU"]
    st2 = dict(st)
    st2["sq"] = sq2

    emit_tok2fm(ctx, "t", rX, rFm, cs)

    with nc.Block() as blk:
        @blk.gpsimd
        def _(g):
            for dc in range(DCH):
                for key, dst in (("thK", rTH[:, dc * 1024:(dc + 1) * 1024]),
                                 ("thV", rTH[:, 8192 + dc * 1024:8192 + (dc + 1) * 1024]),
                                 ("thQ", thq[:, dc * 1024:(dc + 1) * 1024])):
                    ins = g.dma_start(dst, wd[key][dc])
                    ctx.inc(ins, "ttw", 16)
            for ec in range(DCH):
                ins = g.dma_start(wreg[:, ec * 1024:(ec + 1) * 1024], wd["w0T"][ec])
                ctx.inc(ins, "ttw", 16)
                ins = g.dma_start(wbf[:, ec * 1024:(ec + 1) * 1024], wd["w0T"][ec])
                ctx.inc(ins, "ttw", 16)
            ins = g.dma_start(brow[:, :], wd["b0"][:, :])
            ctx.inc(ins, "ttw", 16)
            if extra_gdma is not None:
                extra_gdma(g)
            ctx.wait(g, "ttw")
    emit_bcast_rows_simple(ctx, "bc", [
        (brow[0:1, j * 512:(j + 1) * 512], bbc[:, j * 512:(j + 1) * 512])
        for j in range(2)], cs)

    for c in range(TCH):
        tag = "tt"
        jobs, meta = [], []
        for src, dst in ((rTH, ktok), (None, vtok), (None, qtok)):
            pass
        proj_srcs = [(0, ktok), (1, vtok), (2, qtok)]
        jobs = []
        dsts = []
        for wi, dstt in proj_srcs:
            for it in range(2):
                job = []
                for dc in range(DCH):
                    if wi == 0:
                        rhs = rTH[:, dc * 1024 + it * 512:dc * 1024 + (it + 1) * 512]
                    elif wi == 1:
                        rhs = rTH[:, 8192 + dc * 1024 + it * 512:
                                  8192 + dc * 1024 + (it + 1) * 512]
                    else:
                        rhs = thq[:, dc * 1024 + it * 512:dc * 1024 + (it + 1) * 512]
                    job.append((rFm[:, dc * T + c * 128:dc * T + (c + 1) * 128], rhs))
                jobs.append(job)
                dsts.append((dstt, it))

        def proj_evac(n, ps, v):
            dstt, it = dsts[n]
            return v.tensor_copy(dstt[:, it * 512:(it + 1) * 512], ps)

        with (
            nc.psum_tensor(tag + str(c) + "p0", [128, 512], F32) as p0,
            nc.psum_tensor(tag + str(c) + "p1", [128, 512], F32) as p1,
            nc.psum_tensor(tag + str(c) + "p2", [128, 512], F32) as p2,
            nc.psum_tensor(tag + str(c) + "p3", [128, 512], F32) as p3,
        ):
            emit_matmul_tiles(ctx, "mm", jobs,
                              [p0[:, :], p1[:, :], p2[:, :], p3[:, :]], proj_evac)

        pairs = []
        for ec in range(DCH):
            pairs.append((ktok[:, ec * 128:(ec + 1) * 128],
                          kfm[:, ec * 128:(ec + 1) * 128]))
            pairs.append((qtok[:, ec * 128:(ec + 1) * 128],
                          qfm[:, ec * 128:(ec + 1) * 128]))
        emit_transposes(ctx, "t", pairs, cs)

        with (
            nc.psum_tensor(tag + str(c) + "B0", [128, 512], F32) as B0,
            nc.psum_tensor(tag + str(c) + "B1", [128, 512], F32) as B1,
            nc.psum_tensor(tag + str(c) + "A", [128, 128], F32) as Aps,
            nc.psum_tensor(tag + str(c) + "KQ", [128, 128], F32) as KQps,
            nc.psum_tensor(tag + str(c) + "O0", [128, 512], F32) as O0,
            nc.psum_tensor(tag + str(c) + "O1", [128, 512], F32) as O1,
            nc.psum_tensor(tag + str(c) + "S0", [128, 512], F32) as S0,
            nc.psum_tensor(tag + str(c) + "S1", [128, 512], F32) as S1,
        ):
            Bps = [B0, B1]
            Ops = [O0, O1]
            Sps = [S0, S1]
            with nc.Block() as blk:
                @blk.tensor
                def _(t):
                    for it in range(2):
                        for ec in range(DCH):
                            t.matmul(Bps[it][:, :],
                                     kfm[:, ec * 128:(ec + 1) * 128],
                                     wbf[:, ec * 1024 + it * 512:ec * 1024 + (it + 1) * 512],
                                     start=(ec == 0), stop=(ec == DCH - 1))
                    for ec in range(DCH):
                        t.matmul(Aps[:, :], kfm[:, ec * 128:(ec + 1) * 128],
                                 kfm[:, ec * 128:(ec + 1) * 128],
                                 start=(ec == 0), stop=(ec == DCH - 1))
                    for ec in range(DCH):
                        t.matmul(KQps[:, :], kfm[:, ec * 128:(ec + 1) * 128],
                                 qfm[:, ec * 128:(ec + 1) * 128],
                                 start=(ec == 0), stop=(ec == DCH - 1))
                    for it in range(2):
                        for ec in range(DCH):
                            ins = t.matmul(Ops[it][:, :],
                                           qfm[:, ec * 128:(ec + 1) * 128],
                                           wbf[:, ec * 1024 + it * 512:
                                               ec * 1024 + (it + 1) * 512],
                                           start=(ec == 0), stop=False)
                    ctx.inc(ins, tag + "mm1")
            with nc.Block() as blk:
                @blk.vector
                def _(v):
                    ctx.wait(v, tag + "mm1")
                    for it in range(2):
                        sl = slice(it * 512, (it + 1) * 512)
                        ins = v.tensor_add(tmpB[:, sl], Bps[it][:, :], bbc[:, sl])
                        edge(ctx, v, ins, tag + "c")
                        ins = v.tensor_tensor(tmpB[:, sl], tmpB[:, sl], vtok[:, sl],
                                              ALU.subtract)
                        edge(ctx, v, ins, tag + "c")
                        ins = v.tensor_scalar(Uf[:, sl], tmpB[:, sl], -A_TTT, None,
                                              ALU.mult)
                        edge(ctx, v, ins, tag + "c")
                        v.tensor_copy(ubf[:, sl], Uf[:, sl])
                    ins = v.scalar_tensor_tensor(Xs[0][:, :], Aps[:, :], 1.0,
                                                 cs["c_trils"][:, :], ALU.add, ALU.mult)
                    ins = v.scalar_tensor_tensor(Ys[0][:, :], Aps[:, :], 1.0,
                                                 cs["c_trius"][:, :], ALU.add, ALU.mult)
                    ins = v.scalar_tensor_tensor(Msb[:, :], KQps[:, :], 1.0,
                                                 cs["c_uincl"][:, :], ALU.add, ALU.mult)

            base_svu = ctx.val(tag + "svu")
            with nc.Block() as blk:
                @blk.tensor
                def _(t):
                    for lvl in range(7):
                        cur = lvl % 2
                        if lvl >= 1:
                            ctx.wait(t, tag + "svu", base_svu + lvl)
                        for it in range(2):
                            ins = t.matmul(Sps[it][:, :], Ys[cur][:, :],
                                           ubf[:, it * 512:(it + 1) * 512],
                                           start=True, stop=True)
                        ctx.inc(ins, tag + "sva")
                        if lvl < 6:
                            t.matmul(KQps[:, :], Ys[cur][:, :], Xs[cur][:, :],
                                     start=True, stop=True)
                            ins = t.matmul(Aps[:, :], Xs[cur][:, :], Ys[cur][:, :],
                                           start=True, stop=True)
                            ctx.inc(ins, tag + "svb")

                @blk.vector
                def _(v):
                    for lvl in range(7):
                        nxt = (lvl + 1) % 2
                        ctx.wait(v, tag + "sva", ctx.val(tag + "sva") - 6 + lvl)
                        op = ALU.subtract if lvl == 0 else ALU.add
                        for it in range(2):
                            sl = slice(it * 512, (it + 1) * 512)
                            ins = v.tensor_tensor(Uf[:, sl], Uf[:, sl],
                                                  Sps[it][:, :], op)
                            edge(ctx, v, ins, tag + "c")
                            ins = v.tensor_copy(ubf[:, sl], Uf[:, sl])
                        if lvl < 6:
                            ctx.wait(v, tag + "svb", ctx.val(tag + "svb") - 5 + lvl)
                            v.tensor_copy(Xs[nxt][:, :], KQps[:, :])
                            ins = v.tensor_copy(Ys[nxt][:, :], Aps[:, :])
                        ctx.inc(ins, tag + "svu")

            with nc.Block() as blk:
                @blk.tensor
                def _(t):
                    for it in range(2):
                        ins = t.matmul(Ops[it][:, :], Msb[:, :],
                                       ubf[:, it * 512:(it + 1) * 512],
                                       start=False, stop=True)
                    ctx.inc(ins, tag + "o2")
                @blk.vector
                def _(v):
                    ctx.wait(v, tag + "o2")
                    for it in range(2):
                        sl = slice(it * 512, (it + 1) * 512)
                        ins = v.tensor_add(tmpO[:, sl], Ops[it][:, :], bbc[:, sl])
                        edge(ctx, v, ins, tag + "c")
                        v.tensor_tensor(tmpO[:, sl], tmpO[:, sl],
                                        rX[:, c * D + it * 512:c * D + (it + 1) * 512],
                                        ALU.add)

            emit_ln_stats_apply(ctx, "ln", tmpO, rX[:, c * D:(c + 1) * D],
                                st2, EPS_LN, n_t=1)

            base_we = ctx.val(tag + "we")
            base_wp = ctx.val(tag + "wp")
            with nc.Block() as blk:
                @blk.tensor
                def _(t):
                    n = 0
                    for ec in range(DCH):
                        for it in range(2):
                            if n >= 2:
                                ctx.wait(t, tag + "we", base_we + n - 1)
                            ins = t.matmul(Sps[n % 2][:, :],
                                           ktok[:, ec * 128:(ec + 1) * 128],
                                           ubf[:, it * 512:(it + 1) * 512],
                                           start=True, stop=True)
                            ctx.inc(ins, tag + "wp")
                            n += 1
                    for it in range(2):
                        ins = t.matmul(Bps[it][0:1, :], cs["c_onesb"][:, 0:1],
                                       ubf[:, it * 512:(it + 1) * 512],
                                       start=True, stop=True)
                    ctx.inc(ins, tag + "bp")
                @blk.vector
                def _(v):
                    n = 0
                    for ec in range(DCH):
                        for it in range(2):
                            ctx.wait(v, tag + "wp", base_wp + n + 1)
                            sl = slice(ec * 1024 + it * 512, ec * 1024 + (it + 1) * 512)
                            ins = v.tensor_tensor(wreg[:, sl], wreg[:, sl],
                                                  Sps[n % 2][:, :], ALU.add)
                            edge(ctx, v, ins, tag + "c")
                            ins = v.tensor_copy(wbf[:, sl], wreg[:, sl])
                            ctx.inc(ins, tag + "we")
                            n += 1
                    ctx.wait(v, tag + "bp")
                    for it in range(2):
                        sl = slice(it * 512, (it + 1) * 512)
                        ins = v.tensor_tensor(brow[0:1, sl], brow[0:1, sl],
                                              Bps[it][0:1, :], ALU.add)
                    ctx.inc(ins, tag + "bv")
            with nc.Block() as blk:
                @blk.tensor
                def _(t):
                    ctx.wait(t, tag + "bv")
                    for it in range(2):
                        ins = t.matmul(Sps[it][:, :], cs["c_ones"][0:1, :],
                                       brow[0:1, it * 512:(it + 1) * 512],
                                       start=True, stop=True)
                    ctx.inc(ins, tag + "b2")
                @blk.vector
                def _(v):
                    ctx.wait(v, tag + "b2")
                    for it in range(2):
                        v.tensor_copy(bbc[:, it * 512:(it + 1) * 512],
                                      Sps[it][:, :])


def declare_ttt_weights(nc):
    wd = {}
    for key, src in (("thK", "L1_attn_theta_K"), ("thV", "L1_attn_theta_V"),
                     ("thQ", "L1_attn_theta_Q")):
        w = nc.dram_tensor(key, [DCH, 128, D], F32, kind="ExternalInput")
        wd[key] = [w[i] for i in range(DCH)]
    w = nc.dram_tensor("w0T", [DCH, 128, D], F32, kind="ExternalInput")
    wd["w0T"] = [w[i] for i in range(DCH)]
    wd["b0"] = nc.dram_tensor("b0", [1, D], F32, kind="ExternalInput")
    return wd


def host_ttt_weight_maps(ins):
    m = {}
    for key, src in (("thK", "L1_attn_theta_K"), ("thV", "L1_attn_theta_V"),
                     ("thQ", "L1_attn_theta_Q")):
        W = np.asarray(ins[src])
        m[key] = np.ascontiguousarray(W.T.reshape(DCH, 128, D))
    W0 = np.asarray(ins["L1_attn_W0"])
    m["w0T"] = np.ascontiguousarray(W0.T.reshape(DCH, 128, D))
    m["b0"] = np.asarray(ins["L1_attn_b0"]).reshape(1, D)
    return m


def alloc_ttt_smalls(nc, sm, ar=None):
    A = ar.take if ar else (lambda sh, dt: nc.alloc_sbuf_tensor(
        "ttt_%d_%d" % (sh[0], sh[1]), sh, dt))
    sm["ttO"] = sm["bc0"]
    sm["ttU"] = A([128, D], F32)
    sm["tt_brow"] = A([1, D], F32)
    return sm


def build_stage_d_program():
    nc = bass.Bass("TRN2", target_bir_lowering=False)
    ctx = Ctx(nc)
    x_d = nc.dram_tensor("x_in", [T, D], F32, kind="ExternalInput")
    out_d = nc.dram_tensor("x_out", [T, D], F32, kind="ExternalOutput")
    wd = declare_ttt_weights(nc)
    cs = setup_constants(ctx)
    rg = alloc_regions(nc)
    sm = alloc_smalls(nc)
    alloc_ttt_smalls(nc, sm)
    st = alloc_ln_stats(nc, sq_backing=sm["ws"][:, :2 * D])
    emit_load_x(ctx, x_d, rg["r1"])
    stage_ttt(ctx, cs, st, sm, wd, rX=rg["r1"], rFm=rg["r3"], rTH=rg["r0"],
              rvp=rg["rvp"], wreg=rg["r2"].bitcast(F32))
    emit_store_x(ctx, rg["r1"], out_d)
    nc.finalize()
    return nc



CAPSL = 20


def stage_l1_ffn(ctx, cs, st, sm, wd, rX, rLT, cta, ctb, h_sb, rvp, out_d,
                 r0=None):
    nc = ctx.nc
    ws = sm["ws"]
    sm["lcum"] = cta[0:8, 4104:4104 + T]

    with nc.Block() as blk:
        @blk.sync
        def _(s):
            ins = s.dma_start(
                wd["xg_loc"][:, :].rearrange("(tt p) d -> p tt d", p=128),
                rX[:, :].rearrange("p (tt d) -> p tt d", d=D))
            ctx.inc(ins, "e4s", 16)
        @blk.gpsimd
        def _(g):
            g.wait_ge(ctx.sem("e4s"), 16)
            ins = g.collective_compute(
                "AllGather", ALU.bypass, replica_groups=[list(range(N_CORES))],
                ins=[wd["xg_loc"].ap().opt()], outs=[wd["xg_all"].ap().opt()])
            ctx.inc(ins, "ccx")

    emit_tok2fm(ctx, "t", rX, rLT, cs)

    emit_bcast_rows_simple(ctx, "bc", [(sm["bg_row"][0:1, 0:8], sm["bg_bc"][:, 0:8])],
                           cs)
    lg_tok = sm["lg_tok"]
    with nc.Block() as blk:
        @blk.gpsimd
        def _(g):
            ins = g.dma_start(sm["gw"][:, :].rearrange("p (dc e) -> p dc e", e=8),
                              wd["Wg"][:, :].rearrange("(dc p) e -> p dc e", p=128))
            ctx.inc(ins, "e2w", 16)
    with (
        nc.psum_tensor("e2_p0", [128, 8], F32) as pg0,
        nc.psum_tensor("e2_p1", [128, 8], F32) as pg1,
        nc.Block() as blk,
    ):
        pgs = [pg0, pg1]
        @blk.tensor
        def _(t):
            ctx.wait(t, "e2w")
            for tt in range(TCH):
                if tt >= 2:
                    ctx.wait(t, "e2e", tt - 1)
                for dc in range(DCH):
                    ins = t.matmul(pgs[tt % 2][:, :],
                                   rLT[:, dc * T + tt * 128:dc * T + (tt + 1) * 128],
                                   sm["gw"][:, dc * 8:(dc + 1) * 8],
                                   start=(dc == 0), stop=(dc == DCH - 1))
                ctx.inc(ins, "e2p")
        @blk.vector
        def _(v):
            for tt in range(TCH):
                ctx.wait(v, "e2p", tt + 1)
                ins = v.tensor_add(lg_tok[:, tt * 8:(tt + 1) * 8], pgs[tt % 2][:, :],
                                   sm["bg_bc"][:, 0:8])
                ctx.inc(ins, "e2e")
    L3 = lg_tok[:, :].rearrange("p (t e) -> p t e", e=8)
    m1, m2 = sm["e_m1"], sm["e_m2"]
    eqm1, eqm2, gem = sm["e_eq1"], sm["e_eq2"], sm["e_ge"]
    w1t, w2t = sm["e_w1"], sm["e_w2"]
    with nc.Block() as blk:
        @blk.vector
        def _(v):
            ins = v.tensor_reduce(m1[:, :], L3, AX.X, ALU.max)
            edge(ctx, v, ins, "ec")
            ins = v.tensor_tensor(eqm1[:, :].rearrange("p (t e) -> p t e", e=8),
                                  L3, m1[:, :].unsqueeze(2).broadcast_to((128, 16, 8)),
                                  ALU.is_equal)
            edge(ctx, v, ins, "ec")
            ins = v.scalar_tensor_tensor(eqm2[:, :], eqm1[:, :], -1e30,
                                         lg_tok[:, :], ALU.mult, ALU.add)
            edge(ctx, v, ins, "ec")
            ins = v.tensor_reduce(m2[:, :], eqm2[:, :].rearrange(
                "p (t e) -> p t e", e=8), AX.X, ALU.max)
            edge(ctx, v, ins, "ec")
            ins = v.tensor_tensor(gem[:, :].rearrange("p (t e) -> p t e", e=8),
                                  L3, m2[:, :].unsqueeze(2).broadcast_to((128, 16, 8)),
                                  ALU.is_ge)
            edge(ctx, v, ins, "ec")
            ins = v.tensor_tensor(eqm2[:, :].rearrange("p (t e) -> p t e", e=8),
                                  L3, m2[:, :].unsqueeze(2).broadcast_to((128, 16, 8)),
                                  ALU.is_equal)
            edge(ctx, v, ins, "ec")
            ins = v.tensor_tensor(w2t[:, :], m2[:, :], m1[:, :], ALU.subtract)
            ctx.inc(ins, "ecx")
        @blk.scalar
        def _(a):
            ctx.wait(a, "ecx")
            ins = a.activation(w2t[:, :], w2t[:, :], AF.Exp)
            ctx.inc(ins, "eca")
        @blk.vector
        def _(v):
            ctx.wait(v, "eca")
            ins = v.tensor_scalar(w1t[:, :], w2t[:, :], 1.0, None, ALU.add)
            edge(ctx, v, ins, "ec")
            ins = v.reciprocal(w1t[:, :], w1t[:, :])
            edge(ctx, v, ins, "ec")
            ins = v.tensor_tensor(w2t[:, :], w2t[:, :], w1t[:, :], ALU.mult)
            edge(ctx, v, ins, "ec")

    labl = sm["labl"]
    with (
        nc.psum_tensor("e5t0", [8, 128], F32) as pt0,
        nc.psum_tensor("e5t1", [8, 128], F32) as pt1,
        nc.Block() as blk,
    ):
        pts = [pt0, pt1]
        @blk.tensor
        def _(t):
            for n in range(TCH):
                if n >= 2:
                    ctx.wait(t, "e5e", n - 1)
                ins = t.transpose(pts[n % 2][:, :], gem[:, n * 8:(n + 1) * 8],
                                  cs["c_identf"][:, :])
                ctx.inc(ins, "e5p")
        @blk.vector
        def _(v):
            for n in range(TCH):
                ctx.wait(v, "e5p", n + 1)
                ins = v.tensor_copy(labl[0:8, n * 128:(n + 1) * 128], pts[n % 2][:, :])
                ctx.inc(ins, "e5e")
    with nc.Block() as blk:
        @blk.sync
        def _(s):
            ins = s.dma_start(wd["lab_loc"][:, :], labl[0:8, :])
            ctx.inc(ins, "e3s", 16)
        @blk.gpsimd
        def _(g):
            g.wait_ge(ctx.sem("e3s"), 16)
            ins = g.collective_compute(
                "AllGather", ALU.bypass, replica_groups=[list(range(N_CORES))],
                ins=[wd["lab_loc"].ap().opt()], outs=[wd["lab_all"].ap().opt()])
            ctx.inc(ins, "ccl")
    with nc.Block() as blk:
        @blk.gpsimd
        def _(g):
            ctx.wait(g, "ccl")
            src = bass.AP(wd["lab_all"], 0, [[T, 8], [8 * T, 8], [1, T]])
            ins = g.dma_start(
                rLT[0:8, 0:N_CORES * T].rearrange("e (k t) -> e k t", t=T), src)
            ctx.inc(ins, "e5w", 16)
            ctx.wait(g, "e5w")

    strip = cta[0:8, 0:4096]
    lastc = cta[0:8, 4096:4097]
    with nc.Block() as blk:
        @blk.vector
        def _(v):
            v.memset(lastc, 0.0)
    for q in range(4):
        with nc.Block() as blk:
            @blk.vector
            def _(v):
                if q >= 1:
                    ctx.wait(v, "e6s", 16 * q)
                ins = v.tensor_tensor_scan(strip, rLT[0:8, q * 4096:(q + 1) * 4096],
                                           rLT[0:8, q * 4096:(q + 1) * 4096],
                                           lastc, ALU.add, ALU.bypass)
                edge(ctx, v, ins, "ec")
                v.tensor_copy(lastc, strip[:, 4095:4096])
        with (
            nc.psum_tensor("e6_p0q%d" % q, [1, 512], F32) as pc0,
            nc.psum_tensor("e6_p1q%d" % q, [1, 512], F32) as pc1,
            nc.Block() as blk,
        ):
            pcs = [pc0, pc1]
            base6e = ctx.val("e6e")
            base6p = ctx.val("e6p")
            base6g = ctx.val("e6g")
            @blk.sync
            def _(s):
                ins = s.dma_start(
                    wd["c_all"][:, :].rearrange("(e k) t -> e k t", k=8)
                    [:, 2 * q:2 * q + 2, :],
                    strip.rearrange("e (k t) -> e k t", t=2048))
                ctx.inc(ins, "e6s", 16)
            @blk.tensor
            def _(t):
                for j in range(8):
                    if j >= 2:
                        ctx.wait(t, "e6e", base6e + j - 1)
                    ins = t.matmul(pcs[j % 2][:, :], sm["myhot"][0:8, 0:1],
                                   strip[:, j * 512:(j + 1) * 512],
                                   start=True, stop=True)
                    ctx.inc(ins, "e6p")
            base_g0 = ctx.val("e6g0")
            base_g1 = ctx.val("e6g1")

            @blk.vector
            def _(v):
                for j in range(8):
                    ctx.wait(v, "e6p", base6p + j + 1)
                    if j >= 2:
                        ctx.wait(v, "e6g" + str(j % 2),
                                 (base_g0 if j % 2 == 0 else base_g1) + 16 * (j // 2))
                    ins = v.tensor_copy(
                        sm["crow2"][0:1, (j % 2) * 512:(j % 2 + 1) * 512],
                        pcs[j % 2][:, :])
                    ctx.inc(ins, "e6e")
            @blk.gpsimd
            def _(g):
                for j in range(8):
                    ctx.wait(g, "e6e", base6e + j + 1)
                    ins = g.dma_start(
                        bass.AP(wd["cseg_d"], q * 4096 + j * 512, [[1, 1], [1, 512]]),
                        sm["crow2"][0:1, (j % 2) * 512:(j % 2 + 1) * 512])
                    ctx.inc(ins, "e6g" + str(j % 2), 16)
                ctx.wait(g, "e6g0")
                ctx.wait(g, "e6g1")

    with nc.Block() as blk:
        @blk.gpsimd
        def _(g):
            ins = g.dma_start(sm["cseg_s"][:, :], wd["cseg_d"][:, :])
            ctx.inc(ins, "e7w", 16)
            g.iota(sm["svals"][:, :], [[0, 1]], base=0, channel_multiplier=1,
                   allow_small_or_imprecise_dtypes=True)
    with (
        nc.psum_tensor("e7_t", [1, 128], F32) as ptr,
        nc.psum_tensor("e7_b0", [128, 128], F32) as pb0,
        nc.psum_tensor("e7_b1", [128, 128], F32) as pb1,
    ):
        with nc.Block() as blk:
            @blk.tensor
            def _(t):
                ctx.wait(t, "e7w")
                ins = t.transpose(ptr[:, :], sm["cseg_s"][:, 127:128],
                                  cs["c_identf"][:, :])
                ctx.inc(ins, "e7tp")
            @blk.vector
            def _(v):
                ctx.wait(v, "e7tp")
                ins = v.tensor_copy(sm["prow"][0:1, :], ptr[:, :])
                ctx.inc(ins, "e7v")
                ctx.wait(v, "e7v")
                ins = v.tensor_copy(sm["crow"][0:1, 1:128], sm["prow"][0:1, 0:127])
                edge(ctx, v, ins, "ec")
                ins = v.memset(sm["crow"][0:1, 0:1], 0.0)
                edge(ctx, v, ins, "ec")
                ins = v.tensor_tensor(sm["crow"][0:1, 0:128], sm["prow"][0:1, :],
                                      sm["crow"][0:1, 0:128], ALU.subtract)
                ctx.inc(ins, "e7v")
        with nc.Block() as blk:
            @blk.tensor
            def _(t):
                ins = t.matmul(pb0[:, :], cs["c_ones"][0:1, :], sm["prow"][0:1, :],
                               start=True, stop=True)
                ins = t.matmul(pb1[:, :], cs["c_ones"][0:1, :],
                               sm["crow"][0:1, 0:128], start=True, stop=True)
                ctx.inc(ins, "e7t2")
            @blk.vector
            def _(v):
                ctx.wait(v, "e7t2")
                v.tensor_copy(sm["pbc"][:, :], pb0[:, :])
                v.tensor_copy(sm["cbc"][:, :], pb1[:, :])
    with nc.Block() as blk:
        @blk.vector
        def _(v):
            for s_t in range(CAPSL):
                ins = v.tensor_scalar(sm["sv2"][:, s_t:s_t + 1], sm["svals"][:, :],
                                      float(s_t * 128), None, ALU.add)
                edge(ctx, v, ins, "ec")
                ins = v.tensor_scalar(sm["msk"][:, :], sm["pbc"][:, :],
                                      sm["sv2"][:, s_t:s_t + 1], None, ALU.is_le)
                edge(ctx, v, ins, "ec")
                ins = v.tensor_reduce(sm["gval"][:, s_t:s_t + 1], sm["msk"][:, :],
                                      AX.X, ALU.add)
                edge(ctx, v, ins, "ec")
            ins = v.tensor_copy(sm["gidx"][:, :], sm["gval"][:, :])
            edge(ctx, v, ins, "ec")
            v.memset(sm["cg"][:, :], 0.0)
    for s_t in range(CAPSL):
        with nc.Block() as blk:
            @blk.gpsimd
            def _(g):
                ins = g.indirect_dma_start(
                    sm["cg"][:, :], None, wd["cseg_d"][:, :],
                    IndirectOffsetOnAxis(sm["gidx"][:, s_t:s_t + 1], 0),
                    bounds_check=ctx.breg(g, 127), oob_is_err=False)
                ctx.inc(ins, "e8g", 16)
                ctx.wait(g, "e8g")
        with nc.Block() as blk:
            @blk.vector
            def _(v):
                ins = v.tensor_scalar(sm["msk"][:, :], sm["cg"][:, :],
                                      sm["sv2"][:, s_t:s_t + 1], None, ALU.is_le)
                edge(ctx, v, ins, "ec")
                ins = v.tensor_reduce(sm["tok1"][:, :], sm["msk"][:, :], AX.X,
                                      ALU.add)
                edge(ctx, v, ins, "ec")
                ins = v.scalar_tensor_tensor(sm["tok1"][:, :],
                                             sm["gval"][:, s_t:s_t + 1], 128.0,
                                             sm["tok1"][:, :], ALU.mult, ALU.add)
                edge(ctx, v, ins, "ec")
                ins = v.tensor_copy(sm["tokidx"][:, s_t:s_t + 1], sm["tok1"][:, :])
                edge(ctx, v, ins, "ec")
                v.memset(sm["cg"][:, :], 0.0)

    with nc.Block() as blk:
        @blk.gpsimd
        def _(g):
            ctx.wait(g, "e6s")
            ins = g.indirect_dma_start(
                sm["lcum"][0:8, :], None, wd["c_all"][:, :],
                IndirectOffsetOnAxis(sm["lcum_idx"][0:8, 0:1], 0),
                bounds_check=ctx.breg(g, 63), oob_is_err=False)
            ctx.inc(ins, "e9g", 16)
            ctx.wait(g, "e9g")
    with (
        nc.psum_tensor("e9t0", [128, 8], F32) as pq0,
        nc.psum_tensor("e9t1", [128, 8], F32) as pq1,
        nc.Block() as blk,
    ):
        pqs = [pq0, pq1]
        @blk.tensor
        def _(t):
            for tt in range(TCH):
                if tt >= 2:
                    ctx.wait(t, "e9e", tt - 1)
                ins = t.transpose(pqs[tt % 2][:, :],
                                  sm["lcum"][0:8, tt * 128:(tt + 1) * 128],
                                  cs["c_identf"][0:8, 0:8])
                ctx.inc(ins, "e9p")
        @blk.vector
        def _(v):
            for tt in range(TCH):
                ctx.wait(v, "e9p", tt + 1)
                ins = v.tensor_copy(sm["seqt"][:, tt * 8:(tt + 1) * 8],
                                    pqs[tt % 2][:, :])
                ctx.inc(ins, "e9e")
    with nc.Block() as blk:
        @blk.vector
        def _(v):
            ins = v.tensor_scalar(sm["vmat"][:, :], sm["seqt"][:, :], float(8 * 320),
                                  None, ALU.is_le)
            edge(ctx, v, ins, "ec")
            ins = v.tensor_tensor(sm["vmat"][:, :], sm["vmat"][:, :], gem[:, :],
                                  ALU.mult)
            edge(ctx, v, ins, "ec")
            ins = v.tensor_scalar(sm["fmat"][:, :], sm["seqt"][:, :], -1.0, None,
                                  ALU.add)
            edge(ctx, v, ins, "ec")
            ins = v.tensor_scalar(sm["cg"][:, :], sm["fmat"][:, :], 512.0, None,
                                  ALU.is_ge)
            edge(ctx, v, ins, "ec")
            for thr in (1024.0, 1536.0, 2048.0):
                ins = v.tensor_scalar(sm["msk"][:, :], sm["fmat"][:, :], thr,
                                      None, ALU.is_ge)
                edge(ctx, v, ins, "ec")
                ins = v.tensor_tensor(sm["cg"][:, :], sm["cg"][:, :],
                                      sm["msk"][:, :], ALU.add)
                edge(ctx, v, ins, "ec")
            ins = v.scalar_tensor_tensor(sm["fmat"][:, :], sm["cg"][:, :],
                                         3584.0, sm["fmat"][:, :], ALU.mult,
                                         ALU.add)
            edge(ctx, v, ins, "ec")
            for tt in range(TCH):
                ins = v.scalar_tensor_tensor(
                    sm["fmat"][:, tt * 8:(tt + 1) * 8], cs["c_iota_e"][:, :],
                    512.0, sm["fmat"][:, tt * 8:(tt + 1) * 8], ALU.mult, ALU.add)
            edge(ctx, v, ins, "ec")
            ins = v.tensor_tensor(sm["fmat"][:, :], sm["fmat"][:, :], sm["vmat"][:, :],
                                  ALU.mult)
            edge(ctx, v, ins, "ec")
            ins = v.tensor_scalar(sm["vmat"][:, :], sm["vmat"][:, :], -20480.0, 20480.0,
                                  ALU.mult, ALU.add)
            edge(ctx, v, ins, "ec")
            ins = v.tensor_tensor(sm["fmat"][:, :], sm["fmat"][:, :], sm["vmat"][:, :],
                                  ALU.add)
            edge(ctx, v, ins, "ec")
            ins = v.tensor_tensor(sm["msk"][:, :], eqm1[:, :], sm["fmat"][:, :],
                                  ALU.mult)
            edge(ctx, v, ins, "ec")
            ins = v.tensor_reduce(sm["fl1"][:, :], sm["msk"][:, :].rearrange(
                "p (t e) -> p t e", e=8), AX.X, ALU.add)
            edge(ctx, v, ins, "ec")
            ins = v.tensor_tensor(sm["msk"][:, :], eqm2[:, :], sm["fmat"][:, :],
                                  ALU.mult)
            edge(ctx, v, ins, "ec")
            ins = v.tensor_reduce(sm["fl2"][:, :], sm["msk"][:, :].rearrange(
                "p (t e) -> p t e", e=8), AX.X, ALU.add)
            edge(ctx, v, ins, "ec")
            ins = v.tensor_copy(sm["fl1i"][:, :], sm["fl1"][:, :])
            ins = v.tensor_copy(sm["fl2i"][:, :], sm["fl2"][:, :])

    bt = [rLT[:, 0:4096], r0[:, 0:4096]]
    bf = [rLT[:, 4096:8192], rLT[:, 8192:12288]]
    y_fm = rLT[:, 12288:16384]
    y_tok = r0[:, 4096:8192]
    w3sl = [r0[:, 8192:12288], r0[:, 12288:16384]]
    w1s = ws[:, 0:2048]
    w2s = ws[:, 2048:4096]
    stmp = sm["bc0"].bitcast(BF16)[:, 0:1024]

    tp_done = {}
    ytp_done = {}
    bS = {n: ctx.val(n) for n in
          ("etpp", "etpe", "efg0", "efg1", "efw10", "efw11", "efw20", "efw21",
           "efw30", "efw31", "efs")}
    epc_t = ctx.val("epc")

    with (
        nc.psum_tensor("effh0", [128, 512], F32) as hp0,
        nc.psum_tensor("effh1", [128, 512], F32) as hp1,
        nc.psum_tensor("effh2", [128, 512], F32) as hp2,
        nc.psum_tensor("effh3", [128, 512], F32) as hp3,
        nc.psum_tensor("effy0", [128, 512], F32) as yq0,
        nc.psum_tensor("effy1", [128, 512], F32) as yq1,
        nc.psum_tensor("efftp0", [128, 128], BF16) as tpq0,
        nc.psum_tensor("efftp1", [128, 128], BF16) as tpq1,
        nc.Block() as blk,
    ):
        hpa = [hp0, hp2]
        hpb = [hp1, hp3]
        yqs = [yq0, yq1]
        tpq = [tpq0, tpq1]

        def t_pairs(ft):
            return [(bt[ft % 2][:, j * 1024 + dc * 128:j * 1024 + (dc + 1) * 128],
                     bf[ft % 2][:, dc * 512 + j * 128:dc * 512 + (j + 1) * 128])
                    for j in range(4) for dc in range(DCH)]

        def yt_pairs(ft):
            return [(y_fm[:, dc * 512 + j * 128:dc * 512 + (j + 1) * 128],
                     y_tok[:, j * 1024 + dc * 128:j * 1024 + (dc + 1) * 128])
                    for dc in range(DCH) for j in range(4)]

        @blk.tensor
        def _(t):
            tpn = [0]

            def run_tp(pairs, pre=None):
                for i, (src, dst) in enumerate(pairs):
                    n = tpn[0]
                    if i == 0 and pre:
                        for semname, val in pre:
                            ctx.wait(t, semname, bS.get(semname, 0) + val)
                    if n >= 2:
                        ctx.wait(t, "etpe", bS["etpe"] + n - 1)
                    ins = t.transpose(tpq[n % 2][:, :],
                                      src, cs["c_ident"][:, :])
                    ctx.inc(ins, "etpp")
                    tpn[0] += 1

            jh = 0
            jy = 0
            for ft in range(5):
                if ft == 0:
                    run_tp(t_pairs(0), pre=[("efg0", 64)])
                    tp_done[0] = ctx.val("etpp")
                for hc in range(32):
                    if hc == 0:
                        ctx.wait(t, "etpe", tp_done[ft] - bS["etpp"] + bS["etpe"])
                    ctx.wait(t, "efw1" + str(jh % 2),
                             bS["efw1" + str(jh % 2)] + 16 * (jh // 2 + 1))
                    ctx.wait(t, "efw2" + str(jh % 2),
                             bS["efw2" + str(jh % 2)] + 16 * (jh // 2 + 1))
                    if jh >= 2:
                        ctx.wait(t, "efhe", jh - 1)
                        ctx.wait(t, "efha", jh - 1)
                    pa, pb = hpa[jh % 2], hpb[jh % 2]
                    for dc in range(DCH):
                        t.matmul(pa[:, :],
                                 w1s[:, (jh % 2) * 1024 + dc * 128:
                                     (jh % 2) * 1024 + (dc + 1) * 128],
                                 bf[ft % 2][:, dc * 512:(dc + 1) * 512],
                                 start=(dc == 0), stop=(dc == DCH - 1))
                    for dc in range(DCH):
                        ins = t.matmul(pb[:, :],
                                       w2s[:, (jh % 2) * 1024 + dc * 128:
                                           (jh % 2) * 1024 + (dc + 1) * 128],
                                       bf[ft % 2][:, dc * 512:(dc + 1) * 512],
                                       start=(dc == 0), stop=(dc == DCH - 1))
                    ctx.inc(ins, "efhp")
                    jh += 1
                if ft < 4:
                    run_tp(t_pairs(ft + 1),
                           pre=[("efg" + str((ft + 1) % 2),
                                 64 * ((ft + 1) // 2 + 1))])
                    tp_done[ft + 1] = ctx.val("etpp")
                for dc in range(8):
                    ctx.wait(t, "efw3" + str(jy % 2),
                             bS["efw3" + str(jy % 2)] + 16 * (jy // 2 + 1))
                    if jy >= 2:
                        ctx.wait(t, "efye", jy - 1)
                    if dc == 0:
                        ctx.wait(t, "efhe", 32 * (ft + 1))
                    for hc in range(32):
                        ins = t.matmul(yqs[jy % 2][:, :],
                                       w3sl[jy % 2][:, hc * 128:(hc + 1) * 128],
                                       h_sb[:, hc * 512:(hc + 1) * 512],
                                       start=(hc == 0), stop=(hc == 31))
                    ctx.inc(ins, "efyp")
                    jy += 1
                run_tp(yt_pairs(ft), pre=[("efye", 8 * (ft + 1)),
                                          ("efs", 16 * ft)])

        @blk.vector
        def _(v):
            tpe = [0]

            def evac_tp(pairs):
                for src, dst in pairs:
                    n = tpe[0]
                    ctx.wait(v, "etpp", bS["etpp"] + n + 1)
                    ins = v.tensor_copy(dst, tpq[n % 2][:, :])
                    ctx.inc(ins, "etpe")
                    tpe[0] += 1

            jh = 0
            jy = 0
            for ft in range(5):
                if ft == 0:
                    evac_tp(t_pairs(0))
                for hc in range(32):
                    ctx.wait(v, "efha", jh + 1)
                    if hc == 0 and ft > 0:
                        ctx.wait(v, "efyp", 8 * ft)
                    ins = v.tensor_tensor(
                        h_sb[:, hc * 512:(hc + 1) * 512],
                        stmp[:, (jh % 2) * 512:(jh % 2 + 1) * 512],
                        hpb[jh % 2][:, :], ALU.mult)
                    ctx.inc(ins, "efhe")
                    jh += 1
                if ft < 4:
                    evac_tp(t_pairs(ft + 1))
                for dc in range(8):
                    ctx.wait(v, "efyp", jy + 1)
                    ins = v.tensor_copy(y_fm[:, dc * 512:(dc + 1) * 512],
                                        yqs[jy % 2][:, :])
                    ctx.inc(ins, "efye")
                    jy += 1
                evac_tp(yt_pairs(ft))
                ytp_done[ft] = ctx.val("etpe")

        @blk.scalar
        def _(a):
            ctx.wait(a, "epc", epc_t)
            for hw in range(2):
                ins = a.dma_start(w1s[:, hw * 1024:(hw + 1) * 1024],
                                  wd["W1b"][:, hw * 1024:(hw + 1) * 1024])
                ctx.inc(ins, "efw1" + str(hw % 2), 16)
            jh = 0
            for ft in range(5):
                for hc in range(32):
                    hw = jh + 2
                    if hw < 160:
                        ctx.wait(a, "efhp", hw - 1)
                        hc2 = hw % 32
                        ins = a.dma_start(
                            w1s[:, (hw % 2) * 1024:(hw % 2 + 1) * 1024],
                            wd["W1b"][:, hc2 * 1024:(hc2 + 1) * 1024])
                        ctx.inc(ins, "efw1" + str(hw % 2), 16)
                    ctx.wait(a, "efhp", jh + 1)
                    ins = a.activation(stmp[:, (jh % 2) * 512:(jh % 2 + 1) * 512],
                                       hpa[jh % 2][:, :], AF.Silu)
                    ctx.inc(ins, "efha")
                    jh += 1

        @blk.sync
        def _(s):
            ctx.wait(s, "epc", epc_t)
            for hw in range(2):
                ins = s.dma_start(w2s[:, hw * 1024:(hw + 1) * 1024],
                                  wd["W2b"][:, hw * 1024:(hw + 1) * 1024])
                ctx.inc(ins, "efw2" + str(hw % 2), 16)
            jh = 0
            for ft in range(5):
                if ft > 0:
                    ctx.wait(s, "etpe", ytp_done[ft - 1])
                    ins = s.dma_start(
                        wd["ye_loc"][(ft - 1) * 512:ft * 512, :]
                        .rearrange("(j p) d -> p j d", p=128),
                        y_tok[:, :].rearrange("p (j d) -> p j d", d=D))
                    ctx.inc(ins, "efs", 16)
                for hc in range(32):
                    hw = jh + 2
                    if hw < 160:
                        ctx.wait(s, "efhp", hw - 1)
                        hc2 = hw % 32
                        ins = s.dma_start(
                            w2s[:, (hw % 2) * 1024:(hw % 2 + 1) * 1024],
                            wd["W2b"][:, hc2 * 1024:(hc2 + 1) * 1024])
                        ctx.inc(ins, "efw2" + str(hw % 2), 16)
                    jh += 1
            ctx.wait(s, "etpe", ytp_done[4])
            ins = s.dma_start(
                wd["ye_loc"][4 * 512:5 * 512, :]
                .rearrange("(j p) d -> p j d", p=128),
                y_tok[:, :].rearrange("p (j d) -> p j d", d=D))
            ctx.inc(ins, "efs", 16)

        @blk.gpsimd
        def _(g):
            ctx.wait(g, "epc", epc_t)
            ctx.wait(g, "ccx")

            def gather(ft):
                if ft >= 2:
                    ctx.wait(g, "etpp", tp_done[ft - 2])
                for j in range(4):
                    ins = g.indirect_dma_start(
                        bt[ft % 2][:, j * 1024:(j + 1) * 1024], None,
                        wd["xg_all"][:, :],
                        IndirectOffsetOnAxis(
                            sm["tokidx"][:, ft * 4 + j:ft * 4 + j + 1], 0),
                        bounds_check=ctx.breg(g, N_CORES * T - 1),
                        oob_is_err=False)
                    ctx.inc(ins, "efg" + str(ft % 2), 16)

            def w3_loads(ft):
                for dc in range(8):
                    jy = ft * 8 + dc
                    if jy >= 2:
                        ctx.wait(g, "efyp", jy - 1)
                    ins = g.dma_start(w3sl[jy % 2][:, :],
                                      wd["W3b"][:, dc * 4096:(dc + 1) * 4096])
                    ctx.inc(ins, "efw3" + str(jy % 2), 16)

            base_efs = bS["efs"]

            def ye_ag(ft):
                ctx.wait(g, "efs", base_efs + 16 * (ft + 1))
                src = bass.AP(wd["ye_loc"], ft * 512 * D, [[1, 512 * D]])
                dst = bass.AP(wd["ye_all"], ft * N_CORES * 512 * D,
                              [[1, N_CORES * 512 * D]])
                ins = g.collective_compute(
                    "AllGather", ALU.bypass,
                    replica_groups=[list(range(N_CORES))],
                    ins=[src.opt()], outs=[dst.opt()])
                ctx.inc(ins, "ccy")

            gather(0)
            gather(1)
            gather(2)
            w3_loads(0)
            gather(3)
            w3_loads(1)
            gather(4)
            w3_loads(2)
            ye_ag(0)
            w3_loads(3)
            ye_ag(1)
            w3_loads(4)
            ye_ag(2)
            ye_ag(3)
            ye_ag(4)

    rows = [(rvp[:, 0:1024], rvp[:, 1024:2048]),
            (rvp[:, 2048:3072], rvp[:, 3072:4096])]
    ocmb3 = [rvp[:, 4096 + 2048 * i:4096 + 2048 * (i + 1)].bitcast(F32)
             for i in range(3)]
    sqs = rvp[:, 10240:12288]
    outs = [sm["bc1"], sm["bc0"]]
    with nc.Block() as blk:
        @blk.vector
        def _(v):
            ins = v.memset(st["eps"][:, :], EPS_LN)
            edge(ctx, v, ins, "ec")
            v.memset(rvp[:, 0:4096].bitcast(F32), 0.0)

    bC = {n: ctx.val(n) for n in
          ("egc", "egv", "ega", "egs", "egr", "ego", "egg0", "egg1",
           "eout0", "eout1")}
    ccy_t = ctx.val("ccy")
    with nc.Block() as blk:
        @blk.gpsimd
        def _(g):
            ctx.wait(g, "ccy", ccy_t)
            for tt in range(TCH):
                if tt >= 2:
                    ctx.wait(g, "egc", bC["egc"] + tt - 1)
                r1, r2 = rows[tt % 2]
                ins = g.indirect_dma_start(
                    r1, None, wd["ye_all"][:, :],
                    IndirectOffsetOnAxis(sm["fl1i"][:, tt:tt + 1], 0),
                    bounds_check=ctx.breg(g, N_CORES * CAP - 1), oob_is_err=False)
                ctx.inc(ins, "egg" + str(tt % 2), 16)
                ins = g.indirect_dma_start(
                    r2, None, wd["ye_all"][:, :],
                    IndirectOffsetOnAxis(sm["fl2i"][:, tt:tt + 1], 0),
                    bounds_check=ctx.breg(g, N_CORES * CAP - 1), oob_is_err=False)
                ctx.inc(ins, "egg" + str(tt % 2), 16)

        @blk.vector
        def _(v):
            for r in range(TCH + 2):
                if r < TCH:
                    tt = r
                    p = tt % 2
                    oc = ocmb3[tt % 3]
                    r1, r2 = rows[p]
                    ctx.wait(v, "egg" + str(p),
                             bC["egg" + str(p)] + 32 * (tt // 2 + 1))
                    if tt >= 3:
                        ctx.wait(v, "egs", bC["egs"] + tt - 2)
                        ctx.wait(v, "ego", bC["ego"] + tt - 2)
                    ins = v.tensor_scalar(oc[:, :], r1, w1t[:, tt:tt + 1],
                                          None, ALU.mult)
                    edge(ctx, v, ins, "ec")
                    ins = v.scalar_tensor_tensor(oc[:, :], r2, w2t[:, tt:tt + 1],
                                                 oc[:, :], ALU.mult, ALU.add)
                    edge(ctx, v, ins, "ec")
                    ins = v.tensor_tensor(oc[:, :], oc[:, :],
                                          rX[:, tt * D:(tt + 1) * D], ALU.add)
                    ctx.inc(ins, "egv")
                    ins = v.memset(r1, 0.0)
                    edge(ctx, v, ins, "ec")
                    ins = v.memset(r2, 0.0)
                    ctx.inc(ins, "egc")
                if 1 <= r <= TCH:
                    tt = r - 1
                    oc = ocmb3[tt % 3]
                    ctx.wait(v, "ega", bC["ega"] + tt + 1)
                    ins = v.tensor_reduce(st["ss"][:, tt:tt + 1],
                                          sqs[:, (tt % 2) * 1024:(tt % 2 + 1) * 1024],
                                          AX.X, ALU.add)
                    edge(ctx, v, ins, "ec")
                    ins = v.tensor_reduce(st["mean"][:, tt:tt + 1], oc[:, :],
                                          AX.X, ALU.add)
                    edge(ctx, v, ins, "ec")
                    ins = v.tensor_scalar(st["var"][:, tt:tt + 1],
                                          st["ss"][:, tt:tt + 1], 1.0 / D, None,
                                          ALU.mult)
                    edge(ctx, v, ins, "ec")
                    ins = v.tensor_scalar(st["mean"][:, tt:tt + 1],
                                          st["mean"][:, tt:tt + 1], 1.0 / D, None,
                                          ALU.mult)
                    edge(ctx, v, ins, "ec")
                    ins = v.tensor_tensor(st["msq"][:, tt:tt + 1],
                                          st["mean"][:, tt:tt + 1],
                                          st["mean"][:, tt:tt + 1], ALU.mult)
                    edge(ctx, v, ins, "ec")
                    ins = v.tensor_tensor(st["var"][:, tt:tt + 1],
                                          st["var"][:, tt:tt + 1],
                                          st["msq"][:, tt:tt + 1], ALU.subtract)
                    ctx.inc(ins, "egs")
                if r >= 2:
                    tt = r - 2
                    oc = ocmb3[tt % 3]
                    ctx.wait(v, "egr", bC["egr"] + tt + 1)
                    if tt >= 2:
                        ctx.wait(v, "eout" + str(tt % 2),
                                 bC["eout" + str(tt % 2)] + 16 * (tt // 2))
                    ins = v.tensor_scalar(outs[tt % 2][:, :], oc[:, :],
                                          st["mean"][:, tt:tt + 1],
                                          st["rstd"][:, tt:tt + 1],
                                          ALU.subtract, ALU.mult)
                    ctx.inc(ins, "ego")

        @blk.scalar
        def _(a):
            for r in range(TCH + 1):
                if r < TCH:
                    tt = r
                    ctx.wait(a, "egv", bC["egv"] + tt + 1)
                    if tt >= 2:
                        ctx.wait(a, "egs", bC["egs"] + tt - 1)
                    ins = a.activation(sqs[:, (tt % 2) * 1024:(tt % 2 + 1) * 1024],
                                       ocmb3[tt % 3][:, :], AF.Square)
                    ctx.inc(ins, "ega")
                if r >= 1:
                    tt = r - 1
                    ctx.wait(a, "egs", bC["egs"] + tt + 1)
                    ins = a.activation(st["rstd"][:, tt:tt + 1],
                                       st["var"][:, tt:tt + 1], AF.Ln,
                                       bias=st["eps"][:, :])
                    c2 = ctx.inc(ins, "eg2")
                    ctx.wait(a, "eg2", c2)
                    ins = a.activation(st["rstd"][:, tt:tt + 1],
                                       st["rstd"][:, tt:tt + 1], AF.Exp,
                                       scale=-0.5)
                    ctx.inc(ins, "egr")

        @blk.sync
        def _(s):
            for tt in range(TCH):
                ctx.wait(s, "ego", bC["ego"] + tt + 1)
                ins = s.dma_start(out_d[tt * 128:(tt + 1) * 128, :],
                                  outs[tt % 2][:, :])
                ctx.inc(ins, "eout" + str(tt % 2), 16)
            ctx.wait(s, "eout0")
            ctx.wait(s, "eout1")


def declare_l1_ffn_weights(nc):
    wd = {}
    for key, n in (("W1e", 32), ("W2e", 32)):
        w = nc.dram_tensor(key, [n, D, 128], F32, kind="ExternalInput")
        wd[key] = [w[i] for i in range(n)]
        wd[key + "_t"] = w
    w = nc.dram_tensor("W3e", [8, 32 * 128, 128], F32, kind="ExternalInput")
    wd["W3e"] = [w[i] for i in range(8)]
    wd["W3e_t"] = w
    wd["W1b"] = nc.dram_tensor("W1b", [128, 32768], BF16)
    wd["W2b"] = nc.dram_tensor("W2b", [128, 32768], BF16)
    wd["W3b"] = nc.dram_tensor("W3b", [128, 32768], BF16)
    wd["Wg"] = nc.dram_tensor("Wg", [D, 8], F32, kind="ExternalInput")
    wd["bg_row"] = nc.dram_tensor("bg_row", [1, 8], F32, kind="ExternalInput")
    wd["myhot"] = nc.dram_tensor("myhot", [8, 1], F32, kind="ExternalInput")
    wd["lcum_idx"] = nc.dram_tensor("lcum_idx", [8, 1], I32, kind="ExternalInput")
    wd["xg_loc"] = nc.dram_tensor("xg_loc", [T, D], BF16)
    wd["xg_all"] = nc.dram_tensor("xg_all", [N_CORES * T, D], BF16)
    wd["lab_loc"] = nc.dram_tensor("lab_loc", [8, T], BF16)
    wd["lab_all"] = nc.dram_tensor("lab_all", [N_CORES * 8, T], BF16)
    wd["c_all"] = nc.dram_tensor("c_all", [64, T], F32)
    wd["cseg_d"] = nc.dram_tensor("cseg_d", [128, 128], F32)
    wd["ye_loc"] = nc.dram_tensor("ye_loc", [CAP, D], BF16)
    wd["ye_all"] = nc.dram_tensor("ye_all", [N_CORES * CAP, D], BF16)
    return wd


def host_l1_ffn_weight_maps(ins, core):
    m = {}
    W1 = np.asarray(ins["L1_ffn_W1"])[core]
    m["W1e"] = np.ascontiguousarray(W1.reshape(D, 32, 128).transpose(1, 0, 2))
    W2 = np.asarray(ins["L1_ffn_W2"])[core]
    m["W2e"] = np.ascontiguousarray(W2.reshape(D, 32, 128).transpose(1, 0, 2))
    W3 = np.asarray(ins["L1_ffn_W3"])[core]
    m["W3e"] = np.ascontiguousarray(
        W3.reshape(32, 128, 8, 128).transpose(2, 0, 1, 3).reshape(8, 32 * 128, 128))
    m["Wg"] = np.asarray(ins["L1_ffn_Wg"])
    m["bg_row"] = np.asarray(ins["L1_ffn_bg"]).reshape(1, 8)
    hot = np.zeros((8, 1), np.float32)
    hot[core, 0] = 1.0
    m["myhot"] = hot
    m["lcum_idx"] = (np.arange(8, dtype=np.int32) * 8 + core).reshape(8, 1)
    return m


def emit_l1_precast(ctx, wd, g):
    for key, bkey in (("W1e_t", "W1b"), ("W2e_t", "W2b")):
        w = wd[key]
        tw = wd[bkey]
        for h0 in range(0, 32, 8):
            src = w[h0:h0 + 8].rearrange("hc (dc p) c -> (hc dc) p c", p=128)
            dst = tw[:, h0 * 1024:(h0 + 8) * 1024].rearrange(
                "p (hc dc c) -> (hc dc) p c", hc=8, c=128)
            ins = g.dma_start(dst, src)
            ctx.inc(ins, "epc", 16)
    w = wd["W3e_t"]
    tw = wd["W3b"]
    for d0 in range(0, 8, 2):
        src = w[d0:d0 + 2].rearrange("dc (hc p) c -> (dc hc) p c", p=128)
        dst = tw[:, d0 * 4096:(d0 + 2) * 4096].rearrange(
            "p (dc hc c) -> (dc hc) p c", dc=2, c=128)
        ins = g.dma_start(dst, src)
        ctx.inc(ins, "epc", 16)
    wd["_epc"] = ctx.val("epc")


def alloc_l1_smalls(nc, sm, ar=None):
    uid = [0]
    def _mk(sh, dt):
        uid[0] += 1
        return nc.alloc_sbuf_tensor("l1_%d" % uid[0], sh, dt)
    A = ar.take if ar else _mk
    sm["gw"] = A([128, 64], BF16)
    sm["bg_row"] = A([1, 8], F32)
    sm["bg_bc"] = A([128, 8], F32)
    sm["lg_tok"] = A([128, 128], F32)
    sm["e_m1"] = A([128, 16], F32)
    sm["e_m2"] = A([128, 16], F32)
    sm["e_eq1"] = A([128, 128], F32)
    sm["e_eq2"] = A([128, 128], F32)
    sm["e_ge"] = A([128, 128], F32)
    sm["e_w1"] = A([128, 16], F32)
    sm["e_w2"] = A([128, 16], F32)
    sm["labl"] = sm["ws"][0:8, 0:T]
    sm["myhot"] = A([8, 1], F32)
    sm["crow2"] = A([1, 1024], F32)
    sm["crow"] = sm["crow2"][0:1, 0:128]
    sm["prow"] = sm["crow2"][0:1, 128:256]
    sm["cseg_s"] = A([128, 128], F32)
    sm["svals"] = A([128, 1], F32)
    sm["sv2"] = A([128, CAPSL], F32)
    sm["pbc"] = A([128, 128], F32)
    sm["cbc"] = A([128, 128], F32)
    sm["msk"] = A([128, 128], F32)
    sm["gval"] = A([128, CAPSL], F32)
    sm["gidx"] = A([128, CAPSL], I32)
    sm["cg"] = A([128, 128], F32)
    sm["tok1"] = A([128, 1], F32)
    sm["tokidx"] = A([128, CAPSL], I32)
    sm["lcum_idx"] = A([8, 1], I32)
    sm["seqt"] = A([128, 128], F32)
    sm["vmat"] = A([128, 128], F32)
    sm["fmat"] = A([128, 128], F32)
    sm["fl1"] = A([128, 16], F32)
    sm["fl2"] = A([128, 16], F32)
    sm["fl1i"] = A([128, 16], I32)
    sm["fl2i"] = A([128, 16], I32)
    return sm


def emit_load_l1_smalls(ctx, sm, wd):
    emit_load_small(ctx, [
        (sm["bg_row"][:, :], wd["bg_row"][:, :]),
        (sm["myhot"][:, :], wd["myhot"][:, :]),
        (sm["lcum_idx"][:, :], wd["lcum_idx"][:, :]),
    ])


def build_stage_e_program():
    nc = bass.Bass("TRN2", target_bir_lowering=False,
                   dynamic_dma_scratch_size=26624)
    ctx = Ctx(nc)
    x_d = nc.dram_tensor("x_in", [T, D], F32, kind="ExternalInput")
    out_d = nc.dram_tensor("x_out", [T, D], F32, kind="ExternalOutput")
    wd = declare_l1_ffn_weights(nc)
    cs = setup_constants(ctx, resident_only=True)
    rg = alloc_regions(nc, names=("r0", "r1", "r2", "r3", "rvp"))
    sm = alloc_smalls(nc, skip_rows=True)
    ar = Arena(nc)
    alloc_l1_smalls(nc, sm, ar)
    emit_arena_consts(ctx, cs, ar, ("c_iota_e",))
    st = alloc_ln_stats(nc, sq_backing=sm["ws"][:, :2 * D])
    with nc.Block() as blk:
        @blk.gpsimd
        def _(g):
            emit_l1_precast(ctx, wd, g)
    emit_load_l1_smalls(ctx, sm, wd)
    emit_load_x(ctx, x_d, rg["r1"])
    stage_l1_ffn(ctx, cs, st, sm, wd, rX=rg["r1"], rLT=rg["r2"],
                 cta=rg["r3"].bitcast(F32), ctb=None,
                 h_sb=rg["rvp"][:, 0:16 * 1024], rvp=rg["rvp"],
                 out_d=out_d, r0=rg["r0"])
    nc.finalize()
    return nc



def alloc_regions(nc, names=("r0", "r1", "r2", "r3", "rvp")):
    b = {}
    for nm in names:
        if nm == "rvp":
            b[nm] = nc.alloc_sbuf_tensor(nm, [128, TCH * NH * VPW], BF16)
        else:
            b[nm] = nc.alloc_sbuf_tensor(nm, [128, TCH * D], BF16)
    return b


def alloc_smalls(nc, skip_rows=False):
    sm = {}
    sm["ws"] = nc.alloc_sbuf_tensor("ws", [128, 8192], BF16)
    sm["bc0"] = nc.alloc_sbuf_tensor("bc0", [128, D], F32)
    sm["bc1"] = nc.alloc_sbuf_tensor("bc1", [128, D], F32)
    if not skip_rows:
        sm["b3_pp"] = nc.alloc_sbuf_tensor("b3_pp_s", [128, 24], F32)
        sm["bv_row"] = nc.alloc_sbuf_tensor("bv_row_s", [1, D], BF16)
        sm["bp_row"] = nc.alloc_sbuf_tensor("bp_row_s", [1, D], BF16)
    return sm


def emit_load_x(ctx, x_d, xa):
    nc = ctx.nc
    with nc.Block() as blk:
        @blk.gpsimd
        def _(g):
            for tc in range(TCH):
                ins = g.dma_start(xa[:, tc * D:(tc + 1) * D],
                                  x_d[tc * 128:(tc + 1) * 128, :])
                ctx.inc(ins, "xld", 16)
            ctx.wait(g, "xld")


def emit_store_raw(ctx, sb, out_d, tag):
    nc = ctx.nc
    with nc.Block() as blk:
        @blk.gpsimd
        def _(g):
            ins = g.dma_start(out_d[:, :], sb[:, :])
            ctx.inc(ins, "dmp" + tag, 16)
            ctx.wait(g, "dmp" + tag)


def emit_store_x(ctx, x_tok, out_d):
    nc = ctx.nc
    with nc.Block() as blk:
        @blk.gpsimd
        def _(g):
            for tc in range(TCH):
                ins = g.dma_start(out_d[tc * 128:(tc + 1) * 128, :],
                                  x_tok[:, tc * D:(tc + 1) * D])
                ctx.inc(ins, "xst", 16)
            ctx.wait(g, "xst")


def emit_load_small(ctx, items):
    nc = ctx.nc
    with nc.Block() as blk:
        @blk.gpsimd
        def _(s):
            for dst, src in items:
                ins = s.dma_start(dst, src)
                ctx.inc(ins, "sld", 16)
            ctx.wait(s, "sld")


def declare_l0a_weights(nc):
    wd = {}
    wa = nc.dram_tensor("W_attn_ocm", [24, D, 128], F32, kind="ExternalInput")
    wd["W_attn_ocm"] = [wa[i] for i in range(24)]
    wp = nc.dram_tensor("W_proj_hm", [2, D, 512], F32, kind="ExternalInput")
    wd["W_proj_hm"] = [wp[i] for i in range(2)]
    wd["b3_pp"] = nc.dram_tensor("b3_pp", [128, 24], F32, kind="ExternalInput")
    wd["bv_row"] = nc.dram_tensor("bv_row", [1, D], F32, kind="ExternalInput")
    wd["bp_row"] = nc.dram_tensor("bp_row", [1, D], F32, kind="ExternalInput")
    return wd


def host_l0a_weight_maps(ins):
    m = {}
    W = np.asarray(ins["L0_attn_W_attn"])
    m["W_attn_ocm"] = np.ascontiguousarray(
        W.reshape(D, 24, 128).transpose(1, 0, 2))
    Wp = np.asarray(ins["L0_attn_W_proj"])
    m["W_proj_hm"] = np.ascontiguousarray(
        Wp.reshape(D, 2, 512).transpose(1, 0, 2))
    m["b3_pp"] = np.ascontiguousarray(ins["L0_attn_b_attn"].reshape(24, 128).T)
    m["bv_row"] = np.ascontiguousarray(ins["L0_attn_b_attn"][2048:].reshape(1, D))
    m["bp_row"] = np.ascontiguousarray(ins["L0_attn_b_proj"].reshape(1, D))
    return m


def build_stage_a_program(dumps=()):
    nc = bass.Bass("TRN2", target_bir_lowering=False)
    ctx = Ctx(nc)
    x_d = nc.dram_tensor("x_in", [T, D], F32, kind="ExternalInput")
    out_d = nc.dram_tensor("x_out", [T, D], F32, kind="ExternalOutput")
    wd = declare_l0a_weights(nc)
    cs = setup_constants(ctx)
    rg = alloc_regions(nc)
    sm = alloc_smalls(nc)
    st = alloc_ln_stats(nc, sq_backing=sm['ws'][:, :2 * D])
    emit_load_small(ctx, [(sm["b3_pp"][:, :], wd["b3_pp"][:, :]),
                          (sm["bv_row"][:, :], wd["bv_row"][:, :]),
                          (sm["bp_row"][:, :], wd["bp_row"][:, :])])
    emit_load_x(ctx, x_d, rg["r0"])
    dd = {}
    for name in dumps:
        w = TCH * NH * VPW if name == "vp" else (4 * TCH if name == "lnst" else TCH * D)
        dd[name] = nc.dram_tensor("dump_" + name, [128, w], F32,
                                  kind="ExternalOutput")
    stage_l0_attn(ctx, cs, st, sm, wd, x_in=rg["r0"], x_out=rg["r2"],
                  rfm=rg["r1"], rq=rg["r2"], rk=rg["r3"], rvp=rg["rvp"],
                  dd=dd, rg=rg)
    emit_store_x(ctx, rg["r2"], out_d)
    nc.finalize()
    return nc


_PROGS = {}


def _prog(name):
    if name not in _PROGS:
        _PROGS[name] = {
            "a": build_stage_a_program,
            "b": build_stage_b_program,
            "c": build_stage_c_program,
            "d": build_stage_d_program,
            "e": build_stage_e_program,
            "fused": build_fused_program,
        }[name]()
    return _PROGS[name]


def kernel(**inputs):
    return kernel_fused(**inputs)


def kernel_multi(**inputs):
    ins = inputs
    cs = host_constants()
    x = np.ascontiguousarray(np.asarray(ins["L0_attn_x"], dtype=np.float32))

    def run(name, per_core_maps):
        nc = _prog(name)
        res = run_bass_kernel_spmd(nc, per_core_maps, core_ids=list(range(N_CORES)))
        return [np.ascontiguousarray(np.asarray(r["x_out"], dtype=np.float32))
                for r in res.results]

    wa = host_l0a_weight_maps(ins)
    maps = [dict(cs, **wa, x_in=np.ascontiguousarray(x[c])) for c in range(N_CORES)]
    x1 = run("a", maps)

    wb = host_l0f_attn_weight_maps(ins)
    maps = [dict(cs, **wb, x_in=x1[c]) for c in range(N_CORES)]
    h = run("b", maps)

    wc = host_l0_moe_weight_maps(ins)
    maps = [dict(cs, **wc, h_in=h[c], x1_in=x1[c]) for c in range(N_CORES)]
    x2 = run("c", maps)

    wdt = host_ttt_weight_maps(ins)
    maps = [dict(cs, **wdt, x_in=x2[c]) for c in range(N_CORES)]
    x3 = run("d", maps)

    maps = [dict(cs, **host_l1_ffn_weight_maps(ins, c), x_in=x3[c])
            for c in range(N_CORES)]
    out = run("e", maps)
    return np.stack(out, axis=0)


def build_fused_program():
    nc = bass.Bass("TRN2", target_bir_lowering=False,
                   dynamic_dma_scratch_size=26624, num_swdge_queues=2)
    ctx = Ctx(nc)
    x_d = nc.dram_tensor("x_in", [T, D], F32, kind="ExternalInput")
    out_d = nc.dram_tensor("x_out", [T, D], F32, kind="ExternalOutput")
    wda = declare_l0a_weights(nc)
    wdb = declare_l0f_attn_weights(nc)
    wdc = declare_l0_moe_weights(nc)
    wdd = declare_ttt_weights(nc)
    wde = declare_l1_ffn_weights(nc)
    cs = setup_constants(ctx, resident_only=True)
    rg = alloc_regions(nc)
    sm = alloc_smalls(nc, skip_rows=True)
    ar = Arena(nc)
    st = alloc_ln_stats(nc, sq_backing=sm["ws"][:, :2 * D])
    sm["b3_pp"] = ar.take([128, 24], F32)
    sm["bv_row"] = ar.take([1, D], BF16)
    sm["bp_row"] = ar.take([1, D], BF16)
    emit_arena_consts(ctx, cs, ar, ("c_mask",))
    emit_load_small(ctx, [(sm["b3_pp"][:, :], wda["b3_pp"][:, :]),
                          (sm["bv_row"][:, :], wda["bv_row"][:, :]),
                          (sm["bp_row"][:, :], wda["bp_row"][:, :])])
    emit_load_x(ctx, x_d, rg["r0"])
    stage_l0_attn(ctx, cs, st, sm, wda, x_in=rg["r0"], x_out=rg["r2"],
                  rfm=rg["r1"], rq=rg["r2"], rk=rg["r3"], rvp=rg["rvp"])

    ar.reset()
    sm["cr"] = ar.take([128, 1024], BF16)
    sm["ci"] = ar.take([128, 1024], BF16)
    sm["g1_row"] = ar.take([1, D], BF16)
    emit_arena_consts(ctx, cs, ar, ("c_mask",))
    emit_load_small(ctx, [(sm["g1_row"][:, :], wdb["g1_row"][:, :])])
    stage_l0_ffn_attn(ctx, cs, st, sm, wdb, x_in=rg["r2"], rA=rg["r0"],
                      rB=rg["r1"], rC=rg["r3"], rvp=rg["rvp"])

    ar.reset()
    alloc_moe_smalls(nc, sm, ar)
    sm["g2_row"] = ar.take([1, D], BF16)
    emit_load_small(ctx, [(sm["g2_row"][:, :], wdc["g2_row"][:, :])])
    acc32 = [rg["r0"].bitcast(F32)[:, 0:4096], rg["r0"].bitcast(F32)[:, 4096:8192]]
    stage_l0_moe(ctx, cs, st, sm, wdc, rH=rg["r0"], rX1=rg["r2"], rOut=rg["r1"],
                 rFm=rg["r3"], h_sb=rg["rvp"][:, :16 * 1024], acc32=acc32)

    ar.reset()
    alloc_ttt_smalls(nc, sm, ar)
    emit_arena_consts(ctx, cs, ar, ("c_trils", "c_trius", "c_uincl"))
    stage_ttt(ctx, cs, st, sm, wdd, rX=rg["r1"], rFm=rg["r3"], rTH=rg["r0"],
              rvp=rg["rvp"], wreg=rg["r2"].bitcast(F32),
              extra_gdma=lambda g: emit_l1_precast(ctx, wde, g))

    ar.reset()
    alloc_l1_smalls(nc, sm, ar)
    emit_arena_consts(ctx, cs, ar, ("c_iota_e",))
    emit_load_l1_smalls(ctx, sm, wde)
    stage_l1_ffn(ctx, cs, st, sm, wde, rX=rg["r1"], rLT=rg["r2"],
                 cta=rg["r3"].bitcast(F32), ctb=None,
                 h_sb=rg["rvp"][:, 0:16 * 1024], rvp=rg["rvp"], out_d=out_d,
                 r0=rg["r0"])
    nc.finalize()
    return nc


def host_full_maps(ins):
    cs = host_constants()
    base = dict(cs)
    base.update(host_l0a_weight_maps(ins))
    base.update(host_l0f_attn_weight_maps(ins))
    base.update(host_l0_moe_weight_maps(ins))
    base.update(host_ttt_weight_maps(ins))
    x = np.ascontiguousarray(np.asarray(ins["L0_attn_x"], dtype=np.float32))
    maps = []
    for c in range(N_CORES):
        m = dict(base)
        m.update(host_l1_ffn_weight_maps(ins, c))
        m["x_in"] = np.ascontiguousarray(x[c])
        maps.append(m)
    return maps


def kernel_fused(**inputs):
    nc = _prog("fused")
    maps = host_full_maps(inputs)
    res = run_bass_kernel_spmd(nc, maps, core_ids=list(range(N_CORES)))
    return np.stack([np.ascontiguousarray(np.asarray(r["x_out"], dtype=np.float32))
                     for r in res.results], axis=0)
